# version 04
# baseline (speedup 1.0000x reference)
import numpy as np
import ml_dtypes
import concourse.bass as bass
import concourse.mybir as mybir
from concourse.bass_types import AP
from concourse.bass_utils import run_bass_kernel_spmd

F32 = mybir.dt.float32
BF16 = mybir.dt.bfloat16
AF = mybir.ActivationFunctionType
ALU = mybir.AluOpType
AX = mybir.AxisListType

T, D = 2048, 1024
NT, ND = T // 128, D // 128
NEG = -1.0e30

MODE = "unfused"
DBG_EXP = False
DBG_IDENT = False
DBG_ROUTE = False

ENGINES = ("tensor", "vector", "scalar", "sync", "gpsimd")


class Buf:
    __slots__ = ("name", "writers", "readers")

    def __init__(self, name):
        self.name = name
        self.writers = {}
        self.readers = {}


class Prog:

    def __init__(self, nc):
        self.nc = nc
        self.keys = [e + ".C" for e in ENGINES] + [e + ".D" for e in ENGINES]
        self.sem = {k: nc.alloc_semaphore("sem_" + k.replace(".", "_"))
                    for k in self.keys}
        self.cnt = {k: 0 for k in self.keys}
        self.hwm = {e: {} for e in ENGINES}
        self.ops = {e: [] for e in ENGINES}
        self.bufsem = {}

    def _buf_key(self, buf):
        k = self.bufsem.get(id(buf))
        if k is None:
            k = "B%d.%s" % (len(self.bufsem), buf.name)
            self.bufsem[id(buf)] = k
            self.sem[k] = self.nc.alloc_semaphore("semb_%d" % len(self.bufsem))
            self.cnt[k] = 0
        return k

    def tok(self, e):
        return (e + ".C", self.cnt[e + ".C"])

    def tokd(self, e):
        return (e + ".D", self.cnt[e + ".D"])

    def _plan(self, e, key, emit_wrap, reads, writes, deps, inc):
        want = list(deps)
        for b in reads:
            want.extend(b.writers.items())
        for b in writes:
            want.extend(b.writers.items())
            want.extend(b.readers.items())
        waits = []
        for (k2, c) in want:
            if k2 == e + ".C" and e == "tensor":
                continue
            if c > self.hwm[e].get(k2, 0):
                self.hwm[e][k2] = c
                waits.append((self.sem[k2], c))
        self.ops[e].append((waits, emit_wrap))
        self.cnt[key] += inc
        t = (key, self.cnt[key])
        for b in reads:
            b.readers[key] = self.cnt[key]
        for b in writes:
            b.writers[key] = self.cnt[key]
            b.readers = {}
        return t

    def step(self, e, emit, reads=(), writes=(), deps=(), inc=1):
        key = e + ".C"
        sem = self.sem[key]

        def wrap(eng, emit=emit, inc=inc, sem=sem):
            last = emit(eng)
            if inc and last is not None:
                last.then_inc(sem, inc)
        return self._plan(e, key, wrap, reads, writes, deps, inc)

    def dma(self, e, movs, reads=(), writes=(), deps=()):
        if writes:
            key = self._buf_key(writes[0])
        else:
            key = e + ".D"
        sem = self.sem[key]

        def wrap(eng, movs=movs, sem=sem):
            for m in movs:
                kw = m[2] if len(m) > 2 else {}
                eng.dma_start(out=m[0], in_=m[1], **kw).then_inc(sem, 16)
        return self._plan(e, key, wrap, reads, writes, deps, 16 * len(movs))

    def fence(self, deps):
        for e in ENGINES:
            self.step(e, lambda eng: eng.nop(), deps=deps, inc=1)

    def barrier_deps(self):
        return ([(e + ".C", self.cnt[e + ".C"]) for e in ENGINES if self.cnt[e + ".C"] > 0]
                + [(e + ".D", self.cnt[e + ".D"]) for e in ENGINES if self.cnt[e + ".D"] > 0])

    def emit_all(self):
        with self.nc.Block() as block:
            for e in ENGINES:
                def runner(eng, steps=self.ops[e]):
                    for waits, wrap in steps:
                        for (s, c) in waits:
                            eng.wait_ge(s, c)
                        wrap(eng)
                getattr(block, e)(runner)


def build_program(sections, dump=()):
    nc = bass.Bass("TRN2", target_bir_lowering=False,
                   detect_race_conditions=False)
    P = Prog(nc)
    dt_in = {}

    def din(name, shape, dtype=F32):
        dt_in[name] = nc.dram_tensor(name, shape, dtype, kind="ExternalInput").ap()
        return dt_in[name]

    resid_in = din("resid_in", [T, D])
    out = nc.dram_tensor("out", [T, D], F32, kind="ExternalOutput").ap()

    din("ident_bf", [128, 128], BF16)
    din("ident_f32", [128, 128])
    din("triu_incl", [128, 128])
    din("slo_mask", [128, 128])
    din("gamma", [1, 1])
    din("eps_bc", [128, 2])

    need = set(sections)
    if "l0attn" in need:
        din("l0_wqkv", [D, 3 * D])
        din("l0_bqkv_pk", [128, 24])
        din("l0_vbias_bc", [128, D])
        din("l0_wproj", [D, D])
        din("l0_bproj_bc", [128, D])
    for lx in ("l0", "l1"):
        if lx + "ffn" in need:
            din(lx + "_gatewT", [D, 32])
            din(lx + "_gbias_bc", [128, 32])
            din(lx + "_eg", [32, D, 512])
            din(lx + "_eu", [32, D, 512])
            din(lx + "_ed", [32, 512, D])
            din(lx + "_sg", [D, D])
            din(lx + "_su", [D, D])
            din(lx + "_sd", [D, D])
    if "l1attn" in need:
        din("l1_lnw_bc", [128, D])
        din("l1_lnb_bc", [128, D])
        din("l1_wq", [D, D])
        din("l1_wk", [D, D])
        din("l1_wv", [D, 2 * D])
        din("l1_wb", [D, 4])
        din("l1_onw_bc", [128, 512])
        din("l1_wo", [2 * D, D])

    scr_prev = nc.dram_tensor("scr_prev", [1, 255], F32, kind="Internal").ap()
    scr_wT = nc.dram_tensor("scr_wT", [32, T], F32, kind="Internal").ap()
    scr_nT = nc.dram_tensor("scr_nT", [D, T], BF16, kind="Internal").ap()
    b_prevd = Buf("prevd")
    b_wTd = Buf("wTd")
    b_nTd = Buf("nTd")

    RES = nc.alloc_sbuf_tensor("RES", [128, NT * D], F32)
    SM = {
        "ident_bf": nc.alloc_sbuf_tensor("sm_identbf", [128, 128], BF16),
        "ident_f32": nc.alloc_sbuf_tensor("sm_identf32", [128, 128], F32),
        "triu": nc.alloc_sbuf_tensor("sm_triu", [128, 128], F32),
        "slo": nc.alloc_sbuf_tensor("sm_slo", [128, 128], F32),
        "D": nc.alloc_sbuf_tensor("sm_D", [128, 128], F32),
        "gp": nc.alloc_sbuf_tensor("sm_gp", [1, 512], F32),
        "stats": nc.alloc_sbuf_tensor("sm_stats", [128, 8], F32),
        "scr1k": nc.alloc_sbuf_tensor("sm_scr1k", [128, 1024], F32),
        "rt": nc.alloc_sbuf_tensor("sm_rt", [128, 1024], F32),
        "eps": nc.alloc_sbuf_tensor("sm_eps", [128, 2], F32),
    }
    b_res = [Buf(f"res{k}") for k in range(NT)]
    b_sm = Buf("sm")
    b_rt = Buf("rt")
    b_stats = Buf("stats")
    b_scr = Buf("scr")

    PS_A = nc.alloc_psum_tensor("PS_A", [128, 2048], F32)
    PS_B = nc.alloc_psum_tensor("PS_B", [128, 1024], F32)
    PS_C = nc.alloc_psum_tensor("PS_C", [128, 1024], F32)
    bA = [Buf(f"psA{i}") for i in range(4)]
    bB0, bB1 = Buf("psB0"), Buf("psB1")
    bC0, bC1 = Buf("psC0"), Buf("psC1")

    def psA(i):
        return PS_A[:, i * 512:(i + 1) * 512]

    P_reg = {}
    PS_B_bf = PS_B.ap().bitcast(BF16)
    PS_C_bf = PS_C.ap().bitcast(BF16)
    PS_A_bf = PS_A.ap().bitcast(BF16)

    st_ln = SM["stats"][:, 4:8]

    def ln_tile(out_ap, b_out, w_ap=None, b_ap=None, b_w=None):
        rt = SM["rt"][:]
        s1, mu, ve, rs = (st_ln[:, i:i + 1] for i in range(4))
        P.step("vector", lambda v: v.reduce_sum(s1, rt, axis=AX.X),
               reads=[b_rt], writes=[b_stats])
        P.step("scalar", lambda s: s.mul(mu, s1, 1.0 / 1024.0),
               reads=[b_stats], writes=[b_stats])
        P.step("vector", lambda v: v.tensor_scalar(rt, rt, mu, None, ALU.subtract),
               reads=[b_rt, b_stats], writes=[b_rt])
        P.step("vector", lambda v: v.tensor_mul(SM["scr1k"][:], rt, rt),
               reads=[b_rt], writes=[b_scr])
        P.step("vector", lambda v: v.reduce_sum(ve, SM["scr1k"][:], axis=AX.X),
               reads=[b_scr], writes=[b_stats])
        P.step("scalar", lambda s: s.activation(rs, ve, AF.Ln, scale=1.0 / 1024.0,
                                                bias=SM["eps"][:, 0:1]),
               reads=[b_stats], writes=[b_stats])
        P.step("scalar", lambda s: s.activation(rs, rs, AF.Exp, scale=-0.5),
               reads=[b_stats], writes=[b_stats])
        if w_ap is None:
            return P.step("vector", lambda v: v.tensor_scalar(
                out_ap, rt, rs, None, ALU.mult),
                reads=[b_rt, b_stats], writes=[b_out])
        P.step("vector", lambda v: v.scalar_tensor_tensor(
            out_ap, rt, rs, w_ap, ALU.mult, ALU.mult),
            reads=[b_rt, b_stats, b_w], writes=[b_out])
        return P.step("vector", lambda v: v.tensor_tensor(out_ap, out_ap, b_ap, ALU.add),
                      reads=[b_out, b_w], writes=[b_out])

    def transpose128(src_ap, dst_ap, ps_ap, ps_buf, src_buf, dst_buf,
                     ident=None, evac="scalar"):
        idn = SM["ident_bf"][:] if ident is None else ident
        P.step("tensor", lambda t: t.transpose(ps_ap, src_ap, idn),
               reads=[src_buf, b_sm], writes=[ps_buf])
        if evac == "scalar":
            return P.step("scalar", lambda s: s.copy(dst_ap, ps_ap),
                          reads=[ps_buf], writes=[dst_buf])
        return P.step("vector", lambda v: v.tensor_copy(dst_ap, ps_ap),
                      reads=[ps_buf], writes=[dst_buf])

    P.dma("sync", [(RES[:].rearrange("p (k d) -> p k d", d=D),
                    resid_in.rearrange("(k p) d -> p k d", p=128))],
          writes=b_res)
    P.dma("sync", [(SM["ident_bf"][:], dt_in["ident_bf"]),
                   (SM["ident_f32"][:], dt_in["ident_f32"]),
                   (SM["triu"][:], dt_in["triu_incl"]),
                   (SM["slo"][:], dt_in["slo_mask"]),
                   (SM["gp"][0:1, 0:1], dt_in["gamma"]),
                   (SM["eps"][:], dt_in["eps_bc"])], writes=[b_sm])

    gp = SM["gp"]
    Q0 = 64

    P.step("vector", lambda v: v.memset(gp[0:1, Q0:Q0 + 1], 1.0),
           reads=[b_sm], writes=[b_sm])
    n = 1
    while n <= 64:
        P.step("vector", lambda v, n=n: v.tensor_tensor(
            gp[0:1, 32:33], gp[0:1, Q0 + n - 1:Q0 + n], gp[0:1, 0:1], ALU.mult),
            reads=[b_sm], writes=[b_sm])
        P.step("vector", lambda v, n=n: v.tensor_scalar(
            gp[0:1, Q0 + n:Q0 + 2 * n], gp[0:1, Q0:Q0 + n], gp[0:1, 32:33],
            None, ALU.mult), reads=[b_sm], writes=[b_sm])
        n *= 2
    P.step("vector", lambda v: v.memset(gp[0:1, 384:512], 0.0),
           reads=[b_sm], writes=[b_sm])
    P.step("vector", lambda v: v.tensor_copy(gp[0:1, 256:384],
                                             gp[0:1, Q0 + 127:Q0 - 1:-1]),
           reads=[b_sm], writes=[b_sm])
    P.dma("sync", [(scr_prev, gp[0:1, 256:511])], reads=[b_sm], writes=[b_prevd])
    P.dma("gpsimd",
          [(SM["D"][i:i + 1, :], scr_prev[0:1, 127 - i:255 - i]) for i in range(128)],
          reads=[b_prevd], writes=[b_sm])

    sec_dep = [P.barrier_deps()]

    def build_yt(YT, b_yt, xbf, b_xbf):
        for k in range(NT):
            par = k % 2
            xb = xbf[:, par * 1024:(par + 1) * 1024]
            P.step("scalar", lambda s, k=k, xb=xb: s.copy(xb, RES[:, k * D:(k + 1) * D]),
                   reads=[b_res[k]], writes=[b_xbf[par]])
            for j in range(ND):
                sl = j % 4
                pa = PS_A_bf[:, sl * 1024:sl * 1024 + 128]
                transpose128(xb[:, j * 128:(j + 1) * 128],
                             YT[:, j * T + k * 128: j * T + (k + 1) * 128],
                             pa, bA[sl], b_xbf[par], b_yt,
                             evac="vector" if (j % 2) else "scalar")

    def sec_l0attn():
        ar = nc.auto_arena()
        YT = ar.sbuf_tensor("a0_YT", [128, ND * T], BF16)
        Kf = ar.sbuf_tensor("a0_K", [128, ND * T], BF16)
        Vt = ar.sbuf_tensor("a0_V", [128, NT * 1024], BF16)
        wcol = ar.sbuf_tensor("a0_wcol", [128, 2 * 8 * 128], BF16)
        bpb = ar.sbuf_tensor("a0_bp", [128, 1024], F32)
        bqk = ar.sbuf_tensor("a0_bqk", [128, 24], F32)
        ar.allocate()
        ar_y = nc.auto_arena()
        Wv = ar_y.sbuf_tensor("a0_Wv", [128, 8 * 1024], BF16)
        xbf = ar_y.sbuf_tensor("a0_xbf", [128, 2048], BF16)
        vb = ar_y.sbuf_tensor("a0_vb", [128, 1024], F32)
        ar_y.allocate()

        b_yt = Buf("yt"); b_q = Buf("q"); b_k = Buf("k"); b_v = Buf("v")
        b_wv = Buf("wv"); b_wcol = [Buf("wc0"), Buf("wc1")]
        b_xbf = [Buf("xb0"), Buf("xb1")]
        b_s = Buf("s"); b_p = Buf("p"); b_pt = Buf("pt"); b_o = Buf("o")
        b_ot = Buf("ot"); b_wp = [Buf("wp0"), Buf("wp1")]
        b_cst = Buf("a0cst")

        P.fence(sec_dep[0])
        P.dma("sync", [(vb[:], dt_in["l0_vbias_bc"]), (bpb[:], dt_in["l0_bproj_bc"]),
                       (bqk[:], dt_in["l0_bqkv_pk"])], writes=[b_cst])
        build_yt(YT, b_yt, xbf, b_xbf)

        wqkv = dt_in["l0_wqkv"]
        P.dma("gpsimd", [(Wv[:].rearrange("p (k n) -> p k n", n=1024),
                          wqkv[:, 2 * D:3 * D].rearrange("(k p) n -> p k n", p=128))],
              writes=[b_wv])
        for tt in range(NT):
            ps = PS_B if tt % 2 == 0 else PS_C
            bp = [bB0, bB1] if tt % 2 == 0 else [bC0, bC1]

            def em(t, tt=tt, ps=ps):
                ins = None
                for d in range(ND):
                    lhsT = YT[:, d * T + tt * 128: d * T + (tt + 1) * 128]
                    for h in range(2):
                        ins = t.matmul(ps[:, h * 512:(h + 1) * 512], lhsT,
                                       Wv[:, d * 1024 + h * 512: d * 1024 + (h + 1) * 512],
                                       start=(d == 0), stop=(d == ND - 1))
                return ins
            P.step("tensor", em, reads=[b_yt, b_wv], writes=bp)
            P.step("vector", lambda v, tt=tt, ps=ps: v.scalar_tensor_tensor(
                Vt[:, tt * 1024:(tt + 1) * 1024], ps[:], 0.0, vb[:], ALU.add, ALU.add),
                reads=bp + [b_cst], writes=[b_v])
        for j in range(ND):
            par = j % 2
            wc = wcol[:, par * 1024:(par + 1) * 1024]
            col = D + j * 128
            P.dma("gpsimd", [(wc.rearrange("p (k c) -> p k c", c=128),
                              wqkv[:, col:col + 128].rearrange("(k p) c -> p k c", p=128))],
                  writes=[b_wcol[par]])

            def em(t, wc=wc):
                ins = None
                for d in range(ND):
                    lhsT = wc[:, d * 128:(d + 1) * 128]
                    for tc in range(4):
                        ins = t.matmul(psA(tc), lhsT,
                                       YT[:, d * T + tc * 512: d * T + (tc + 1) * 512],
                                       start=(d == 0), stop=(d == ND - 1))
                return ins
            P.step("tensor", em, reads=[b_wcol[par], b_yt], writes=bA)
            for tc in range(4):
                P.step("scalar", lambda s, j=j, tc=tc: s.activation(
                    Kf[:, j * T + tc * 512: j * T + (tc + 1) * 512], psA(tc),
                    AF.Identity, bias=bqk[:, 8 + j: 9 + j]),
                    reads=[bA[tc], b_cst], writes=[b_k])
        P.fence(P.barrier_deps())
        ar_y.free()
        ar_d = nc.auto_arena()
        Qf = ar_d.sbuf_tensor("a0_Q", [128, 8 * 512], BF16)
        pbf = ar_d.sbuf_tensor("a0_p", [128, 2048], BF16)
        ptb = ar_d.sbuf_tensor("a0_pt", [128, 16 * 128], BF16)
        obf = ar_d.sbuf_tensor("a0_o", [128, 1024], BF16)
        otb = ar_d.sbuf_tensor("a0_ot", [128, 8 * 128], BF16)
        wpb = ar_d.sbuf_tensor("a0_wp", [128, 2 * 1024], BF16)
        ar_d.allocate()
        sc = 1.0 / float(np.sqrt(D))
        st = SM["stats"]
        b_p2_box = [Buf("p2")]
        for g in range(4):
            for j in range(ND):
                par = j % 2
                wc = wcol[:, par * 1024:(par + 1) * 1024]
                P.dma("gpsimd", [(wc.rearrange("p (k c) -> p k c", c=128),
                                  wqkv[:, j * 128:(j + 1) * 128].rearrange("(k p) c -> p k c", p=128))],
                      writes=[b_wcol[par]])
                sl = j % 4

                def em(t, wc=wc, g=g, sl=sl):
                    ins = None
                    for d in range(ND):
                        ins = t.matmul(psA(sl), wc[:, d * 128:(d + 1) * 128],
                                       YT[:, d * T + g * 512: d * T + (g + 1) * 512],
                                       start=(d == 0), stop=(d == ND - 1))
                    return ins
                P.step("tensor", em, reads=[b_wcol[par], b_yt], writes=[bA[sl]])
                P.step("scalar", lambda s, j=j, sl=sl: s.activation(
                    Qf[:, j * 512:(j + 1) * 512], psA(sl), AF.Identity,
                    bias=bqk[:, j:j + 1]), reads=[bA[sl], b_cst], writes=[b_q])
            for qt in range(g * 4, g * 4 + 4):
                def em(t, qt=qt):
                    ins = None
                    for j in range(ND):
                        lhsT = Qf[:, j * 512 + (qt % 4) * 128: j * 512 + (qt % 4 + 1) * 128]
                        for kc in range(4):
                            ins = t.matmul(psA(kc), lhsT,
                                           Kf[:, j * T + kc * 512: j * T + (kc + 1) * 512],
                                           start=(j == 0), stop=(j == ND - 1))
                    return ins
                P.step("tensor", em, reads=[b_q, b_k], writes=bA)
                P.step("vector", lambda v, qt=qt: v.tensor_tensor(
                    PS_A[:, qt * 128:(qt + 1) * 128], PS_A[:, qt * 128:(qt + 1) * 128],
                    SM["D"][:], ALU.mult), reads=bA + [b_sm], writes=bA)
                P.step("vector", lambda v: v.reduce_max(st[:, 0:1], PS_A[:], axis=AX.X,
                                                        negate=True),
                       reads=bA, writes=[b_stats])
                P.step("scalar", lambda s: s.mul(st[:, 1:2], st[:, 0:1], sc),
                       reads=[b_stats], writes=[b_stats])
                P.step("scalar", lambda s: s.activation(pbf[:], PS_A[:],
                                                        AF.Identity if DBG_IDENT else AF.Exp,
                                                        bias=st[:, 1:2], scale=sc),
                       reads=bA + [b_stats], writes=[b_p])
                if DBG_EXP and qt == 15:
                    dap = nc.dram_tensor("dump_exp", [128, 2048], BF16,
                                         kind="ExternalOutput").ap()
                    sap = nc.dram_tensor("dump_st", [128, 8], F32,
                                         kind="ExternalOutput").ap()
                    P.dma("sync", [(dap, pbf[:])], reads=[b_p], writes=[b_p2_box[0]])
                    P.dma("sync", [(sap, st[:])], reads=[b_stats], writes=[b_p2_box[0]])
                P.step("vector", lambda v: v.reduce_sum(st[:, 2:3], pbf[:], axis=AX.X),
                       reads=[b_p], writes=[b_stats])
                P.step("vector", lambda v: v.reciprocal(st[:, 3:4], st[:, 2:3]),
                       reads=[b_stats], writes=[b_stats])
                P.step("vector", lambda v: v.tensor_scalar(pbf[:], pbf[:], st[:, 3:4],
                                                           None, ALU.mult),
                       reads=[b_p, b_stats], writes=[b_p])
                for kt in range(NT):
                    sl2 = kt % 2
                    transpose128(pbf[:, kt * 128:(kt + 1) * 128],
                                 ptb[:, kt * 128:(kt + 1) * 128],
                                 PS_B_bf[:, sl2 * 1024: sl2 * 1024 + 128],
                                 bB0 if sl2 == 0 else bB1, b_p, b_pt,
                                 evac="vector" if kt % 2 else "scalar")

                def em(t):
                    ins = None
                    for kt in range(NT):
                        lhsT = ptb[:, kt * 128:(kt + 1) * 128]
                        for h in range(2):
                            ins = t.matmul(PS_C[:, h * 512:(h + 1) * 512], lhsT,
                                           Vt[:, kt * 1024 + h * 512: kt * 1024 + (h + 1) * 512],
                                           start=(kt == 0), stop=(kt == NT - 1))
                    return ins
                P.step("tensor", em, reads=[b_pt, b_v], writes=[bC0, bC1])
                P.step("scalar", lambda s: s.copy(obf[:], PS_C[:]),
                       reads=[bC0, bC1], writes=[b_o])
                for j in range(ND):
                    sl2 = j % 2
                    transpose128(obf[:, j * 128:(j + 1) * 128],
                                 otb[:, j * 128:(j + 1) * 128],
                                 PS_C_bf[:, sl2 * 1024: sl2 * 1024 + 128],
                                 bC0 if sl2 == 0 else bC1, b_o, b_ot,
                                 evac="vector" if j % 2 else "scalar")
                for j in range(ND):
                    par = j % 2
                    wp = wpb[:, par * 1024:(par + 1) * 1024]
                    P.dma("gpsimd", [(wp, dt_in["l0_wproj"][j * 128:(j + 1) * 128, :])],
                          writes=[b_wp[par]])

                    def em(t, j=j, wp=wp):
                        ins = None
                        for h in range(2):
                            ins = t.matmul(PS_B[:, h * 512:(h + 1) * 512],
                                           otb[:, j * 128:(j + 1) * 128],
                                           wp[:, h * 512:(h + 1) * 512],
                                           start=(j == 0), stop=(j == ND - 1))
                        return ins
                    P.step("tensor", em, reads=[b_ot, b_wp[par]], writes=[bB0, bB1])
                rt = SM["rt"]
                P.step("vector", lambda v, qt=qt: v.tensor_tensor(
                    rt[:], PS_B[:], RES[:, qt * D:(qt + 1) * D], ALU.add),
                    reads=[bB0, bB1, b_res[qt]], writes=[b_rt])
                P.step("vector", lambda v: v.tensor_tensor(rt[:], rt[:], bpb[:], ALU.add),
                       reads=[b_rt, b_cst], writes=[b_rt])
                ln_tile(RES[:, qt * D:(qt + 1) * D], b_res[qt])
        P_reg.update(Dm=(SM["D"], b_sm),
                     YT=(YT, b_yt), Kf=(Kf, b_k), Vt=(Vt, b_v), Qf=(Qf, b_q),
                     pbf=(pbf, b_p), ptb=(ptb, b_pt), obf=(obf, b_o),
                     otb=(otb, b_ot))
        tok = P.barrier_deps()
        ar_d.free()
        ar.free()
        return tok

    def sec_ffn(lx, topk):
        ar = nc.auto_arena()
        YT = ar.sbuf_tensor(lx + "YT", [128, ND * T], BF16)
        ar.allocate()
        ar_r = nc.auto_arena()
        gwT = ar_r.sbuf_tensor(lx + "gwT", [128, 8 * 32], BF16)
        gbb = ar_r.sbuf_tensor(lx + "gbb", [128, 32], F32)
        scs = ar_r.sbuf_tensor(lx + "scs", [128, NT * 32], F32)
        w32 = ar_r.sbuf_tensor(lx + "w32", [128, NT * 32], F32)
        rsc = ar_r.sbuf_tensor(lx + "rsc", [128, 128], F32)
        wTs = ar_r.sbuf_tensor(lx + "wTs", [32, NT * 128], F32)
        xbf = ar_r.sbuf_tensor(lx + "xbf", [128, 2048], BF16)
        ar_r.allocate()

        b_yt = Buf("yt"); b_xbf = [Buf("xb0"), Buf("xb1")]
        b_wg = [Buf("wg0"), Buf("wg1")]; b_wu = [Buf("wu0"), Buf("wu1")]
        b_wd = [Buf("wd0"), Buf("wd1")]; b_wb = [Buf("wb0"), Buf("wb1")]
        b_hid = [[Buf(f"h{p}{i}") for i in range(4)] for p in range(2)]
        b_ts = [Buf("ts0"), Buf("ts1")]; b_ub = [Buf("ub0"), Buf("ub1")]
        b_h1 = Buf("h1")
        b_cst = Buf("cst"); b_scs = Buf("scs"); b_w32 = Buf("w32")
        b_rsc = Buf("rsc"); b_wTs = Buf("wTs")

        P.fence(sec_dep[0])
        P.dma("sync", [(gbb[:], dt_in[lx + "_gbias_bc"])], writes=[b_cst])
        P.dma("gpsimd", [(gwT[:].rearrange("p (k c) -> p k c", c=32),
                          dt_in[lx + "_gatewT"].rearrange("(k p) c -> p k c", p=128))],
              writes=[b_cst])
        build_yt(YT, b_yt, xbf, b_xbf)

        st = SM["stats"]
        for tt in range(NT):
            sl = tt % 4

            def em(t, tt=tt, sl=sl):
                ins = None
                for d in range(ND):
                    ins = t.matmul(psA(sl)[:, 0:32],
                                   YT[:, d * T + tt * 128: d * T + (tt + 1) * 128],
                                   gwT[:, d * 32:(d + 1) * 32],
                                   start=(d == 0), stop=(d == ND - 1))
                return ins
            P.step("tensor", em, reads=[b_yt, b_cst], writes=[bA[sl]])
            P.step("scalar", lambda s, tt=tt, sl=sl: s.activation(
                scs[:, tt * 32:(tt + 1) * 32], psA(sl)[:, 0:32], AF.Sigmoid),
                reads=[bA[sl]], writes=[b_scs])
            sfc = rsc[:, 0:32]; gs = rsc[:, 32:40]; m8 = rsc[:, 40:48]
            gm = rsc[:, 48:52]; msk = rsc[:, 64:96]; zap = rsc[:, 96:128]
            sct = scs[:, tt * 32:(tt + 1) * 32]

            mm8 = rsc[:, 96:128]

            def em_r1(v, sfc=sfc, mm8=mm8, sct=sct):
                v.tensor_tensor(sfc, sct, gbb[:], ALU.add)
                ins = None
                for g in range(4):
                    ins = v.max(out=mm8[:, g * 8:(g + 1) * 8],
                                in_=sfc[:, g * 8:(g + 1) * 8])
                return ins
            P.step("vector", em_r1, reads=[b_scs, b_cst], writes=[b_rsc])

            def em_r1b(v, gs=gs, mm8=mm8):
                v.memset(gs[:, 4:8], NEG)
                ins = None
                for g in range(4):
                    ins = v.tensor_tensor(gs[:, g:g + 1], mm8[:, g * 8:g * 8 + 1],
                                          mm8[:, g * 8 + 1:g * 8 + 2], ALU.add)
                return ins
            P.step("vector", em_r1b, reads=[b_rsc], writes=[b_rsc])
            P.step("vector", lambda v, gs=gs, m8=m8: v.max(out=m8, in_=gs),
                   reads=[b_rsc], writes=[b_rsc])
            P.step("vector", lambda v, gs=gs, gm=gm, m8=m8: v.tensor_scalar(
                gm, gs[:, 0:4], m8[:, 1:2], None, ALU.is_ge),
                reads=[b_rsc], writes=[b_rsc])

            def em_r2(v, sfc=sfc, gm=gm, msk=msk):
                ins = None
                for g in range(4):
                    ins = v.tensor_scalar(msk[:, g * 8:(g + 1) * 8],
                                          sfc[:, g * 8:(g + 1) * 8],
                                          gm[:, g:g + 1], None, ALU.mult)
                return ins
            P.step("vector", em_r2, reads=[b_rsc], writes=[b_rsc])

            P.step("vector", lambda v, msk=msk, m8=m8: v.max(out=m8, in_=msk),
                   reads=[b_rsc], writes=[b_rsc])
            if topk < 8:
                P.step("vector", lambda v, m8=m8: v.memset(m8[:, topk:8], NEG),
                       reads=[b_rsc], writes=[b_rsc])
            zap = rsc[:, 96:128]
            P.step("vector", lambda v, msk=msk, zap=zap, m8=m8: v.match_replace(
                out=zap, in_to_replace=m8, in_values=msk, imm_value=NEG),
                reads=[b_rsc], writes=[b_rsc])

            def em_r3(v, msk=msk, zap=zap, sct=sct):
                v.tensor_tensor(zap, msk, zap, ALU.not_equal)
                v.tensor_tensor(zap, zap, sct, ALU.mult)
                v.reduce_sum(st[:, 0:1], zap, axis=AX.X)
                v.tensor_scalar(st[:, 0:1], st[:, 0:1], 1e-20, None, ALU.add)
                return v.reciprocal(st[:, 1:2], st[:, 0:1])
            P.step("vector", em_r3, reads=[b_rsc, b_scs], writes=[b_rsc, b_stats])
            P.step("vector", lambda v, tt=tt, zap=zap: v.tensor_scalar(
                w32[:, tt * 32:(tt + 1) * 32], zap, st[:, 1:2], 2.5,
                ALU.mult, ALU.mult), reads=[b_rsc, b_stats], writes=[b_w32])
            sl2 = tt % 2
            bps = bB0 if sl2 == 0 else bB1
            P.step("tensor", lambda t, tt=tt, sl2=sl2: t.transpose(
                PS_B[0:32, sl2 * 512: sl2 * 512 + 128],
                w32[:, tt * 32:(tt + 1) * 32], SM["ident_f32"][:]),
                reads=[b_w32, b_sm], writes=[bps])
            P.step("vector", lambda v, tt=tt, sl2=sl2: v.tensor_copy(
                wTs[0:32, tt * 128:(tt + 1) * 128],
                PS_B[0:32, sl2 * 512: sl2 * 512 + 128]),
                reads=[bps], writes=[b_wTs])
        P.dma("sync", [(scr_wT, wTs[0:32, :])], reads=[b_wTs], writes=[b_wTd])
        if DBG_ROUTE:
            d1 = nc.dram_tensor(lx + "dump_w32", [128, NT * 32], F32,
                                kind="ExternalOutput").ap()
            d2 = nc.dram_tensor(lx + "dump_scs", [128, NT * 32], F32,
                                kind="ExternalOutput").ap()
            d3 = nc.dram_tensor(lx + "dump_rsc", [128, 128], F32,
                                kind="ExternalOutput").ap()
            P.dma("sync", [(d1, w32[:]), (d2, scs[:]), (d3, rsc[:])],
                  reads=[b_w32, b_scs, b_rsc], writes=[Buf("dbgrt")])

        P.fence(P.barrier_deps())
        ar_r.free()
        ar_e = nc.auto_arena()
        WG = ar_e.sbuf_tensor(lx + "WG", [128, 2 * 8 * 512], BF16)
        WU = ar_e.sbuf_tensor(lx + "WU", [128, 2 * 8 * 512], BF16)
        WD = ar_e.sbuf_tensor(lx + "WD", [128, 2 * 4 * 1024], BF16)
        HID = ar_e.sbuf_tensor(lx + "HID", [128, 2 * 4 * 2048], BF16)
        WB = ar_e.sbuf_tensor(lx + "WB", [128, 2 * 2048], BF16)
        tsil = ar_e.sbuf_tensor(lx + "tsil", [128, 2 * 1024], BF16)
        ubf = ar_e.sbuf_tensor(lx + "ubf", [128, 2 * 1024], BF16)
        h1b = ar_e.sbuf_tensor(lx + "h1b", [128, 1024], BF16)
        ar_e.allocate()

        eg, eu, ed = dt_in[lx + "_eg"], dt_in[lx + "_eu"], dt_in[lx + "_ed"]
        sg, su, sd = dt_in[lx + "_sg"], dt_in[lx + "_su"], dt_in[lx + "_sd"]

        def wsrc(e):
            if e < 32:
                return (eg[e], eu[e], ed[e], e)
            half = e - 32
            sl = slice(half * 512, (half + 1) * 512)
            return (sg[:, sl], su[:, sl], sd[sl, :], None)

        for e in range(34):
            par = e % 2
            g_src, u_src, d_src, widx = wsrc(e)
            wg = WG[:, par * 4096:(par + 1) * 4096]
            wu = WU[:, par * 4096:(par + 1) * 4096]
            wd = WD[:, par * 4096:(par + 1) * 4096]
            wb = WB[:, par * 2048:(par + 1) * 2048]
            P.dma("gpsimd", [(wg.rearrange("p (k n) -> p k n", n=512),
                              g_src.rearrange("(k p) n -> p k n", p=128)),
                             (wu.rearrange("p (k n) -> p k n", n=512),
                              u_src.rearrange("(k p) n -> p k n", p=128)),
                             (wd.rearrange("p (k n) -> p k n", n=1024),
                              d_src.rearrange("(k p) n -> p k n", p=128))],
                  writes=[b_wg[par], b_wu[par], b_wd[par]])
            if widx is not None:
                P.dma("gpsimd", [(wb, scr_wT[widx:widx + 1, :].to_broadcast([128, 2048]))],
                      reads=[b_wTd], writes=[b_wb[par]])
            else:
                P.step("vector", lambda v, wb=wb: v.memset(wb, 1.0),
                       writes=[b_wb[par]])
            for i in range(4):
                for th in range(2):
                    pp = (2 * i + th) % 2
                    gsl = [0, 1] if pp == 0 else [2, 3]

                    def em_g(t, i=i, th=th, wg=wg, gsl=gsl):
                        ins = None
                        for q in range(2):
                            for d in range(ND):
                                lhsT = wg[:, d * 512 + i * 128: d * 512 + (i + 1) * 128]
                                ins = t.matmul(psA(gsl[q]), lhsT,
                                               YT[:, d * T + th * 1024 + q * 512:
                                                  d * T + th * 1024 + (q + 1) * 512],
                                               start=(d == 0), stop=(d == ND - 1))
                        return ins
                    P.step("tensor", em_g, reads=[b_wg[par], b_yt],
                           writes=[bA[gsl[0]], bA[gsl[1]]])
                    tsl = tsil[:, par * 1024:(par + 1) * 1024]
                    P.step("scalar", lambda s, gsl=gsl, tsl=tsl: s.activation(
                        tsl, PS_A[:, gsl[0] * 512:(gsl[1] + 1) * 512], AF.Silu),
                        reads=[bA[gsl[0]], bA[gsl[1]]], writes=[b_ts[par]])

                    def em_u(t, i=i, th=th, wu=wu, gsl=gsl):
                        ins = None
                        for q in range(2):
                            for d in range(ND):
                                lhsT = wu[:, d * 512 + i * 128: d * 512 + (i + 1) * 128]
                                ins = t.matmul(psA(gsl[q]), lhsT,
                                               YT[:, d * T + th * 1024 + q * 512:
                                                  d * T + th * 1024 + (q + 1) * 512],
                                               start=(d == 0), stop=(d == ND - 1))
                        return ins
                    P.step("tensor", em_u, reads=[b_wu[par], b_yt, b_ts[par]],
                           writes=[bA[gsl[0]], bA[gsl[1]]])
                    ub = ubf[:, par * 1024:(par + 1) * 1024]
                    P.step("scalar", lambda s, gsl=gsl, ub=ub: s.copy(
                        ub, PS_A[:, gsl[0] * 512:(gsl[1] + 1) * 512]),
                        reads=[bA[gsl[0]], bA[gsl[1]]], writes=[b_ub[par]])
                    P.step("vector", lambda v, tsl=tsl, ub=ub: v.tensor_tensor(
                        h1b[:], tsl, ub, ALU.mult),
                        reads=[b_ts[par], b_ub[par]], writes=[b_h1])
                    hd = HID[:, par * 8192 + i * 2048 + th * 1024:
                             par * 8192 + i * 2048 + (th + 1) * 1024]
                    P.step("vector", lambda v, hd=hd, wb=wb, th=th: v.tensor_tensor(
                        hd, h1b[:], wb[:, th * 1024:(th + 1) * 1024], ALU.mult),
                        reads=[b_h1, b_wb[par]], writes=[b_hid[par][i]])
            if e % 2 == 1:
                for tt in range(NT):
                    ps = PS_B if tt % 2 == 0 else PS_C
                    bp = [bB0, bB1] if tt % 2 == 0 else [bC0, bC1]

                    def em_d(t, tt=tt, ps=ps):
                        ins = None
                        n = 0
                        for pp in range(2):
                            wdd = WD[:, pp * 4096:(pp + 1) * 4096]
                            for i in range(4):
                                lhsT = HID[:, pp * 8192 + i * 2048 + tt * 128:
                                           pp * 8192 + i * 2048 + (tt + 1) * 128]
                                for h in range(2):
                                    ins = t.matmul(ps[:, h * 512:(h + 1) * 512], lhsT,
                                                   wdd[:, i * 1024 + h * 512:
                                                       i * 1024 + (h + 1) * 512],
                                                   start=(n == 0), stop=(n == 7))
                                n += 1
                        return ins
                    rd = [b_wd[0], b_wd[1]] + b_hid[0] + b_hid[1]
                    P.step("tensor", em_d, reads=rd, writes=bp)
                    P.step("vector", lambda v, tt=tt, ps=ps: v.tensor_tensor(
                        RES[:, tt * D:(tt + 1) * D], RES[:, tt * D:(tt + 1) * D],
                        ps[:], ALU.add), reads=bp + [b_res[tt]], writes=[b_res[tt]])
        for tt in range(NT):
            P.step("vector", lambda v, tt=tt: v.tensor_copy(
                SM["rt"][:], RES[:, tt * D:(tt + 1) * D]),
                reads=[b_res[tt]], writes=[b_rt])
            ln_tile(RES[:, tt * D:(tt + 1) * D], b_res[tt])
        tok = P.barrier_deps()
        ar_e.free()
        ar.free()
        return tok

    def sec_l1attn():
        H, DK, DV = 4, 256, 512
        ar_pre = nc.auto_arena()
        lnw = ar_pre.sbuf_tensor("a1_lnw", [128, 1024], F32)
        lnb = ar_pre.sbuf_tensor("a1_lnb", [128, 1024], F32)
        nst = ar_pre.sbuf_tensor("a1_nst", [128, 1024], BF16)
        nst2 = ar_pre.sbuf_tensor("a1_nst2", [128, 1024], BF16)
        ar_pre.allocate()

        b_w = Buf("a1w")
        b_S = [Buf(f"S{h}") for h in range(H)]
        b_Sb = [Buf(f"Sb{h}") for h in range(H)]
        b_ncb = [Buf("nc0"), Buf("nc1")]
        b_qc = Buf("qc"); b_kc = Buf("kc"); b_vc = Buf("vc"); b_bay = Buf("bay"); b_QT = Buf("QT"); b_KT = Buf("KT")
        b_PWR = Buf("PWR"); b_Z = Buf("Z"); b_Mt = Buf("Mt"); b_Gt = Buf("Gt")
        b_B = Buf("B"); b_U = Buf("U"); b_of = Buf("of"); b_on = Buf("on")
        b_OT = Buf("OT"); b_cst = Buf("a1cst"); b_nst = Buf("nst")
        b_nst2 = Buf("nst2"); b_woq = [Buf(f"woq{i}") for i in range(2)]

        P.fence(sec_dep[0])
        P.dma("sync", [(lnw[:], dt_in["l1_lnw_bc"]), (lnb[:], dt_in["l1_lnb_bc"])],
              writes=[b_cst])

        nT_r = scr_nT.rearrange("(j p) t -> p j t", p=128)
        for tt in range(NT):
            P.step("vector", lambda v, tt=tt: v.tensor_copy(
                SM["rt"][:], RES[:, tt * D:(tt + 1) * D]),
                reads=[b_res[tt]], writes=[b_rt])
            ln_tile(nst[:], b_nst, w_ap=lnw[:], b_ap=lnb[:], b_w=b_cst)
            for j in range(ND):
                sl = j % 4
                transpose128(nst[:, j * 128:(j + 1) * 128],
                             nst2[:, j * 128:(j + 1) * 128],
                             PS_A_bf[:, sl * 1024: sl * 1024 + 128],
                             bA[sl], b_nst, b_nst2,
                             evac="vector" if j % 2 else "scalar")
            P.dma("sync", [(nT_r[:, :, tt * 128:(tt + 1) * 128],
                            nst2[:].rearrange("p (j c) -> p j c", c=128))],
                  reads=[b_nst2], writes=[b_nTd])

        P.fence(P.barrier_deps())
        ar_pre.free()
        ar = nc.auto_arena()
        Wq = ar.sbuf_tensor("a1_Wq", [128, 8 * 1024], BF16)
        Wk = ar.sbuf_tensor("a1_Wk", [128, 8 * 1024], BF16)
        Wv = ar.sbuf_tensor("a1_Wv", [128, 8 * 2048], BF16)
        Wb = ar.sbuf_tensor("a1_Wb", [128, 8 * 4], BF16)
        woq = ar.sbuf_tensor("a1_woq", [128, 2 * 1024], BF16)
        Sf = ar.sbuf_tensor("a1_S", [128, 4096], F32)
        Sb = ar.sbuf_tensor("a1_Sb", [128, 4096], BF16)
        ncb = ar.sbuf_tensor("a1_ncb", [128, 2 * 1024], BF16)
        qcb = ar.sbuf_tensor("a1_qcb", [128, 1024], BF16)
        kcb = ar.sbuf_tensor("a1_kcb", [128, 1024], BF16)
        vcb = ar.sbuf_tensor("a1_vcb", [128, 2048], BF16)
        bay = ar.sbuf_tensor("a1_bay", [128, 4], F32)
        QT = ar.sbuf_tensor("a1_QT", [128, 8 * 128], BF16)
        KT = ar.sbuf_tensor("a1_KT", [128, 8 * 128], BF16)
        PWR = ar.sbuf_tensor("a1_PWR", [128, 7 * 256], F32)
        Zb = ar.sbuf_tensor("a1_Z", [128, 2 * 128], F32)
        Mtb = ar.sbuf_tensor("a1_Mt", [128, 128], BF16)
        Gtb = ar.sbuf_tensor("a1_Gt", [128, 128], BF16)
        Bb = ar.sbuf_tensor("a1_B", [128, 512], BF16)
        Ub = ar.sbuf_tensor("a1_U", [128, 512], BF16)
        of = ar.sbuf_tensor("a1_of", [128, 512], F32)
        onb = ar.sbuf_tensor("a1_onb", [128, 512], BF16)
        OT = ar.sbuf_tensor("a1_OT", [128, 16 * 128], BF16)
        onw = ar.sbuf_tensor("a1_onw", [128, 512], F32)
        ar.allocate()
        P.dma("sync", [(onw[:], dt_in["l1_onw_bc"])], writes=[b_cst])
        P.dma("gpsimd", [
            (Wq[:].rearrange("p (k n) -> p k n", n=1024),
             dt_in["l1_wq"].rearrange("(k p) n -> p k n", p=128)),
            (Wk[:].rearrange("p (k n) -> p k n", n=1024),
             dt_in["l1_wk"].rearrange("(k p) n -> p k n", p=128)),
            (Wv[:].rearrange("p (k n) -> p k n", n=2048),
             dt_in["l1_wv"].rearrange("(k p) n -> p k n", p=128)),
            (Wb[:].rearrange("p (k n) -> p k n", n=4),
             dt_in["l1_wb"].rearrange("(k p) n -> p k n", p=128)),
        ], writes=[b_w])
        P.step("vector", lambda v: v.memset(Sf[:], 0.0), writes=b_S)
        P.step("vector", lambda v: v.memset(Sb[:], 0.0), writes=b_Sb)

        def PW(j, t):
            return PWR[:, j * 256 + t * 128: j * 256 + (t + 1) * 128]

        st = SM["stats"]
        for c in range(NT):
            par = c % 2
            nb = ncb[:, par * 1024:(par + 1) * 1024]
            P.dma("gpsimd", [(nb.rearrange("p (j x) -> p j x", x=128),
                              nT_r[:, :, c * 128:(c + 1) * 128])],
                  reads=[b_nTd], writes=[b_ncb[par]])

            def em_q(t, nb=nb):
                ins = None
                for d in range(ND):
                    lhsT = nb[:, d * 128:(d + 1) * 128]
                    for h2 in range(2):
                        ins = t.matmul(PS_B[:, h2 * 512:(h2 + 1) * 512], lhsT,
                                       Wq[:, d * 1024 + h2 * 512: d * 1024 + (h2 + 1) * 512],
                                       start=(d == 0), stop=(d == ND - 1))
                return ins
            P.step("tensor", em_q, reads=[b_ncb[par], b_w], writes=[bB0, bB1])
            P.step("scalar", lambda s: s.activation(SM["rt"][:], PS_B[:], AF.Silu),
                   reads=[bB0, bB1], writes=[b_rt])
            for h in range(H):
                qfh = SM["rt"][:, h * 256:(h + 1) * 256]
                P.step("vector", lambda v, qfh=qfh: v.tensor_mul(
                    SM["scr1k"][:, 0:256], qfh, qfh), reads=[b_rt], writes=[b_scr])
                P.step("vector", lambda v: v.reduce_sum(st[:, 0:1],
                                                        SM["scr1k"][:, 0:256], axis=AX.X),
                    reads=[b_scr], writes=[b_stats])
                P.step("scalar", lambda s: s.activation(st[:, 1:2], st[:, 0:1], AF.Ln,
                                                        bias=SM["eps"][:, 1:2]),
                       reads=[b_stats], writes=[b_stats])
                P.step("scalar", lambda s: s.activation(st[:, 1:2], st[:, 1:2],
                                                        AF.Exp, scale=-0.5),
                       reads=[b_stats], writes=[b_stats])
                P.step("scalar", lambda s: s.mul(st[:, 1:2], st[:, 1:2], 0.0625),
                       reads=[b_stats], writes=[b_stats])
                P.step("vector", lambda v, h=h, qfh=qfh: v.tensor_scalar(
                    qcb[:, h * 256:(h + 1) * 256], qfh, st[:, 1:2], None, ALU.mult),
                    reads=[b_rt, b_stats], writes=[b_qc])
            def em_b(t, nb=nb):
                ins = None
                for d in range(ND):
                    ins = t.matmul(PS_B[:, 0:4], nb[:, d * 128:(d + 1) * 128],
                                   Wb[:, d * 4:(d + 1) * 4],
                                   start=(d == 0), stop=(d == ND - 1))
                return ins
            P.step("tensor", em_b, reads=[b_ncb[par], b_w], writes=[bB0])
            P.step("scalar", lambda s: s.activation(bay[:], PS_B[:, 0:4], AF.Sigmoid),
                   reads=[bB0], writes=[b_bay])

            def em_k(t, nb=nb):
                ins = None
                for d in range(ND):
                    lhsT = nb[:, d * 128:(d + 1) * 128]
                    for h2 in range(2):
                        ins = t.matmul(PS_C[:, h2 * 512:(h2 + 1) * 512], lhsT,
                                       Wk[:, d * 1024 + h2 * 512: d * 1024 + (h2 + 1) * 512],
                                       start=(d == 0), stop=(d == ND - 1))
                return ins
            P.step("tensor", em_k, reads=[b_ncb[par], b_w], writes=[bC0, bC1])
            P.step("scalar", lambda s: s.activation(SM["rt"][:], PS_C[:], AF.Silu),
                   reads=[bC0, bC1], writes=[b_rt])
            for h in range(H):
                qfh = SM["rt"][:, h * 256:(h + 1) * 256]
                P.step("vector", lambda v, qfh=qfh: v.tensor_mul(
                    SM["scr1k"][:, 0:256], qfh, qfh), reads=[b_rt], writes=[b_scr])
                P.step("vector", lambda v: v.reduce_sum(st[:, 0:1],
                                                        SM["scr1k"][:, 0:256], axis=AX.X),
                    reads=[b_scr], writes=[b_stats])
                P.step("scalar", lambda s: s.activation(st[:, 1:2], st[:, 0:1], AF.Ln,
                                                        bias=SM["eps"][:, 1:2]),
                       reads=[b_stats], writes=[b_stats])
                P.step("scalar", lambda s: s.activation(st[:, 1:2], st[:, 1:2],
                                                        AF.Exp, scale=-0.5),
                       reads=[b_stats], writes=[b_stats])
                P.step("vector", lambda v, h=h, qfh=qfh: v.tensor_scalar(
                    kcb[:, h * 256:(h + 1) * 256], qfh, st[:, 1:2], None, ALU.mult),
                    reads=[b_rt, b_stats], writes=[b_kc])

            def em_v(t, nb=nb):
                ins = None
                for d in range(ND):
                    lhsT = nb[:, d * 128:(d + 1) * 128]
                    for q4 in range(4):
                        ins = t.matmul(psA(q4), lhsT,
                                       Wv[:, d * 2048 + q4 * 512: d * 2048 + (q4 + 1) * 512],
                                       start=(d == 0), stop=(d == ND - 1))
                return ins
            P.step("tensor", em_v, reads=[b_ncb[par], b_w], writes=bA)
            P.step("scalar", lambda s: s.activation(vcb[:], PS_A[:], AF.Silu),
                   reads=bA, writes=[b_vc])

            for h in range(H):
                for kt in range(2):
                    sl2 = (h * 2 + kt) % 2
                    transpose128(qcb[:, h * 256 + kt * 128: h * 256 + (kt + 1) * 128],
                                 QT[:, (h * 2 + kt) * 128:(h * 2 + kt + 1) * 128],
                                 PS_B_bf[:, sl2 * 1024: sl2 * 1024 + 128],
                                 bB0 if sl2 == 0 else bB1, b_qc, b_QT,
                                 evac="vector" if sl2 else "scalar")
                    transpose128(kcb[:, h * 256 + kt * 128: h * 256 + (kt + 1) * 128],
                                 KT[:, (h * 2 + kt) * 128:(h * 2 + kt + 1) * 128],
                                 PS_C_bf[:, sl2 * 1024: sl2 * 1024 + 128],
                                 bC0 if sl2 == 0 else bC1, b_kc, b_KT,
                                 evac="scalar" if sl2 else "vector")

            for h in range(H):
                slS = PS_B[:, 0:512]
                slO = PS_B[:, 512:1024]
                m0 = PS_C[:, 0:128]
                m1 = PS_C[:, 128:256]
                kth = [KT[:, (h * 2 + kt) * 128:(h * 2 + kt + 1) * 128] for kt in range(2)]
                qth = [QT[:, (h * 2 + kt) * 128:(h * 2 + kt + 1) * 128] for kt in range(2)]

                def em_A(t, kth=kth):
                    t.matmul(m0, kth[0], kth[0], start=True, stop=False)
                    return t.matmul(m0, kth[1], kth[1], start=False, stop=True)
                P.step("tensor", em_A, reads=[b_KT], writes=[bC0])
                P.step("vector", lambda v, h=h: v.scalar_tensor_tensor(
                    PW(0, 0), m0, bay[:, h:h + 1], SM["slo"][:], ALU.mult, ALU.mult),
                    reads=[bC0, b_bay, b_sm], writes=[b_PWR])
                transpose128(PW(0, 0), PW(0, 1),
                             PS_C[:, 512:640], bC1, b_PWR, b_PWR,
                             ident=SM["ident_f32"][:])
                for j in range(1, 7):
                    def em_sq(t, j=j):
                        t.matmul(m0, PW(j - 1, 1), PW(j - 1, 0), start=True, stop=True)
                        return t.matmul(m1, PW(j - 1, 0), PW(j - 1, 1),
                                        start=True, stop=True)
                    P.step("tensor", em_sq, reads=[b_PWR], writes=[bC0])
                    P.step("vector", lambda v, j=j: v.tensor_copy(
                        PWR[:, j * 256:(j + 1) * 256], PS_C[:, 0:256]),
                        reads=[bC0], writes=[b_PWR])
                zc = [Zb[:, 0:128], Zb[:, 128:256]]
                P.step("vector", lambda v: v.tensor_tensor(
                    zc[0], SM["ident_f32"][:], PW(6, 1), ALU.add),
                    reads=[b_PWR, b_sm], writes=[b_Z])
                cur = 0
                for lev in (5, 4, 3, 2, 1):
                    def em_z(t, lev=lev, cur=cur):
                        return t.matmul(m0, PW(lev, 0), zc[cur], start=True, stop=True)
                    P.step("tensor", em_z, reads=[b_PWR, b_Z], writes=[bC0])
                    P.step("vector", lambda v, cur=cur: v.tensor_tensor(
                        zc[1 - cur], m0, zc[cur], ALU.add),
                        reads=[bC0, b_Z], writes=[b_Z])
                    cur = 1 - cur

                def em_z0(t, cur=cur):
                    return t.matmul(m0, PW(0, 0), zc[cur], start=True, stop=True)
                P.step("tensor", em_z0, reads=[b_PWR, b_Z], writes=[bC0])
                P.step("vector", lambda v, cur=cur: v.tensor_tensor(
                    zc[1 - cur], zc[cur], m0, ALU.subtract),
                    reads=[bC0, b_Z], writes=[b_Z])
                cur = 1 - cur
                P.step("vector", lambda v, h=h, cur=cur: v.tensor_scalar(
                    Mtb[:], zc[cur], bay[:, h:h + 1], None, ALU.mult),
                    reads=[b_Z, b_bay], writes=[b_Mt])

                def em_ks(t, h=h, kth=kth):
                    t.matmul(slS, kth[0], Sb[:, (h * 2) * 512:(h * 2 + 1) * 512],
                             start=True, stop=False)
                    return t.matmul(slS, kth[1], Sb[:, (h * 2 + 1) * 512:(h * 2 + 2) * 512],
                                    start=False, stop=True)
                P.step("tensor", em_ks, reads=[b_KT, b_Sb[h]], writes=[bB0])
                P.step("vector", lambda v, h=h: v.tensor_tensor(
                    Bb[:], vcb[:, h * 512:(h + 1) * 512], slS, ALU.subtract),
                    reads=[bB0, b_vc], writes=[b_B])

                def em_U(t):
                    return t.matmul(slS, Mtb[:], Bb[:], start=True, stop=True)
                P.step("tensor", em_U, reads=[b_Mt, b_B], writes=[bB0])
                P.step("vector", lambda v: v.tensor_copy(Ub[:], slS),
                       reads=[bB0], writes=[b_U])

                def em_G(t, kth=kth, qth=qth):
                    t.matmul(m0, kth[0], qth[0], start=True, stop=False)
                    return t.matmul(m0, kth[1], qth[1], start=False, stop=True)
                P.step("tensor", em_G, reads=[b_KT, b_QT], writes=[bC0])
                P.step("vector", lambda v: v.tensor_tensor(
                    Gtb[:], m0, SM["triu"][:], ALU.mult),
                    reads=[bC0, b_sm], writes=[b_Gt])

                def em_O(t, h=h, qth=qth):
                    t.matmul(slO, qth[0], Sb[:, (h * 2) * 512:(h * 2 + 1) * 512],
                             start=True, stop=False)
                    t.matmul(slO, qth[1], Sb[:, (h * 2 + 1) * 512:(h * 2 + 2) * 512],
                             start=False, stop=False)
                    return t.matmul(slO, Gtb[:], Ub[:], start=False, stop=True)
                P.step("tensor", em_O, reads=[b_QT, b_Sb[h], b_Gt, b_U], writes=[bB1])
                P.step("scalar", lambda s: s.copy(of[:], slO), reads=[bB1], writes=[b_of])
                P.step("vector", lambda v: v.tensor_mul(
                    SM["scr1k"][:, 0:512], of[:], of[:]), reads=[b_of], writes=[b_scr])
                P.step("vector", lambda v: v.reduce_sum(st[:, 0:1],
                                                        SM["scr1k"][:, 0:512], axis=AX.X),
                    reads=[b_scr], writes=[b_stats])
                P.step("scalar", lambda s: s.activation(st[:, 1:2], st[:, 0:1], AF.Ln,
                                                        scale=1.0 / 512, bias=SM["eps"][:, 0:1]),
                       reads=[b_stats], writes=[b_stats])
                P.step("scalar", lambda s: s.activation(st[:, 1:2], st[:, 1:2],
                                                        AF.Exp, scale=-0.5),
                       reads=[b_stats], writes=[b_stats])
                P.step("vector", lambda v: v.scalar_tensor_tensor(
                    onb[:], of[:], st[:, 1:2], onw[:], ALU.mult, ALU.mult),
                    reads=[b_of, b_stats, b_cst], writes=[b_on])
                for kt in range(2):
                    def em_su(t, h=h, kt=kt):
                        return t.matmul(slS, kcb[:, h * 256 + kt * 128: h * 256 + (kt + 1) * 128],
                                        Ub[:], start=True, stop=True)
                    P.step("tensor", em_su, reads=[b_kc, b_U], writes=[bB0])
                    sfs = Sf[:, (h * 2 + kt) * 512:(h * 2 + kt + 1) * 512]
                    P.step("vector", lambda v, sfs=sfs: v.tensor_tensor(
                        sfs, sfs, slS, ALU.add), reads=[bB0, b_S[h]], writes=[b_S[h]])
                    P.step("scalar", lambda s, h=h, kt=kt, sfs=sfs: s.copy(
                        Sb[:, (h * 2 + kt) * 512:(h * 2 + kt + 1) * 512], sfs),
                        reads=[b_S[h]], writes=[b_Sb[h]])
                for dv in range(4):
                    sl2 = dv % 2
                    transpose128(onb[:, dv * 128:(dv + 1) * 128],
                                 OT[:, (h * 4 + dv) * 128:(h * 4 + dv + 1) * 128],
                                 PS_C_bf[:, 1024 + sl2 * 512: 1024 + sl2 * 512 + 128],
                                 bC1, b_on, b_OT,
                                 evac="vector" if sl2 else "scalar")

            for kt in range(16):
                sl4 = kt % 2
                wslice = woq[:, sl4 * 1024:(sl4 + 1) * 1024]
                P.dma("gpsimd", [(wslice, dt_in["l1_wo"][kt * 128:(kt + 1) * 128, :])],
                      writes=[b_woq[sl4]])

                def em_wo(t, kt=kt, wslice=wslice):
                    ins = None
                    lhsT = OT[:, kt * 128:(kt + 1) * 128]
                    for h2 in range(2):
                        ins = t.matmul(PS_A[:, h2 * 512:(h2 + 1) * 512], lhsT,
                                       wslice[:, h2 * 512:(h2 + 1) * 512],
                                       start=(kt == 0), stop=(kt == 15))
                    return ins
                if kt == 0:
                    P.step("tensor", em_wo, reads=[b_OT, b_woq[sl4]],
                           writes=[bA[0], bA[1]])
                else:
                    P.step("tensor", em_wo, reads=[b_OT, b_woq[sl4], bA[0], bA[1]],
                           writes=[bA[0], bA[1]] if kt == 15 else [])
            P.step("vector", lambda v, c=c: v.scalar_tensor_tensor(
                SM["rt"][:], RES[:, c * D:(c + 1) * D], 2.0, PS_A[:, 0:1024],
                ALU.mult, ALU.add),
                reads=[bA[0], bA[1], b_res[c]], writes=[b_rt])
            ln_tile(RES[:, c * D:(c + 1) * D], b_res[c])
        tok = P.barrier_deps()
        ar.free()
        return tok

    for s in sections:
        if s == "l0attn":
            sec_dep[0] = sec_l0attn()
        elif s == "l0ffn":
            sec_dep[0] = sec_ffn("l0", 8)
        elif s == "l1attn":
            sec_dep[0] = sec_l1attn()
        elif s == "l1ffn":
            sec_dep[0] = sec_ffn("l1", 4)

    for dn in dump:
        th, bf_ = P_reg[dn]
        dap = nc.dram_tensor("dump_" + dn, list(th.shape), th.dtype,
                             kind="ExternalOutput").ap()
        P.dma("sync", [(dap, th[:])], reads=[bf_], deps=sec_dep[0])
    P.dma("sync", [(out.rearrange("(k p) d -> p k d", p=128),
                    RES[:].rearrange("p (k d) -> p k d", d=D))],
          reads=b_res, deps=sec_dep[0])
    P.ops["sync"].append(
        ([], lambda eng, sem=P.sem["sync.D"], v=P.cnt["sync.D"]: eng.wait_ge(sem, v)))
    P.emit_all()
    return nc, sorted(dt_in.keys())



def _host_prepare(inputs):
    f = np.ascontiguousarray
    w = {}
    w["ident_bf"] = np.eye(128, dtype=ml_dtypes.bfloat16)
    w["ident_f32"] = np.eye(128, dtype=np.float32)
    r = np.arange(128)
    w["triu_incl"] = (r[None, :] >= r[:, None]).astype(np.float32)
    w["slo_mask"] = (r[:, None] > r[None, :]).astype(np.float32)
    w["gamma"] = np.asarray(inputs["L0_attn_gamma"], np.float32).reshape(1, 1)
    w["eps_bc"] = f(np.broadcast_to(np.array([1e-5, 1e-6], np.float32), (128, 2)))
    w["l0_wqkv"] = f(np.asarray(inputs["L0_attn_Wqkv"], np.float32))
    w["l0_bqkv_pk"] = f(np.asarray(inputs["L0_attn_bqkv"], np.float32).reshape(24, 128).T)
    w["l0_vbias_bc"] = f(np.broadcast_to(
        np.asarray(inputs["L0_attn_bqkv"], np.float32)[2048:], (128, 1024)))
    w["l0_wproj"] = f(np.asarray(inputs["L0_attn_Wproj"], np.float32))
    w["l0_bproj_bc"] = f(np.broadcast_to(
        np.asarray(inputs["L0_attn_bproj"], np.float32), (128, 1024)))
    w["l0_gatewT"] = f(np.asarray(inputs["L0_ffn_gate_w"], np.float32).T)
    w["l0_gbias_bc"] = f(np.broadcast_to(
        np.asarray(inputs["L0_ffn_gate_bias"], np.float32), (128, 32)))
    w["l0_eg"] = f(np.transpose(np.asarray(inputs["L0_ffn_expert_gate"], np.float32), (0, 2, 1)))
    w["l0_eu"] = f(np.transpose(np.asarray(inputs["L0_ffn_expert_up"], np.float32), (0, 2, 1)))
    w["l0_ed"] = f(np.transpose(np.asarray(inputs["L0_ffn_expert_down"], np.float32), (0, 2, 1)))
    w["l0_sg"] = f(np.asarray(inputs["L0_ffn_shared_gate"], np.float32).T)
    w["l0_su"] = f(np.asarray(inputs["L0_ffn_shared_up"], np.float32).T)
    w["l0_sd"] = f(np.asarray(inputs["L0_ffn_shared_down"], np.float32).T)
    w["l1_lnw_bc"] = f(np.broadcast_to(np.asarray(inputs["L1_attn_ln_w"], np.float32), (128, 1024)))
    w["l1_lnb_bc"] = f(np.broadcast_to(np.asarray(inputs["L1_attn_ln_b"], np.float32), (128, 1024)))
    w["l1_wq"] = f(np.asarray(inputs["L1_attn_Wq"], np.float32))
    w["l1_wk"] = f(np.asarray(inputs["L1_attn_Wk"], np.float32))
    w["l1_wv"] = f(np.asarray(inputs["L1_attn_Wv"], np.float32))
    w["l1_wb"] = f(np.asarray(inputs["L1_attn_Wb"], np.float32))
    w["l1_onw_bc"] = f(np.broadcast_to(np.asarray(inputs["L1_attn_o_norm_w"], np.float32), (128, 512)))
    w["l1_wo"] = f(np.asarray(inputs["L1_attn_Wo"], np.float32))
    w["l1_gatewT"] = f(np.asarray(inputs["L1_ffn_router_weight"], np.float32).T)
    w["l1_gbias_bc"] = f(np.broadcast_to(
        np.asarray(inputs["L1_ffn_router_bias"], np.float32), (128, 32)))
    w["l1_eg"] = f(np.transpose(np.asarray(inputs["L1_ffn_gate_w"], np.float32), (0, 2, 1)))
    w["l1_eu"] = f(np.transpose(np.asarray(inputs["L1_ffn_up_w"], np.float32), (0, 2, 1)))
    w["l1_ed"] = f(np.transpose(np.asarray(inputs["L1_ffn_down_w"], np.float32), (0, 2, 1)))
    w["l1_sg"] = f(np.asarray(inputs["L1_ffn_shared_gate_w"], np.float32).T)
    w["l1_su"] = f(np.asarray(inputs["L1_ffn_shared_up_w"], np.float32).T)
    w["l1_sd"] = f(np.asarray(inputs["L1_ffn_shared_down_w"], np.float32).T)
    return w


def kernel(**inputs):
    w = _host_prepare(inputs)
    x = np.ascontiguousarray(np.asarray(inputs["L0_attn_x"], np.float32))
    if MODE == "fused":
        stages = [["l0attn", "l0ffn", "l1attn", "l1ffn"]]
    else:
        stages = [["l0attn"], ["l0ffn"], ["l1attn"], ["l1ffn"]]
    cur = x
    for secs in stages:
        nc, names = build_program(secs)
        in_maps = []
        for b in range(8):
            m = {k: w[k] for k in names if k in w}
            m["resid_in"] = np.ascontiguousarray(cur[b])
            in_maps.append(m)
        res = run_bass_kernel_spmd(nc, in_maps, core_ids=list(range(8)))
        cur = np.stack([np.asarray(res.results[b]["out"]) for b in range(8)], 0)
    return cur


# version 05
# speedup vs baseline: 1.1908x; 1.1908x over previous
import numpy as np
import ml_dtypes
import concourse.bass as bass
import concourse.mybir as mybir
from concourse.bass_types import AP
from concourse.bass_utils import run_bass_kernel_spmd

F32 = mybir.dt.float32
BF16 = mybir.dt.bfloat16
AF = mybir.ActivationFunctionType
ALU = mybir.AluOpType
AX = mybir.AxisListType

T, D = 2048, 1024
NT, ND = T // 128, D // 128
NEG = -1.0e30

MODE = "fused"
DBG_EXP = False
DBG_IDENT = False
DBG_ROUTE = False

ENGINES = ("tensor", "vector", "scalar", "sync", "gpsimd")


class Buf:
    __slots__ = ("name", "writers", "readers")

    def __init__(self, name):
        self.name = name
        self.writers = {}
        self.readers = {}


class Prog:

    def __init__(self, nc):
        self.nc = nc
        self.keys = [e + ".C" for e in ENGINES] + [e + ".D" for e in ENGINES]
        self.sem = {k: nc.alloc_semaphore("sem_" + k.replace(".", "_"))
                    for k in self.keys}
        self.cnt = {k: 0 for k in self.keys}
        self.hwm = {e: {} for e in ENGINES}
        self.ops = {e: [] for e in ENGINES}
        self.bufsem = {}

    def _buf_key(self, buf):
        k = self.bufsem.get(id(buf))
        if k is None:
            k = "B%d.%s" % (len(self.bufsem), buf.name)
            self.bufsem[id(buf)] = k
            self.sem[k] = self.nc.alloc_semaphore("semb_%d" % len(self.bufsem))
            self.cnt[k] = 0
        return k

    def tok(self, e):
        return (e + ".C", self.cnt[e + ".C"])

    def tokd(self, e):
        return (e + ".D", self.cnt[e + ".D"])

    def _plan(self, e, key, emit_wrap, reads, writes, deps, inc):
        want = list(deps)
        for b in reads:
            want.extend(b.writers.items())
        for b in writes:
            want.extend(b.writers.items())
            want.extend(b.readers.items())
        waits = []
        for (k2, c) in want:
            if k2 == e + ".C" and e == "tensor":
                continue
            if c > self.hwm[e].get(k2, 0):
                self.hwm[e][k2] = c
                waits.append((self.sem[k2], c))
        self.ops[e].append((waits, emit_wrap))
        self.cnt[key] += inc
        t = (key, self.cnt[key])
        for b in reads:
            b.readers[key] = self.cnt[key]
        for b in writes:
            b.writers[key] = self.cnt[key]
            b.readers = {}
        return t

    def step(self, e, emit, reads=(), writes=(), deps=(), inc=1):
        key = e + ".C"
        sem = self.sem[key]

        def wrap(eng, emit=emit, inc=inc, sem=sem):
            last = emit(eng)
            if inc and last is not None:
                last.then_inc(sem, inc)
        return self._plan(e, key, wrap, reads, writes, deps, inc)

    def dma(self, e, movs, reads=(), writes=(), deps=()):
        if writes:
            key = self._buf_key(writes[0])
        else:
            key = e + ".D"
        sem = self.sem[key]

        def wrap(eng, movs=movs, sem=sem):
            for m in movs:
                kw = m[2] if len(m) > 2 else {}
                eng.dma_start(out=m[0], in_=m[1], **kw).then_inc(sem, 16)
        return self._plan(e, key, wrap, reads, writes, deps, 16 * len(movs))

    def fence(self, deps):
        for e in ENGINES:
            self.step(e, lambda eng: eng.nop(), deps=deps, inc=1)

    def barrier_deps(self):
        return ([(e + ".C", self.cnt[e + ".C"]) for e in ENGINES if self.cnt[e + ".C"] > 0]
                + [(e + ".D", self.cnt[e + ".D"]) for e in ENGINES if self.cnt[e + ".D"] > 0])

    def emit_all(self):
        with self.nc.Block() as block:
            for e in ENGINES:
                def runner(eng, steps=self.ops[e]):
                    for waits, wrap in steps:
                        for (s, c) in waits:
                            eng.wait_ge(s, c)
                        wrap(eng)
                getattr(block, e)(runner)


def build_program(sections, dump=()):
    nc = bass.Bass("TRN2", target_bir_lowering=False,
                   detect_race_conditions=False)
    P = Prog(nc)
    dt_in = {}

    def din(name, shape, dtype=F32):
        dt_in[name] = nc.dram_tensor(name, shape, dtype, kind="ExternalInput").ap()
        return dt_in[name]

    resid_in = din("resid_in", [T, D])
    out = nc.dram_tensor("out", [T, D], F32, kind="ExternalOutput").ap()

    din("ident_bf", [128, 128], BF16)
    din("ident_f32", [128, 128])
    din("triu_incl", [128, 128])
    din("slo_mask", [128, 128])
    din("gamma", [1, 1])
    din("eps_bc", [128, 2])

    need = set(sections)
    if "l0attn" in need:
        din("l0_wqkv", [D, 3 * D])
        din("l0_bqkv_pk", [128, 24])
        din("l0_vbias_bc", [128, D])
        din("l0_wproj", [D, D])
        din("l0_bproj_bc", [128, D])
    for lx in ("l0", "l1"):
        if lx + "ffn" in need:
            din(lx + "_gatewT", [D, 32])
            din(lx + "_gbias_bc", [128, 32])
            din(lx + "_eg", [32, D, 512])
            din(lx + "_eu", [32, D, 512])
            din(lx + "_ed", [32, 512, D])
            din(lx + "_sg", [D, D])
            din(lx + "_su", [D, D])
            din(lx + "_sd", [D, D])
    if "l1attn" in need:
        din("l1_lnw_bc", [128, D])
        din("l1_lnb_bc", [128, D])
        din("l1_wq", [D, D])
        din("l1_wk", [D, D])
        din("l1_wv", [D, 2 * D])
        din("l1_wb", [D, 4])
        din("l1_onw_bc", [128, 512])
        din("l1_wo", [2 * D, D])

    scr_prev = nc.dram_tensor("scr_prev", [1, 255], F32, kind="Internal").ap()
    scr_wT = nc.dram_tensor("scr_wT", [32, T], F32, kind="Internal").ap()
    scr_nT = nc.dram_tensor("scr_nT", [D, T], BF16, kind="Internal").ap()
    b_prevd = Buf("prevd")
    b_wTd = Buf("wTd")
    b_nTd = Buf("nTd")

    RES = nc.alloc_sbuf_tensor("RES", [128, NT * D], F32)
    SM = {
        "ident_bf": nc.alloc_sbuf_tensor("sm_identbf", [128, 128], BF16),
        "ident_f32": nc.alloc_sbuf_tensor("sm_identf32", [128, 128], F32),
        "triu": nc.alloc_sbuf_tensor("sm_triu", [128, 128], F32),
        "slo": nc.alloc_sbuf_tensor("sm_slo", [128, 128], F32),
        "D": nc.alloc_sbuf_tensor("sm_D", [128, 128], F32),
        "gp": nc.alloc_sbuf_tensor("sm_gp", [1, 512], F32),
        "stats": nc.alloc_sbuf_tensor("sm_stats", [128, 8], F32),
        "scr1k": nc.alloc_sbuf_tensor("sm_scr1k", [128, 1024], F32),
        "rt": nc.alloc_sbuf_tensor("sm_rt", [128, 1024], F32),
        "eps": nc.alloc_sbuf_tensor("sm_eps", [128, 2], F32),
    }
    b_res = [Buf(f"res{k}") for k in range(NT)]
    b_sm = Buf("sm")
    b_rt = Buf("rt")
    b_stats = Buf("stats")
    b_scr = Buf("scr")

    PS_A = nc.alloc_psum_tensor("PS_A", [128, 2048], F32)
    PS_B = nc.alloc_psum_tensor("PS_B", [128, 1024], F32)
    PS_C = nc.alloc_psum_tensor("PS_C", [128, 1024], F32)
    bA = [Buf(f"psA{i}") for i in range(4)]
    bB0, bB1 = Buf("psB0"), Buf("psB1")
    bC0, bC1 = Buf("psC0"), Buf("psC1")

    def psA(i):
        return PS_A[:, i * 512:(i + 1) * 512]

    P_reg = {}
    PS_B_bf = PS_B.ap().bitcast(BF16)
    PS_C_bf = PS_C.ap().bitcast(BF16)
    PS_A_bf = PS_A.ap().bitcast(BF16)

    st_ln = SM["stats"][:, 4:8]

    def ln_tile(out_ap, b_out, w_ap=None, b_ap=None, b_w=None):
        rt = SM["rt"][:]
        s1, mu, ve, rs = (st_ln[:, i:i + 1] for i in range(4))
        P.step("vector", lambda v: v.reduce_sum(s1, rt, axis=AX.X),
               reads=[b_rt], writes=[b_stats])
        P.step("scalar", lambda s: s.mul(mu, s1, 1.0 / 1024.0),
               reads=[b_stats], writes=[b_stats])
        P.step("vector", lambda v: v.tensor_scalar(rt, rt, mu, None, ALU.subtract),
               reads=[b_rt, b_stats], writes=[b_rt])
        P.step("vector", lambda v: v.tensor_mul(SM["scr1k"][:], rt, rt),
               reads=[b_rt], writes=[b_scr])
        P.step("vector", lambda v: v.reduce_sum(ve, SM["scr1k"][:], axis=AX.X),
               reads=[b_scr], writes=[b_stats])
        P.step("scalar", lambda s: s.activation(rs, ve, AF.Ln, scale=1.0 / 1024.0,
                                                bias=SM["eps"][:, 0:1]),
               reads=[b_stats], writes=[b_stats])
        P.step("scalar", lambda s: s.activation(rs, rs, AF.Exp, scale=-0.5),
               reads=[b_stats], writes=[b_stats])
        if w_ap is None:
            return P.step("vector", lambda v: v.tensor_scalar(
                out_ap, rt, rs, None, ALU.mult),
                reads=[b_rt, b_stats], writes=[b_out])
        P.step("vector", lambda v: v.scalar_tensor_tensor(
            out_ap, rt, rs, w_ap, ALU.mult, ALU.mult),
            reads=[b_rt, b_stats, b_w], writes=[b_out])
        return P.step("vector", lambda v: v.tensor_tensor(out_ap, out_ap, b_ap, ALU.add),
                      reads=[b_out, b_w], writes=[b_out])

    def transpose128(src_ap, dst_ap, ps_ap, ps_buf, src_buf, dst_buf,
                     ident=None, evac="scalar"):
        idn = SM["ident_bf"][:] if ident is None else ident
        P.step("tensor", lambda t: t.transpose(ps_ap, src_ap, idn),
               reads=[src_buf, b_sm], writes=[ps_buf])
        if evac == "scalar":
            return P.step("scalar", lambda s: s.copy(dst_ap, ps_ap),
                          reads=[ps_buf], writes=[dst_buf])
        return P.step("vector", lambda v: v.tensor_copy(dst_ap, ps_ap),
                      reads=[ps_buf], writes=[dst_buf])

    P.dma("sync", [(RES[:].rearrange("p (k d) -> p k d", d=D),
                    resid_in.rearrange("(k p) d -> p k d", p=128))],
          writes=b_res)
    P.dma("sync", [(SM["ident_bf"][:], dt_in["ident_bf"]),
                   (SM["ident_f32"][:], dt_in["ident_f32"]),
                   (SM["triu"][:], dt_in["triu_incl"]),
                   (SM["slo"][:], dt_in["slo_mask"]),
                   (SM["gp"][0:1, 0:1], dt_in["gamma"]),
                   (SM["eps"][:], dt_in["eps_bc"])], writes=[b_sm])

    gp = SM["gp"]
    Q0 = 64

    P.step("vector", lambda v: v.memset(gp[0:1, Q0:Q0 + 1], 1.0),
           reads=[b_sm], writes=[b_sm])
    n = 1
    while n <= 64:
        P.step("vector", lambda v, n=n: v.tensor_tensor(
            gp[0:1, 32:33], gp[0:1, Q0 + n - 1:Q0 + n], gp[0:1, 0:1], ALU.mult),
            reads=[b_sm], writes=[b_sm])
        P.step("vector", lambda v, n=n: v.tensor_scalar(
            gp[0:1, Q0 + n:Q0 + 2 * n], gp[0:1, Q0:Q0 + n], gp[0:1, 32:33],
            None, ALU.mult), reads=[b_sm], writes=[b_sm])
        n *= 2
    P.step("vector", lambda v: v.memset(gp[0:1, 384:512], 0.0),
           reads=[b_sm], writes=[b_sm])
    P.step("vector", lambda v: v.tensor_copy(gp[0:1, 256:384],
                                             gp[0:1, Q0 + 127:Q0 - 1:-1]),
           reads=[b_sm], writes=[b_sm])
    P.dma("sync", [(scr_prev, gp[0:1, 256:511])], reads=[b_sm], writes=[b_prevd])
    P.dma("gpsimd",
          [(SM["D"][i:i + 1, :], scr_prev[0:1, 127 - i:255 - i]) for i in range(128)],
          reads=[b_prevd], writes=[b_sm])

    sec_dep = [P.barrier_deps()]

    def build_yt(YT, b_yt, xbf, b_xbf):
        for k in range(NT):
            par = k % 2
            xb = xbf[:, par * 1024:(par + 1) * 1024]
            P.step("scalar", lambda s, k=k, xb=xb: s.copy(xb, RES[:, k * D:(k + 1) * D]),
                   reads=[b_res[k]], writes=[b_xbf[par]])
            for j in range(ND):
                sl = j % 4
                pa = PS_A_bf[:, sl * 1024:sl * 1024 + 128]
                transpose128(xb[:, j * 128:(j + 1) * 128],
                             YT[:, j * T + k * 128: j * T + (k + 1) * 128],
                             pa, bA[sl], b_xbf[par], b_yt,
                             evac="vector" if (j % 2) else "scalar")

    def sec_l0attn():
        ar = nc.auto_arena()
        YT = ar.sbuf_tensor("a0_YT", [128, ND * T], BF16)
        Kf = ar.sbuf_tensor("a0_K", [128, ND * T], BF16)
        Vt = ar.sbuf_tensor("a0_V", [128, NT * 1024], BF16)
        wcol = ar.sbuf_tensor("a0_wcol", [128, 2 * 8 * 128], BF16)
        bpb = ar.sbuf_tensor("a0_bp", [128, 1024], F32)
        bqk = ar.sbuf_tensor("a0_bqk", [128, 24], F32)
        ar.allocate()
        ar_y = nc.auto_arena()
        Wv = ar_y.sbuf_tensor("a0_Wv", [128, 8 * 1024], BF16)
        xbf = ar_y.sbuf_tensor("a0_xbf", [128, 2048], BF16)
        vb = ar_y.sbuf_tensor("a0_vb", [128, 1024], F32)
        ar_y.allocate()

        b_yt = Buf("yt"); b_q = Buf("q"); b_k = Buf("k"); b_v = Buf("v")
        b_wv = Buf("wv"); b_wcol = [Buf("wc0"), Buf("wc1")]
        b_xbf = [Buf("xb0"), Buf("xb1")]
        b_s = Buf("s"); b_p = Buf("p"); b_pt = Buf("pt"); b_o = Buf("o")
        b_ot = Buf("ot"); b_wp = [Buf("wp0"), Buf("wp1")]
        b_cst = Buf("a0cst")

        P.fence(sec_dep[0])
        P.dma("sync", [(vb[:], dt_in["l0_vbias_bc"]), (bpb[:], dt_in["l0_bproj_bc"]),
                       (bqk[:], dt_in["l0_bqkv_pk"])], writes=[b_cst])
        build_yt(YT, b_yt, xbf, b_xbf)

        wqkv = dt_in["l0_wqkv"]
        P.dma("gpsimd", [(Wv[:].rearrange("p (k n) -> p k n", n=1024),
                          wqkv[:, 2 * D:3 * D].rearrange("(k p) n -> p k n", p=128))],
              writes=[b_wv])
        for tt in range(NT):
            ps = PS_B if tt % 2 == 0 else PS_C
            bp = [bB0, bB1] if tt % 2 == 0 else [bC0, bC1]

            def em(t, tt=tt, ps=ps):
                ins = None
                for d in range(ND):
                    lhsT = YT[:, d * T + tt * 128: d * T + (tt + 1) * 128]
                    for h in range(2):
                        ins = t.matmul(ps[:, h * 512:(h + 1) * 512], lhsT,
                                       Wv[:, d * 1024 + h * 512: d * 1024 + (h + 1) * 512],
                                       start=(d == 0), stop=(d == ND - 1))
                return ins
            P.step("tensor", em, reads=[b_yt, b_wv], writes=bp)
            P.step("vector", lambda v, tt=tt, ps=ps: v.scalar_tensor_tensor(
                Vt[:, tt * 1024:(tt + 1) * 1024], ps[:], 0.0, vb[:], ALU.add, ALU.add),
                reads=bp + [b_cst], writes=[b_v])
        for j in range(ND):
            par = j % 2
            wc = wcol[:, par * 1024:(par + 1) * 1024]
            col = D + j * 128
            P.dma("gpsimd", [(wc.rearrange("p (k c) -> p k c", c=128),
                              wqkv[:, col:col + 128].rearrange("(k p) c -> p k c", p=128))],
                  writes=[b_wcol[par]])

            def em(t, wc=wc):
                ins = None
                for d in range(ND):
                    lhsT = wc[:, d * 128:(d + 1) * 128]
                    for tc in range(4):
                        ins = t.matmul(psA(tc), lhsT,
                                       YT[:, d * T + tc * 512: d * T + (tc + 1) * 512],
                                       start=(d == 0), stop=(d == ND - 1))
                return ins
            P.step("tensor", em, reads=[b_wcol[par], b_yt], writes=bA)
            for tc in range(4):
                P.step("scalar", lambda s, j=j, tc=tc: s.activation(
                    Kf[:, j * T + tc * 512: j * T + (tc + 1) * 512], psA(tc),
                    AF.Identity, bias=bqk[:, 8 + j: 9 + j]),
                    reads=[bA[tc], b_cst], writes=[b_k])
        P.fence(P.barrier_deps())
        ar_y.free()
        ar_d = nc.auto_arena()
        Qf = ar_d.sbuf_tensor("a0_Q", [128, 8 * 512], BF16)
        pbf = ar_d.sbuf_tensor("a0_p", [128, 2048], BF16)
        ptb = ar_d.sbuf_tensor("a0_pt", [128, 16 * 128], BF16)
        obf = ar_d.sbuf_tensor("a0_o", [128, 1024], BF16)
        otb = ar_d.sbuf_tensor("a0_ot", [128, 8 * 128], BF16)
        wpb = ar_d.sbuf_tensor("a0_wp", [128, 2 * 1024], BF16)
        ar_d.allocate()
        sc = 1.0 / float(np.sqrt(D))
        st = SM["stats"]
        b_p2_box = [Buf("p2")]
        for g in range(4):
            for j in range(ND):
                par = j % 2
                wc = wcol[:, par * 1024:(par + 1) * 1024]
                P.dma("gpsimd", [(wc.rearrange("p (k c) -> p k c", c=128),
                                  wqkv[:, j * 128:(j + 1) * 128].rearrange("(k p) c -> p k c", p=128))],
                      writes=[b_wcol[par]])
                sl = j % 4

                def em(t, wc=wc, g=g, sl=sl):
                    ins = None
                    for d in range(ND):
                        ins = t.matmul(psA(sl), wc[:, d * 128:(d + 1) * 128],
                                       YT[:, d * T + g * 512: d * T + (g + 1) * 512],
                                       start=(d == 0), stop=(d == ND - 1))
                    return ins
                P.step("tensor", em, reads=[b_wcol[par], b_yt], writes=[bA[sl]])
                P.step("scalar", lambda s, j=j, sl=sl: s.activation(
                    Qf[:, j * 512:(j + 1) * 512], psA(sl), AF.Identity,
                    bias=bqk[:, j:j + 1]), reads=[bA[sl], b_cst], writes=[b_q])
            for qt in range(g * 4, g * 4 + 4):
                def em(t, qt=qt):
                    ins = None
                    for j in range(ND):
                        lhsT = Qf[:, j * 512 + (qt % 4) * 128: j * 512 + (qt % 4 + 1) * 128]
                        for kc in range(4):
                            ins = t.matmul(psA(kc), lhsT,
                                           Kf[:, j * T + kc * 512: j * T + (kc + 1) * 512],
                                           start=(j == 0), stop=(j == ND - 1))
                    return ins
                P.step("tensor", em, reads=[b_q, b_k], writes=bA)
                P.step("vector", lambda v, qt=qt: v.tensor_tensor(
                    PS_A[:, qt * 128:(qt + 1) * 128], PS_A[:, qt * 128:(qt + 1) * 128],
                    SM["D"][:], ALU.mult), reads=bA + [b_sm], writes=bA)
                P.step("vector", lambda v: v.reduce_max(st[:, 0:1], PS_A[:], axis=AX.X,
                                                        negate=True),
                       reads=bA, writes=[b_stats])
                P.step("scalar", lambda s: s.mul(st[:, 1:2], st[:, 0:1], sc),
                       reads=[b_stats], writes=[b_stats])
                P.step("scalar", lambda s: s.activation(pbf[:], PS_A[:],
                                                        AF.Identity if DBG_IDENT else AF.Exp,
                                                        bias=st[:, 1:2], scale=sc),
                       reads=bA + [b_stats], writes=[b_p])
                if DBG_EXP and qt == 15:
                    dap = nc.dram_tensor("dump_exp", [128, 2048], BF16,
                                         kind="ExternalOutput").ap()
                    sap = nc.dram_tensor("dump_st", [128, 8], F32,
                                         kind="ExternalOutput").ap()
                    P.dma("sync", [(dap, pbf[:])], reads=[b_p], writes=[b_p2_box[0]])
                    P.dma("sync", [(sap, st[:])], reads=[b_stats], writes=[b_p2_box[0]])
                P.step("vector", lambda v: v.reduce_sum(st[:, 2:3], pbf[:], axis=AX.X),
                       reads=[b_p], writes=[b_stats])
                P.step("vector", lambda v: v.reciprocal(st[:, 3:4], st[:, 2:3]),
                       reads=[b_stats], writes=[b_stats])
                P.step("vector", lambda v: v.tensor_scalar(pbf[:], pbf[:], st[:, 3:4],
                                                           None, ALU.mult),
                       reads=[b_p, b_stats], writes=[b_p])
                for kt in range(NT):
                    sl2 = kt % 2
                    transpose128(pbf[:, kt * 128:(kt + 1) * 128],
                                 ptb[:, kt * 128:(kt + 1) * 128],
                                 PS_B_bf[:, sl2 * 1024: sl2 * 1024 + 128],
                                 bB0 if sl2 == 0 else bB1, b_p, b_pt,
                                 evac="vector" if kt % 2 else "scalar")

                def em(t):
                    ins = None
                    for kt in range(NT):
                        lhsT = ptb[:, kt * 128:(kt + 1) * 128]
                        for h in range(2):
                            ins = t.matmul(PS_C[:, h * 512:(h + 1) * 512], lhsT,
                                           Vt[:, kt * 1024 + h * 512: kt * 1024 + (h + 1) * 512],
                                           start=(kt == 0), stop=(kt == NT - 1))
                    return ins
                P.step("tensor", em, reads=[b_pt, b_v], writes=[bC0, bC1])
                P.step("scalar", lambda s: s.copy(obf[:], PS_C[:]),
                       reads=[bC0, bC1], writes=[b_o])
                for j in range(ND):
                    sl2 = j % 2
                    transpose128(obf[:, j * 128:(j + 1) * 128],
                                 otb[:, j * 128:(j + 1) * 128],
                                 PS_C_bf[:, sl2 * 1024: sl2 * 1024 + 128],
                                 bC0 if sl2 == 0 else bC1, b_o, b_ot,
                                 evac="vector" if j % 2 else "scalar")
                for j in range(ND):
                    par = j % 2
                    wp = wpb[:, par * 1024:(par + 1) * 1024]
                    P.dma("gpsimd", [(wp, dt_in["l0_wproj"][j * 128:(j + 1) * 128, :])],
                          writes=[b_wp[par]])

                    def em(t, j=j, wp=wp):
                        ins = None
                        for h in range(2):
                            ins = t.matmul(PS_B[:, h * 512:(h + 1) * 512],
                                           otb[:, j * 128:(j + 1) * 128],
                                           wp[:, h * 512:(h + 1) * 512],
                                           start=(j == 0), stop=(j == ND - 1))
                        return ins
                    P.step("tensor", em, reads=[b_ot, b_wp[par]], writes=[bB0, bB1])
                rt = SM["rt"]
                P.step("vector", lambda v, qt=qt: v.tensor_tensor(
                    rt[:], PS_B[:], RES[:, qt * D:(qt + 1) * D], ALU.add),
                    reads=[bB0, bB1, b_res[qt]], writes=[b_rt])
                P.step("vector", lambda v: v.tensor_tensor(rt[:], rt[:], bpb[:], ALU.add),
                       reads=[b_rt, b_cst], writes=[b_rt])
                ln_tile(RES[:, qt * D:(qt + 1) * D], b_res[qt])
        P_reg.update(Dm=(SM["D"], b_sm),
                     YT=(YT, b_yt), Kf=(Kf, b_k), Vt=(Vt, b_v), Qf=(Qf, b_q),
                     pbf=(pbf, b_p), ptb=(ptb, b_pt), obf=(obf, b_o),
                     otb=(otb, b_ot))
        tok = P.barrier_deps()
        ar_d.free()
        ar.free()
        return tok

    def sec_ffn(lx, topk):
        ar = nc.auto_arena()
        YT = ar.sbuf_tensor(lx + "YT", [128, ND * T], BF16)
        ar.allocate()
        ar_r = nc.auto_arena()
        gwT = ar_r.sbuf_tensor(lx + "gwT", [128, 8 * 32], BF16)
        gbb = ar_r.sbuf_tensor(lx + "gbb", [128, 32], F32)
        scs = ar_r.sbuf_tensor(lx + "scs", [128, NT * 32], F32)
        w32 = ar_r.sbuf_tensor(lx + "w32", [128, NT * 32], F32)
        rsc = ar_r.sbuf_tensor(lx + "rsc", [128, 128], F32)
        wTs = ar_r.sbuf_tensor(lx + "wTs", [32, NT * 128], F32)
        xbf = ar_r.sbuf_tensor(lx + "xbf", [128, 2048], BF16)
        ar_r.allocate()

        b_yt = Buf("yt"); b_xbf = [Buf("xb0"), Buf("xb1")]
        b_wg = [Buf("wg0"), Buf("wg1")]; b_wu = [Buf("wu0"), Buf("wu1")]
        b_wd = [Buf("wd0"), Buf("wd1")]; b_wb = [Buf("wb0"), Buf("wb1")]
        b_hid = [[Buf(f"h{p}{i}") for i in range(4)] for p in range(2)]
        b_ts = [Buf("ts0"), Buf("ts1")]; b_ub = [Buf("ub0"), Buf("ub1")]
        b_h1 = Buf("h1")
        b_cst = Buf("cst"); b_scs = Buf("scs"); b_w32 = Buf("w32")
        b_rsc = Buf("rsc"); b_wTs = Buf("wTs")

        P.fence(sec_dep[0])
        P.dma("sync", [(gbb[:], dt_in[lx + "_gbias_bc"])], writes=[b_cst])
        P.dma("gpsimd", [(gwT[:].rearrange("p (k c) -> p k c", c=32),
                          dt_in[lx + "_gatewT"].rearrange("(k p) c -> p k c", p=128))],
              writes=[b_cst])
        build_yt(YT, b_yt, xbf, b_xbf)

        st = SM["stats"]
        for tt in range(NT):
            sl = tt % 4

            def em(t, tt=tt, sl=sl):
                ins = None
                for d in range(ND):
                    ins = t.matmul(psA(sl)[:, 0:32],
                                   YT[:, d * T + tt * 128: d * T + (tt + 1) * 128],
                                   gwT[:, d * 32:(d + 1) * 32],
                                   start=(d == 0), stop=(d == ND - 1))
                return ins
            P.step("tensor", em, reads=[b_yt, b_cst], writes=[bA[sl]])
            P.step("scalar", lambda s, tt=tt, sl=sl: s.activation(
                scs[:, tt * 32:(tt + 1) * 32], psA(sl)[:, 0:32], AF.Sigmoid),
                reads=[bA[sl]], writes=[b_scs])
            sfc = rsc[:, 0:32]; gs = rsc[:, 32:40]; m8 = rsc[:, 40:48]
            gm = rsc[:, 48:52]; msk = rsc[:, 64:96]; zap = rsc[:, 96:128]
            sct = scs[:, tt * 32:(tt + 1) * 32]

            mm8 = rsc[:, 96:128]

            def em_r1(v, sfc=sfc, mm8=mm8, sct=sct):
                v.tensor_tensor(sfc, sct, gbb[:], ALU.add)
                ins = None
                for g in range(4):
                    ins = v.max(out=mm8[:, g * 8:(g + 1) * 8],
                                in_=sfc[:, g * 8:(g + 1) * 8])
                return ins
            P.step("vector", em_r1, reads=[b_scs, b_cst], writes=[b_rsc])

            def em_r1b(v, gs=gs, mm8=mm8):
                v.memset(gs[:, 4:8], NEG)
                ins = None
                for g in range(4):
                    ins = v.tensor_tensor(gs[:, g:g + 1], mm8[:, g * 8:g * 8 + 1],
                                          mm8[:, g * 8 + 1:g * 8 + 2], ALU.add)
                return ins
            P.step("vector", em_r1b, reads=[b_rsc], writes=[b_rsc])
            P.step("vector", lambda v, gs=gs, m8=m8: v.max(out=m8, in_=gs),
                   reads=[b_rsc], writes=[b_rsc])
            P.step("vector", lambda v, gs=gs, gm=gm, m8=m8: v.tensor_scalar(
                gm, gs[:, 0:4], m8[:, 1:2], None, ALU.is_ge),
                reads=[b_rsc], writes=[b_rsc])

            def em_r2(v, sfc=sfc, gm=gm, msk=msk):
                ins = None
                for g in range(4):
                    ins = v.tensor_scalar(msk[:, g * 8:(g + 1) * 8],
                                          sfc[:, g * 8:(g + 1) * 8],
                                          gm[:, g:g + 1], None, ALU.mult)
                return ins
            P.step("vector", em_r2, reads=[b_rsc], writes=[b_rsc])

            P.step("vector", lambda v, msk=msk, m8=m8: v.max(out=m8, in_=msk),
                   reads=[b_rsc], writes=[b_rsc])
            if topk < 8:
                P.step("vector", lambda v, m8=m8: v.memset(m8[:, topk:8], NEG),
                       reads=[b_rsc], writes=[b_rsc])
            zap = rsc[:, 96:128]
            P.step("vector", lambda v, msk=msk, zap=zap, m8=m8: v.match_replace(
                out=zap, in_to_replace=m8, in_values=msk, imm_value=NEG),
                reads=[b_rsc], writes=[b_rsc])

            def em_r3(v, msk=msk, zap=zap, sct=sct):
                v.tensor_tensor(zap, msk, zap, ALU.not_equal)
                v.tensor_tensor(zap, zap, sct, ALU.mult)
                v.reduce_sum(st[:, 0:1], zap, axis=AX.X)
                v.tensor_scalar(st[:, 0:1], st[:, 0:1], 1e-20, None, ALU.add)
                return v.reciprocal(st[:, 1:2], st[:, 0:1])
            P.step("vector", em_r3, reads=[b_rsc, b_scs], writes=[b_rsc, b_stats])
            P.step("vector", lambda v, tt=tt, zap=zap: v.tensor_scalar(
                w32[:, tt * 32:(tt + 1) * 32], zap, st[:, 1:2], 2.5,
                ALU.mult, ALU.mult), reads=[b_rsc, b_stats], writes=[b_w32])
            sl2 = tt % 2
            bps = bB0 if sl2 == 0 else bB1
            P.step("tensor", lambda t, tt=tt, sl2=sl2: t.transpose(
                PS_B[0:32, sl2 * 512: sl2 * 512 + 128],
                w32[:, tt * 32:(tt + 1) * 32], SM["ident_f32"][:]),
                reads=[b_w32, b_sm], writes=[bps])
            P.step("vector", lambda v, tt=tt, sl2=sl2: v.tensor_copy(
                wTs[0:32, tt * 128:(tt + 1) * 128],
                PS_B[0:32, sl2 * 512: sl2 * 512 + 128]),
                reads=[bps], writes=[b_wTs])
        P.dma("sync", [(scr_wT, wTs[0:32, :])], reads=[b_wTs], writes=[b_wTd])
        if DBG_ROUTE:
            d1 = nc.dram_tensor(lx + "dump_w32", [128, NT * 32], F32,
                                kind="ExternalOutput").ap()
            d2 = nc.dram_tensor(lx + "dump_scs", [128, NT * 32], F32,
                                kind="ExternalOutput").ap()
            d3 = nc.dram_tensor(lx + "dump_rsc", [128, 128], F32,
                                kind="ExternalOutput").ap()
            P.dma("sync", [(d1, w32[:]), (d2, scs[:]), (d3, rsc[:])],
                  reads=[b_w32, b_scs, b_rsc], writes=[Buf("dbgrt")])

        P.fence(P.barrier_deps())
        ar_r.free()
        ar_e = nc.auto_arena()
        WG = ar_e.sbuf_tensor(lx + "WG", [128, 2 * 8 * 512], BF16)
        WU = ar_e.sbuf_tensor(lx + "WU", [128, 2 * 8 * 512], BF16)
        WD = ar_e.sbuf_tensor(lx + "WD", [128, 2 * 4 * 1024], BF16)
        HID = ar_e.sbuf_tensor(lx + "HID", [128, 2 * 4 * 2048], BF16)
        WB = ar_e.sbuf_tensor(lx + "WB", [128, 2 * 2048], BF16)
        tsil = ar_e.sbuf_tensor(lx + "tsil", [128, 2 * 1024], BF16)
        ubf = ar_e.sbuf_tensor(lx + "ubf", [128, 2 * 1024], BF16)
        h1b = ar_e.sbuf_tensor(lx + "h1b", [128, 1024], BF16)
        ar_e.allocate()

        eg, eu, ed = dt_in[lx + "_eg"], dt_in[lx + "_eu"], dt_in[lx + "_ed"]
        sg, su, sd = dt_in[lx + "_sg"], dt_in[lx + "_su"], dt_in[lx + "_sd"]

        def wsrc(e):
            if e < 32:
                return (eg[e], eu[e], ed[e], e)
            half = e - 32
            sl = slice(half * 512, (half + 1) * 512)
            return (sg[:, sl], su[:, sl], sd[sl, :], None)

        for e in range(34):
            par = e % 2
            g_src, u_src, d_src, widx = wsrc(e)
            wg = WG[:, par * 4096:(par + 1) * 4096]
            wu = WU[:, par * 4096:(par + 1) * 4096]
            wd = WD[:, par * 4096:(par + 1) * 4096]
            wb = WB[:, par * 2048:(par + 1) * 2048]
            P.dma("gpsimd", [(wg.rearrange("p (k n) -> p k n", n=512),
                              g_src.rearrange("(k p) n -> p k n", p=128)),
                             (wu.rearrange("p (k n) -> p k n", n=512),
                              u_src.rearrange("(k p) n -> p k n", p=128)),
                             (wd.rearrange("p (k n) -> p k n", n=1024),
                              d_src.rearrange("(k p) n -> p k n", p=128))],
                  writes=[b_wg[par], b_wu[par], b_wd[par]])
            if widx is not None:
                P.dma("gpsimd", [(wb, scr_wT[widx:widx + 1, :].to_broadcast([128, 2048]))],
                      reads=[b_wTd], writes=[b_wb[par]])
            else:
                P.step("vector", lambda v, wb=wb: v.memset(wb, 1.0),
                       writes=[b_wb[par]])
            for i in range(4):
                for th in range(2):
                    pp = (2 * i + th) % 2
                    gsl = [0, 1] if pp == 0 else [2, 3]

                    def em_g(t, i=i, th=th, wg=wg, gsl=gsl):
                        ins = None
                        for q in range(2):
                            for d in range(ND):
                                lhsT = wg[:, d * 512 + i * 128: d * 512 + (i + 1) * 128]
                                ins = t.matmul(psA(gsl[q]), lhsT,
                                               YT[:, d * T + th * 1024 + q * 512:
                                                  d * T + th * 1024 + (q + 1) * 512],
                                               start=(d == 0), stop=(d == ND - 1))
                        return ins
                    P.step("tensor", em_g, reads=[b_wg[par], b_yt],
                           writes=[bA[gsl[0]], bA[gsl[1]]])
                    tsl = tsil[:, par * 1024:(par + 1) * 1024]
                    P.step("scalar", lambda s, gsl=gsl, tsl=tsl: s.activation(
                        tsl, PS_A[:, gsl[0] * 512:(gsl[1] + 1) * 512], AF.Silu),
                        reads=[bA[gsl[0]], bA[gsl[1]]], writes=[b_ts[par]])

                    def em_u(t, i=i, th=th, wu=wu, gsl=gsl):
                        ins = None
                        for q in range(2):
                            for d in range(ND):
                                lhsT = wu[:, d * 512 + i * 128: d * 512 + (i + 1) * 128]
                                ins = t.matmul(psA(gsl[q]), lhsT,
                                               YT[:, d * T + th * 1024 + q * 512:
                                                  d * T + th * 1024 + (q + 1) * 512],
                                               start=(d == 0), stop=(d == ND - 1))
                        return ins
                    P.step("tensor", em_u, reads=[b_wu[par], b_yt, b_ts[par]],
                           writes=[bA[gsl[0]], bA[gsl[1]]])
                    ub = ubf[:, par * 1024:(par + 1) * 1024]
                    P.step("scalar", lambda s, gsl=gsl, ub=ub: s.copy(
                        ub, PS_A[:, gsl[0] * 512:(gsl[1] + 1) * 512]),
                        reads=[bA[gsl[0]], bA[gsl[1]]], writes=[b_ub[par]])
                    P.step("vector", lambda v, tsl=tsl, ub=ub: v.tensor_tensor(
                        h1b[:], tsl, ub, ALU.mult),
                        reads=[b_ts[par], b_ub[par]], writes=[b_h1])
                    hd = HID[:, par * 8192 + i * 2048 + th * 1024:
                             par * 8192 + i * 2048 + (th + 1) * 1024]
                    P.step("vector", lambda v, hd=hd, wb=wb, th=th: v.tensor_tensor(
                        hd, h1b[:], wb[:, th * 1024:(th + 1) * 1024], ALU.mult),
                        reads=[b_h1, b_wb[par]], writes=[b_hid[par][i]])
            if e % 2 == 1:
                for tt in range(NT):
                    ps = PS_B if tt % 2 == 0 else PS_C
                    bp = [bB0, bB1] if tt % 2 == 0 else [bC0, bC1]

                    def em_d(t, tt=tt, ps=ps):
                        ins = None
                        n = 0
                        for pp in range(2):
                            wdd = WD[:, pp * 4096:(pp + 1) * 4096]
                            for i in range(4):
                                lhsT = HID[:, pp * 8192 + i * 2048 + tt * 128:
                                           pp * 8192 + i * 2048 + (tt + 1) * 128]
                                for h in range(2):
                                    ins = t.matmul(ps[:, h * 512:(h + 1) * 512], lhsT,
                                                   wdd[:, i * 1024 + h * 512:
                                                       i * 1024 + (h + 1) * 512],
                                                   start=(n == 0), stop=(n == 7))
                                n += 1
                        return ins
                    rd = [b_wd[0], b_wd[1]] + b_hid[0] + b_hid[1]
                    P.step("tensor", em_d, reads=rd, writes=bp)
                    P.step("vector", lambda v, tt=tt, ps=ps: v.tensor_tensor(
                        RES[:, tt * D:(tt + 1) * D], RES[:, tt * D:(tt + 1) * D],
                        ps[:], ALU.add), reads=bp + [b_res[tt]], writes=[b_res[tt]])
        for tt in range(NT):
            P.step("vector", lambda v, tt=tt: v.tensor_copy(
                SM["rt"][:], RES[:, tt * D:(tt + 1) * D]),
                reads=[b_res[tt]], writes=[b_rt])
            ln_tile(RES[:, tt * D:(tt + 1) * D], b_res[tt])
        tok = P.barrier_deps()
        ar_e.free()
        ar.free()
        return tok

    def sec_l1attn():
        H, DK, DV = 4, 256, 512
        ar_pre = nc.auto_arena()
        lnw = ar_pre.sbuf_tensor("a1_lnw", [128, 1024], F32)
        lnb = ar_pre.sbuf_tensor("a1_lnb", [128, 1024], F32)
        nst = ar_pre.sbuf_tensor("a1_nst", [128, 1024], BF16)
        nst2 = ar_pre.sbuf_tensor("a1_nst2", [128, 1024], BF16)
        ar_pre.allocate()

        b_w = Buf("a1w")
        b_S = [Buf(f"S{h}") for h in range(H)]
        b_Sb = [Buf(f"Sb{h}") for h in range(H)]
        b_ncb = [Buf("nc0"), Buf("nc1")]
        b_qc = Buf("qc"); b_kc = Buf("kc"); b_vc = Buf("vc"); b_bay = Buf("bay"); b_QT = Buf("QT"); b_KT = Buf("KT")
        b_PWR = Buf("PWR"); b_Z = Buf("Z"); b_Mt = Buf("Mt"); b_Gt = Buf("Gt")
        b_B = Buf("B"); b_U = Buf("U"); b_of = Buf("of"); b_on = Buf("on")
        b_OT = Buf("OT"); b_cst = Buf("a1cst"); b_nst = Buf("nst")
        b_nst2 = Buf("nst2"); b_woq = [Buf(f"woq{i}") for i in range(2)]

        P.fence(sec_dep[0])
        P.dma("sync", [(lnw[:], dt_in["l1_lnw_bc"]), (lnb[:], dt_in["l1_lnb_bc"])],
              writes=[b_cst])

        nT_r = scr_nT.rearrange("(j p) t -> p j t", p=128)
        for tt in range(NT):
            P.step("vector", lambda v, tt=tt: v.tensor_copy(
                SM["rt"][:], RES[:, tt * D:(tt + 1) * D]),
                reads=[b_res[tt]], writes=[b_rt])
            ln_tile(nst[:], b_nst, w_ap=lnw[:], b_ap=lnb[:], b_w=b_cst)
            for j in range(ND):
                sl = j % 4
                transpose128(nst[:, j * 128:(j + 1) * 128],
                             nst2[:, j * 128:(j + 1) * 128],
                             PS_A_bf[:, sl * 1024: sl * 1024 + 128],
                             bA[sl], b_nst, b_nst2,
                             evac="vector" if j % 2 else "scalar")
            P.dma("sync", [(nT_r[:, :, tt * 128:(tt + 1) * 128],
                            nst2[:].rearrange("p (j c) -> p j c", c=128))],
                  reads=[b_nst2], writes=[b_nTd])

        P.fence(P.barrier_deps())
        ar_pre.free()
        ar = nc.auto_arena()
        Wq = ar.sbuf_tensor("a1_Wq", [128, 8 * 1024], BF16)
        Wk = ar.sbuf_tensor("a1_Wk", [128, 8 * 1024], BF16)
        Wv = ar.sbuf_tensor("a1_Wv", [128, 8 * 2048], BF16)
        Wb = ar.sbuf_tensor("a1_Wb", [128, 8 * 4], BF16)
        woq = ar.sbuf_tensor("a1_woq", [128, 2 * 1024], BF16)
        Sf = ar.sbuf_tensor("a1_S", [128, 4096], F32)
        Sb = ar.sbuf_tensor("a1_Sb", [128, 4096], BF16)
        ncb = ar.sbuf_tensor("a1_ncb", [128, 2 * 1024], BF16)
        qcb = ar.sbuf_tensor("a1_qcb", [128, 1024], BF16)
        kcb = ar.sbuf_tensor("a1_kcb", [128, 1024], BF16)
        vcb = ar.sbuf_tensor("a1_vcb", [128, 2048], BF16)
        bay = ar.sbuf_tensor("a1_bay", [128, 4], F32)
        QT = ar.sbuf_tensor("a1_QT", [128, 8 * 128], BF16)
        KT = ar.sbuf_tensor("a1_KT", [128, 8 * 128], BF16)
        PWR = ar.sbuf_tensor("a1_PWR", [128, 7 * 256], F32)
        Zb = ar.sbuf_tensor("a1_Z", [128, 2 * 128], F32)
        Mtb = ar.sbuf_tensor("a1_Mt", [128, 128], BF16)
        Gtb = ar.sbuf_tensor("a1_Gt", [128, 128], BF16)
        Bb = ar.sbuf_tensor("a1_B", [128, 512], BF16)
        Ub = ar.sbuf_tensor("a1_U", [128, 512], BF16)
        of = ar.sbuf_tensor("a1_of", [128, 512], F32)
        onb = ar.sbuf_tensor("a1_onb", [128, 512], BF16)
        OT = ar.sbuf_tensor("a1_OT", [128, 16 * 128], BF16)
        onw = ar.sbuf_tensor("a1_onw", [128, 512], F32)
        ar.allocate()
        P.dma("sync", [(onw[:], dt_in["l1_onw_bc"])], writes=[b_cst])
        P.dma("gpsimd", [
            (Wq[:].rearrange("p (k n) -> p k n", n=1024),
             dt_in["l1_wq"].rearrange("(k p) n -> p k n", p=128)),
            (Wk[:].rearrange("p (k n) -> p k n", n=1024),
             dt_in["l1_wk"].rearrange("(k p) n -> p k n", p=128)),
            (Wv[:].rearrange("p (k n) -> p k n", n=2048),
             dt_in["l1_wv"].rearrange("(k p) n -> p k n", p=128)),
            (Wb[:].rearrange("p (k n) -> p k n", n=4),
             dt_in["l1_wb"].rearrange("(k p) n -> p k n", p=128)),
        ], writes=[b_w])
        P.step("vector", lambda v: v.memset(Sf[:], 0.0), writes=b_S)
        P.step("vector", lambda v: v.memset(Sb[:], 0.0), writes=b_Sb)

        def PW(j, t):
            return PWR[:, j * 256 + t * 128: j * 256 + (t + 1) * 128]

        st = SM["stats"]
        for c in range(NT):
            par = c % 2
            nb = ncb[:, par * 1024:(par + 1) * 1024]
            P.dma("gpsimd", [(nb.rearrange("p (j x) -> p j x", x=128),
                              nT_r[:, :, c * 128:(c + 1) * 128])],
                  reads=[b_nTd], writes=[b_ncb[par]])

            def em_q(t, nb=nb):
                ins = None
                for d in range(ND):
                    lhsT = nb[:, d * 128:(d + 1) * 128]
                    for h2 in range(2):
                        ins = t.matmul(PS_B[:, h2 * 512:(h2 + 1) * 512], lhsT,
                                       Wq[:, d * 1024 + h2 * 512: d * 1024 + (h2 + 1) * 512],
                                       start=(d == 0), stop=(d == ND - 1))
                return ins
            P.step("tensor", em_q, reads=[b_ncb[par], b_w], writes=[bB0, bB1])
            P.step("scalar", lambda s: s.activation(SM["rt"][:], PS_B[:], AF.Silu),
                   reads=[bB0, bB1], writes=[b_rt])
            for h in range(H):
                qfh = SM["rt"][:, h * 256:(h + 1) * 256]
                P.step("vector", lambda v, qfh=qfh: v.tensor_mul(
                    SM["scr1k"][:, 0:256], qfh, qfh), reads=[b_rt], writes=[b_scr])
                P.step("vector", lambda v: v.reduce_sum(st[:, 0:1],
                                                        SM["scr1k"][:, 0:256], axis=AX.X),
                    reads=[b_scr], writes=[b_stats])
                P.step("scalar", lambda s: s.activation(st[:, 1:2], st[:, 0:1], AF.Ln,
                                                        bias=SM["eps"][:, 1:2]),
                       reads=[b_stats], writes=[b_stats])
                P.step("scalar", lambda s: s.activation(st[:, 1:2], st[:, 1:2],
                                                        AF.Exp, scale=-0.5),
                       reads=[b_stats], writes=[b_stats])
                P.step("scalar", lambda s: s.mul(st[:, 1:2], st[:, 1:2], 0.0625),
                       reads=[b_stats], writes=[b_stats])
                P.step("vector", lambda v, h=h, qfh=qfh: v.tensor_scalar(
                    qcb[:, h * 256:(h + 1) * 256], qfh, st[:, 1:2], None, ALU.mult),
                    reads=[b_rt, b_stats], writes=[b_qc])
            def em_b(t, nb=nb):
                ins = None
                for d in range(ND):
                    ins = t.matmul(PS_B[:, 0:4], nb[:, d * 128:(d + 1) * 128],
                                   Wb[:, d * 4:(d + 1) * 4],
                                   start=(d == 0), stop=(d == ND - 1))
                return ins
            P.step("tensor", em_b, reads=[b_ncb[par], b_w], writes=[bB0])
            P.step("scalar", lambda s: s.activation(bay[:], PS_B[:, 0:4], AF.Sigmoid),
                   reads=[bB0], writes=[b_bay])

            def em_k(t, nb=nb):
                ins = None
                for d in range(ND):
                    lhsT = nb[:, d * 128:(d + 1) * 128]
                    for h2 in range(2):
                        ins = t.matmul(PS_C[:, h2 * 512:(h2 + 1) * 512], lhsT,
                                       Wk[:, d * 1024 + h2 * 512: d * 1024 + (h2 + 1) * 512],
                                       start=(d == 0), stop=(d == ND - 1))
                return ins
            P.step("tensor", em_k, reads=[b_ncb[par], b_w], writes=[bC0, bC1])
            P.step("scalar", lambda s: s.activation(SM["rt"][:], PS_C[:], AF.Silu),
                   reads=[bC0, bC1], writes=[b_rt])
            for h in range(H):
                qfh = SM["rt"][:, h * 256:(h + 1) * 256]
                P.step("vector", lambda v, qfh=qfh: v.tensor_mul(
                    SM["scr1k"][:, 0:256], qfh, qfh), reads=[b_rt], writes=[b_scr])
                P.step("vector", lambda v: v.reduce_sum(st[:, 0:1],
                                                        SM["scr1k"][:, 0:256], axis=AX.X),
                    reads=[b_scr], writes=[b_stats])
                P.step("scalar", lambda s: s.activation(st[:, 1:2], st[:, 0:1], AF.Ln,
                                                        bias=SM["eps"][:, 1:2]),
                       reads=[b_stats], writes=[b_stats])
                P.step("scalar", lambda s: s.activation(st[:, 1:2], st[:, 1:2],
                                                        AF.Exp, scale=-0.5),
                       reads=[b_stats], writes=[b_stats])
                P.step("vector", lambda v, h=h, qfh=qfh: v.tensor_scalar(
                    kcb[:, h * 256:(h + 1) * 256], qfh, st[:, 1:2], None, ALU.mult),
                    reads=[b_rt, b_stats], writes=[b_kc])

            def em_v(t, nb=nb):
                ins = None
                for d in range(ND):
                    lhsT = nb[:, d * 128:(d + 1) * 128]
                    for q4 in range(4):
                        ins = t.matmul(psA(q4), lhsT,
                                       Wv[:, d * 2048 + q4 * 512: d * 2048 + (q4 + 1) * 512],
                                       start=(d == 0), stop=(d == ND - 1))
                return ins
            P.step("tensor", em_v, reads=[b_ncb[par], b_w], writes=bA)
            P.step("scalar", lambda s: s.activation(vcb[:], PS_A[:], AF.Silu),
                   reads=bA, writes=[b_vc])

            for h in range(H):
                for kt in range(2):
                    sl2 = (h * 2 + kt) % 2
                    transpose128(qcb[:, h * 256 + kt * 128: h * 256 + (kt + 1) * 128],
                                 QT[:, (h * 2 + kt) * 128:(h * 2 + kt + 1) * 128],
                                 PS_B_bf[:, sl2 * 1024: sl2 * 1024 + 128],
                                 bB0 if sl2 == 0 else bB1, b_qc, b_QT,
                                 evac="vector" if sl2 else "scalar")
                    transpose128(kcb[:, h * 256 + kt * 128: h * 256 + (kt + 1) * 128],
                                 KT[:, (h * 2 + kt) * 128:(h * 2 + kt + 1) * 128],
                                 PS_C_bf[:, sl2 * 1024: sl2 * 1024 + 128],
                                 bC0 if sl2 == 0 else bC1, b_kc, b_KT,
                                 evac="scalar" if sl2 else "vector")

            for h in range(H):
                slS = PS_B[:, 0:512]
                slO = PS_B[:, 512:1024]
                m0 = PS_C[:, 0:128]
                m1 = PS_C[:, 128:256]
                kth = [KT[:, (h * 2 + kt) * 128:(h * 2 + kt + 1) * 128] for kt in range(2)]
                qth = [QT[:, (h * 2 + kt) * 128:(h * 2 + kt + 1) * 128] for kt in range(2)]

                def em_A(t, kth=kth):
                    t.matmul(m0, kth[0], kth[0], start=True, stop=False)
                    return t.matmul(m0, kth[1], kth[1], start=False, stop=True)
                P.step("tensor", em_A, reads=[b_KT], writes=[bC0])
                P.step("vector", lambda v, h=h: v.scalar_tensor_tensor(
                    PW(0, 0), m0, bay[:, h:h + 1], SM["slo"][:], ALU.mult, ALU.mult),
                    reads=[bC0, b_bay, b_sm], writes=[b_PWR])
                transpose128(PW(0, 0), PW(0, 1),
                             PS_C[:, 512:640], bC1, b_PWR, b_PWR,
                             ident=SM["ident_f32"][:])
                for j in range(1, 7):
                    def em_sq(t, j=j):
                        t.matmul(m0, PW(j - 1, 1), PW(j - 1, 0), start=True, stop=True)
                        return t.matmul(m1, PW(j - 1, 0), PW(j - 1, 1),
                                        start=True, stop=True)
                    P.step("tensor", em_sq, reads=[b_PWR], writes=[bC0])
                    P.step("vector", lambda v, j=j: v.tensor_copy(
                        PWR[:, j * 256:(j + 1) * 256], PS_C[:, 0:256]),
                        reads=[bC0], writes=[b_PWR])
                zc = [Zb[:, 0:128], Zb[:, 128:256]]
                P.step("vector", lambda v: v.tensor_tensor(
                    zc[0], SM["ident_f32"][:], PW(6, 1), ALU.add),
                    reads=[b_PWR, b_sm], writes=[b_Z])
                cur = 0
                for lev in (5, 4, 3, 2, 1):
                    def em_z(t, lev=lev, cur=cur):
                        return t.matmul(m0, PW(lev, 0), zc[cur], start=True, stop=True)
                    P.step("tensor", em_z, reads=[b_PWR, b_Z], writes=[bC0])
                    P.step("vector", lambda v, cur=cur: v.tensor_tensor(
                        zc[1 - cur], m0, zc[cur], ALU.add),
                        reads=[bC0, b_Z], writes=[b_Z])
                    cur = 1 - cur

                def em_z0(t, cur=cur):
                    return t.matmul(m0, PW(0, 0), zc[cur], start=True, stop=True)
                P.step("tensor", em_z0, reads=[b_PWR, b_Z], writes=[bC0])
                P.step("vector", lambda v, cur=cur: v.tensor_tensor(
                    zc[1 - cur], zc[cur], m0, ALU.subtract),
                    reads=[bC0, b_Z], writes=[b_Z])
                cur = 1 - cur
                P.step("vector", lambda v, h=h, cur=cur: v.tensor_scalar(
                    Mtb[:], zc[cur], bay[:, h:h + 1], None, ALU.mult),
                    reads=[b_Z, b_bay], writes=[b_Mt])

                def em_ks(t, h=h, kth=kth):
                    t.matmul(slS, kth[0], Sb[:, (h * 2) * 512:(h * 2 + 1) * 512],
                             start=True, stop=False)
                    return t.matmul(slS, kth[1], Sb[:, (h * 2 + 1) * 512:(h * 2 + 2) * 512],
                                    start=False, stop=True)
                P.step("tensor", em_ks, reads=[b_KT, b_Sb[h]], writes=[bB0])
                P.step("vector", lambda v, h=h: v.tensor_tensor(
                    Bb[:], vcb[:, h * 512:(h + 1) * 512], slS, ALU.subtract),
                    reads=[bB0, b_vc], writes=[b_B])

                def em_U(t):
                    return t.matmul(slS, Mtb[:], Bb[:], start=True, stop=True)
                P.step("tensor", em_U, reads=[b_Mt, b_B], writes=[bB0])
                P.step("vector", lambda v: v.tensor_copy(Ub[:], slS),
                       reads=[bB0], writes=[b_U])

                def em_G(t, kth=kth, qth=qth):
                    t.matmul(m0, kth[0], qth[0], start=True, stop=False)
                    return t.matmul(m0, kth[1], qth[1], start=False, stop=True)
                P.step("tensor", em_G, reads=[b_KT, b_QT], writes=[bC0])
                P.step("vector", lambda v: v.tensor_tensor(
                    Gtb[:], m0, SM["triu"][:], ALU.mult),
                    reads=[bC0, b_sm], writes=[b_Gt])

                def em_O(t, h=h, qth=qth):
                    t.matmul(slO, qth[0], Sb[:, (h * 2) * 512:(h * 2 + 1) * 512],
                             start=True, stop=False)
                    t.matmul(slO, qth[1], Sb[:, (h * 2 + 1) * 512:(h * 2 + 2) * 512],
                             start=False, stop=False)
                    return t.matmul(slO, Gtb[:], Ub[:], start=False, stop=True)
                P.step("tensor", em_O, reads=[b_QT, b_Sb[h], b_Gt, b_U], writes=[bB1])
                P.step("scalar", lambda s: s.copy(of[:], slO), reads=[bB1], writes=[b_of])
                P.step("vector", lambda v: v.tensor_mul(
                    SM["scr1k"][:, 0:512], of[:], of[:]), reads=[b_of], writes=[b_scr])
                P.step("vector", lambda v: v.reduce_sum(st[:, 0:1],
                                                        SM["scr1k"][:, 0:512], axis=AX.X),
                    reads=[b_scr], writes=[b_stats])
                P.step("scalar", lambda s: s.activation(st[:, 1:2], st[:, 0:1], AF.Ln,
                                                        scale=1.0 / 512, bias=SM["eps"][:, 0:1]),
                       reads=[b_stats], writes=[b_stats])
                P.step("scalar", lambda s: s.activation(st[:, 1:2], st[:, 1:2],
                                                        AF.Exp, scale=-0.5),
                       reads=[b_stats], writes=[b_stats])
                P.step("vector", lambda v: v.scalar_tensor_tensor(
                    onb[:], of[:], st[:, 1:2], onw[:], ALU.mult, ALU.mult),
                    reads=[b_of, b_stats, b_cst], writes=[b_on])
                for kt in range(2):
                    def em_su(t, h=h, kt=kt):
                        return t.matmul(slS, kcb[:, h * 256 + kt * 128: h * 256 + (kt + 1) * 128],
                                        Ub[:], start=True, stop=True)
                    P.step("tensor", em_su, reads=[b_kc, b_U], writes=[bB0])
                    sfs = Sf[:, (h * 2 + kt) * 512:(h * 2 + kt + 1) * 512]
                    P.step("vector", lambda v, sfs=sfs: v.tensor_tensor(
                        sfs, sfs, slS, ALU.add), reads=[bB0, b_S[h]], writes=[b_S[h]])
                    P.step("scalar", lambda s, h=h, kt=kt, sfs=sfs: s.copy(
                        Sb[:, (h * 2 + kt) * 512:(h * 2 + kt + 1) * 512], sfs),
                        reads=[b_S[h]], writes=[b_Sb[h]])
                for dv in range(4):
                    sl2 = dv % 2
                    transpose128(onb[:, dv * 128:(dv + 1) * 128],
                                 OT[:, (h * 4 + dv) * 128:(h * 4 + dv + 1) * 128],
                                 PS_C_bf[:, 1024 + sl2 * 512: 1024 + sl2 * 512 + 128],
                                 bC1, b_on, b_OT,
                                 evac="vector" if sl2 else "scalar")

            for kt in range(16):
                sl4 = kt % 2
                wslice = woq[:, sl4 * 1024:(sl4 + 1) * 1024]
                P.dma("gpsimd", [(wslice, dt_in["l1_wo"][kt * 128:(kt + 1) * 128, :])],
                      writes=[b_woq[sl4]])

                def em_wo(t, kt=kt, wslice=wslice):
                    ins = None
                    lhsT = OT[:, kt * 128:(kt + 1) * 128]
                    for h2 in range(2):
                        ins = t.matmul(PS_A[:, h2 * 512:(h2 + 1) * 512], lhsT,
                                       wslice[:, h2 * 512:(h2 + 1) * 512],
                                       start=(kt == 0), stop=(kt == 15))
                    return ins
                if kt == 0:
                    P.step("tensor", em_wo, reads=[b_OT, b_woq[sl4]],
                           writes=[bA[0], bA[1]])
                else:
                    P.step("tensor", em_wo, reads=[b_OT, b_woq[sl4], bA[0], bA[1]],
                           writes=[bA[0], bA[1]] if kt == 15 else [])
            P.step("vector", lambda v, c=c: v.scalar_tensor_tensor(
                SM["rt"][:], RES[:, c * D:(c + 1) * D], 2.0, PS_A[:, 0:1024],
                ALU.mult, ALU.add),
                reads=[bA[0], bA[1], b_res[c]], writes=[b_rt])
            ln_tile(RES[:, c * D:(c + 1) * D], b_res[c])
        tok = P.barrier_deps()
        ar.free()
        return tok

    for s in sections:
        if s == "l0attn":
            sec_dep[0] = sec_l0attn()
        elif s == "l0ffn":
            sec_dep[0] = sec_ffn("l0", 8)
        elif s == "l1attn":
            sec_dep[0] = sec_l1attn()
        elif s == "l1ffn":
            sec_dep[0] = sec_ffn("l1", 4)

    for dn in dump:
        th, bf_ = P_reg[dn]
        dap = nc.dram_tensor("dump_" + dn, list(th.shape), th.dtype,
                             kind="ExternalOutput").ap()
        P.dma("sync", [(dap, th[:])], reads=[bf_], deps=sec_dep[0])
    P.dma("sync", [(out.rearrange("(k p) d -> p k d", p=128),
                    RES[:].rearrange("p (k d) -> p k d", d=D))],
          reads=b_res, deps=sec_dep[0])
    P.ops["sync"].append(
        ([], lambda eng, sem=P.sem["sync.D"], v=P.cnt["sync.D"]: eng.wait_ge(sem, v)))
    P.emit_all()
    return nc, sorted(dt_in.keys())



def _host_prepare(inputs):
    f = np.ascontiguousarray
    w = {}
    w["ident_bf"] = np.eye(128, dtype=ml_dtypes.bfloat16)
    w["ident_f32"] = np.eye(128, dtype=np.float32)
    r = np.arange(128)
    w["triu_incl"] = (r[None, :] >= r[:, None]).astype(np.float32)
    w["slo_mask"] = (r[:, None] > r[None, :]).astype(np.float32)
    w["gamma"] = np.asarray(inputs["L0_attn_gamma"], np.float32).reshape(1, 1)
    w["eps_bc"] = f(np.broadcast_to(np.array([1e-5, 1e-6], np.float32), (128, 2)))
    w["l0_wqkv"] = f(np.asarray(inputs["L0_attn_Wqkv"], np.float32))
    w["l0_bqkv_pk"] = f(np.asarray(inputs["L0_attn_bqkv"], np.float32).reshape(24, 128).T)
    w["l0_vbias_bc"] = f(np.broadcast_to(
        np.asarray(inputs["L0_attn_bqkv"], np.float32)[2048:], (128, 1024)))
    w["l0_wproj"] = f(np.asarray(inputs["L0_attn_Wproj"], np.float32))
    w["l0_bproj_bc"] = f(np.broadcast_to(
        np.asarray(inputs["L0_attn_bproj"], np.float32), (128, 1024)))
    w["l0_gatewT"] = f(np.asarray(inputs["L0_ffn_gate_w"], np.float32).T)
    w["l0_gbias_bc"] = f(np.broadcast_to(
        np.asarray(inputs["L0_ffn_gate_bias"], np.float32), (128, 32)))
    w["l0_eg"] = f(np.transpose(np.asarray(inputs["L0_ffn_expert_gate"], np.float32), (0, 2, 1)))
    w["l0_eu"] = f(np.transpose(np.asarray(inputs["L0_ffn_expert_up"], np.float32), (0, 2, 1)))
    w["l0_ed"] = f(np.transpose(np.asarray(inputs["L0_ffn_expert_down"], np.float32), (0, 2, 1)))
    w["l0_sg"] = f(np.asarray(inputs["L0_ffn_shared_gate"], np.float32).T)
    w["l0_su"] = f(np.asarray(inputs["L0_ffn_shared_up"], np.float32).T)
    w["l0_sd"] = f(np.asarray(inputs["L0_ffn_shared_down"], np.float32).T)
    w["l1_lnw_bc"] = f(np.broadcast_to(np.asarray(inputs["L1_attn_ln_w"], np.float32), (128, 1024)))
    w["l1_lnb_bc"] = f(np.broadcast_to(np.asarray(inputs["L1_attn_ln_b"], np.float32), (128, 1024)))
    w["l1_wq"] = f(np.asarray(inputs["L1_attn_Wq"], np.float32))
    w["l1_wk"] = f(np.asarray(inputs["L1_attn_Wk"], np.float32))
    w["l1_wv"] = f(np.asarray(inputs["L1_attn_Wv"], np.float32))
    w["l1_wb"] = f(np.asarray(inputs["L1_attn_Wb"], np.float32))
    w["l1_onw_bc"] = f(np.broadcast_to(np.asarray(inputs["L1_attn_o_norm_w"], np.float32), (128, 512)))
    w["l1_wo"] = f(np.asarray(inputs["L1_attn_Wo"], np.float32))
    w["l1_gatewT"] = f(np.asarray(inputs["L1_ffn_router_weight"], np.float32).T)
    w["l1_gbias_bc"] = f(np.broadcast_to(
        np.asarray(inputs["L1_ffn_router_bias"], np.float32), (128, 32)))
    w["l1_eg"] = f(np.transpose(np.asarray(inputs["L1_ffn_gate_w"], np.float32), (0, 2, 1)))
    w["l1_eu"] = f(np.transpose(np.asarray(inputs["L1_ffn_up_w"], np.float32), (0, 2, 1)))
    w["l1_ed"] = f(np.transpose(np.asarray(inputs["L1_ffn_down_w"], np.float32), (0, 2, 1)))
    w["l1_sg"] = f(np.asarray(inputs["L1_ffn_shared_gate_w"], np.float32).T)
    w["l1_su"] = f(np.asarray(inputs["L1_ffn_shared_up_w"], np.float32).T)
    w["l1_sd"] = f(np.asarray(inputs["L1_ffn_shared_down_w"], np.float32).T)
    return w


def kernel(**inputs):
    w = _host_prepare(inputs)
    x = np.ascontiguousarray(np.asarray(inputs["L0_attn_x"], np.float32))
    if MODE == "fused":
        stages = [["l0attn", "l0ffn", "l1attn", "l1ffn"]]
    else:
        stages = [["l0attn"], ["l0ffn"], ["l1attn"], ["l1ffn"]]
    cur = x
    for secs in stages:
        nc, names = build_program(secs)
        in_maps = []
        for b in range(8):
            m = {k: w[k] for k in names if k in w}
            m["resid_in"] = np.ascontiguousarray(cur[b])
            in_maps.append(m)
        res = run_bass_kernel_spmd(nc, in_maps, core_ids=list(range(8)))
        cur = np.stack([np.asarray(res.results[b]["out"]) for b in range(8)], 0)
    return cur


# version 06
# speedup vs baseline: 1.2432x; 1.0440x over previous
import numpy as np
import ml_dtypes
import concourse.bass as bass
import concourse.mybir as mybir
from concourse.bass_types import AP
from concourse.bass_utils import run_bass_kernel_spmd

F32 = mybir.dt.float32
BF16 = mybir.dt.bfloat16
AF = mybir.ActivationFunctionType
ALU = mybir.AluOpType
AX = mybir.AxisListType

T, D = 2048, 1024
NT, ND = T // 128, D // 128
NEG = -1.0e30

MODE = "fused"
DBG_EXP = False
DBG_IDENT = False
DBG_ROUTE = False

ENGINES = ("tensor", "vector", "scalar", "sync", "gpsimd")


class Buf:
    __slots__ = ("name", "writers", "readers")

    def __init__(self, name):
        self.name = name
        self.writers = {}
        self.readers = {}


class Prog:

    def __init__(self, nc):
        self.nc = nc
        self.keys = [e + ".C" for e in ENGINES] + [e + ".D" for e in ENGINES]
        self.sem = {k: nc.alloc_semaphore("sem_" + k.replace(".", "_"))
                    for k in self.keys}
        self.cnt = {k: 0 for k in self.keys}
        self.hwm = {e: {} for e in ENGINES}
        self.ops = {e: [] for e in ENGINES}
        self.bufsem = {}

    def _buf_key(self, buf):
        k = self.bufsem.get(id(buf))
        if k is None:
            k = "B%d.%s" % (len(self.bufsem), buf.name)
            self.bufsem[id(buf)] = k
            self.sem[k] = self.nc.alloc_semaphore("semb_%d" % len(self.bufsem))
            self.cnt[k] = 0
        return k

    def tok(self, e):
        return (e + ".C", self.cnt[e + ".C"])

    def tokd(self, e):
        return (e + ".D", self.cnt[e + ".D"])

    def _plan(self, e, key, emit_wrap, reads, writes, deps, inc):
        want = list(deps)
        for b in reads:
            want.extend(b.writers.items())
        for b in writes:
            want.extend(b.writers.items())
            want.extend(b.readers.items())
        waits = []
        for (k2, c) in want:
            if k2 == e + ".C" and e == "tensor":
                continue
            if c > self.hwm[e].get(k2, 0):
                self.hwm[e][k2] = c
                waits.append((self.sem[k2], c))
        self.ops[e].append((waits, emit_wrap))
        self.cnt[key] += inc
        t = (key, self.cnt[key])
        for b in reads:
            b.readers[key] = self.cnt[key]
        for b in writes:
            b.writers[key] = self.cnt[key]
            b.readers = {}
        return t

    def step(self, e, emit, reads=(), writes=(), deps=(), inc=1):
        key = e + ".C"
        sem = self.sem[key]

        def wrap(eng, emit=emit, inc=inc, sem=sem):
            last = emit(eng)
            if inc and last is not None:
                last.then_inc(sem, inc)
        return self._plan(e, key, wrap, reads, writes, deps, inc)

    def dma(self, e, movs, reads=(), writes=(), deps=()):
        if writes:
            key = self._buf_key(writes[0])
        else:
            key = e + ".D"
        sem = self.sem[key]

        def wrap(eng, movs=movs, sem=sem):
            for m in movs:
                kw = m[2] if len(m) > 2 else {}
                eng.dma_start(out=m[0], in_=m[1], **kw).then_inc(sem, 16)
        return self._plan(e, key, wrap, reads, writes, deps, 16 * len(movs))

    def fence(self, deps):
        for e in ENGINES:
            self.step(e, lambda eng: eng.nop(), deps=deps, inc=1)

    def barrier_deps(self):
        return ([(e + ".C", self.cnt[e + ".C"]) for e in ENGINES if self.cnt[e + ".C"] > 0]
                + [(e + ".D", self.cnt[e + ".D"]) for e in ENGINES if self.cnt[e + ".D"] > 0])

    def emit_all(self):
        with self.nc.Block() as block:
            for e in ENGINES:
                def runner(eng, steps=self.ops[e]):
                    for waits, wrap in steps:
                        for (s, c) in waits:
                            eng.wait_ge(s, c)
                        wrap(eng)
                getattr(block, e)(runner)


def build_program(sections, dump=()):
    nc = bass.Bass("TRN2", target_bir_lowering=False,
                   detect_race_conditions=False)
    P = Prog(nc)
    dt_in = {}

    def din(name, shape, dtype=F32):
        dt_in[name] = nc.dram_tensor(name, shape, dtype, kind="ExternalInput").ap()
        return dt_in[name]

    resid_in = din("resid_in", [T, D])
    out = nc.dram_tensor("out", [T, D], F32, kind="ExternalOutput").ap()

    din("ident_bf", [128, 128], BF16)
    din("ident_f32", [128, 128])
    din("triu_incl", [128, 128])
    din("slo_mask", [128, 128])
    din("gamma", [1, 1])
    din("eps_bc", [128, 2])

    need = set(sections)
    if "l0attn" in need:
        din("l0_wqkv", [D, 3 * D])
        din("l0_bqkv_pk", [128, 24])
        din("l0_vbias_bc", [128, D])
        din("l0_wproj", [D, D])
        din("l0_bproj_bc", [128, D])
    for lx in ("l0", "l1"):
        if lx + "ffn" in need:
            din(lx + "_gatewT", [D, 32])
            din(lx + "_gbias_bc", [128, 32])
            din(lx + "_eg", [32, D, 512])
            din(lx + "_eu", [32, D, 512])
            din(lx + "_ed", [32, 512, D])
            din(lx + "_sg", [D, D])
            din(lx + "_su", [D, D])
            din(lx + "_sd", [D, D])
    if "l1attn" in need:
        din("l1_lnw_bc", [128, D])
        din("l1_lnb_bc", [128, D])
        din("l1_wq", [D, D])
        din("l1_wk", [D, D])
        din("l1_wv", [D, 2 * D])
        din("l1_wb", [D, 4])
        din("l1_onw_bc", [128, 512])
        din("l1_wo", [2 * D, D])

    scr_prev = nc.dram_tensor("scr_prev", [1, 255], F32, kind="Internal").ap()
    scr_wT = nc.dram_tensor("scr_wT", [32, T], F32, kind="Internal").ap()
    scr_nT = nc.dram_tensor("scr_nT", [D, T], BF16, kind="Internal").ap()
    b_prevd = Buf("prevd")
    b_wTd = Buf("wTd")
    b_nTd = Buf("nTd")

    RES = nc.alloc_sbuf_tensor("RES", [128, NT * D], F32)
    SM = {
        "ident_bf": nc.alloc_sbuf_tensor("sm_identbf", [128, 128], BF16),
        "ident_f32": nc.alloc_sbuf_tensor("sm_identf32", [128, 128], F32),
        "triu": nc.alloc_sbuf_tensor("sm_triu", [128, 128], F32),
        "slo": nc.alloc_sbuf_tensor("sm_slo", [128, 128], F32),
        "D": nc.alloc_sbuf_tensor("sm_D", [128, 128], F32),
        "gp": nc.alloc_sbuf_tensor("sm_gp", [1, 512], F32),
        "stats": nc.alloc_sbuf_tensor("sm_stats", [128, 8], F32),
        "scr1k": nc.alloc_sbuf_tensor("sm_scr1k", [128, 1024], F32),
        "rt": nc.alloc_sbuf_tensor("sm_rt", [128, 1024], F32),
        "eps": nc.alloc_sbuf_tensor("sm_eps", [128, 2], F32),
    }
    b_res = [Buf(f"res{k}") for k in range(NT)]
    b_sm = Buf("sm")
    b_rt = Buf("rt")
    b_stats = Buf("stats")
    b_scr = Buf("scr")

    PS_A = nc.alloc_psum_tensor("PS_A", [128, 2048], F32)
    PS_B = nc.alloc_psum_tensor("PS_B", [128, 1024], F32)
    PS_C = nc.alloc_psum_tensor("PS_C", [128, 1024], F32)
    bA = [Buf(f"psA{i}") for i in range(4)]
    bB0, bB1 = Buf("psB0"), Buf("psB1")
    bC0, bC1 = Buf("psC0"), Buf("psC1")

    def psA(i):
        return PS_A[:, i * 512:(i + 1) * 512]

    P_reg = {}
    PS_B_bf = PS_B.ap().bitcast(BF16)
    PS_C_bf = PS_C.ap().bitcast(BF16)
    PS_A_bf = PS_A.ap().bitcast(BF16)

    st_ln = SM["stats"][:, 4:8]

    def ln_tile(out_ap, b_out, w_ap=None, b_ap=None, b_w=None):
        rt = SM["rt"][:]
        s1, mu, ve, rs = (st_ln[:, i:i + 1] for i in range(4))
        P.step("vector", lambda v: v.reduce_sum(s1, rt, axis=AX.X),
               reads=[b_rt], writes=[b_stats])
        P.step("scalar", lambda s: s.mul(mu, s1, 1.0 / 1024.0),
               reads=[b_stats], writes=[b_stats])
        P.step("vector", lambda v: v.tensor_scalar(rt, rt, mu, None, ALU.subtract),
               reads=[b_rt, b_stats], writes=[b_rt])
        P.step("vector", lambda v: v.tensor_mul(SM["scr1k"][:], rt, rt),
               reads=[b_rt], writes=[b_scr])
        P.step("vector", lambda v: v.reduce_sum(ve, SM["scr1k"][:], axis=AX.X),
               reads=[b_scr], writes=[b_stats])
        P.step("scalar", lambda s: s.activation(rs, ve, AF.Ln, scale=1.0 / 1024.0,
                                                bias=SM["eps"][:, 0:1]),
               reads=[b_stats], writes=[b_stats])
        P.step("scalar", lambda s: s.activation(rs, rs, AF.Exp, scale=-0.5),
               reads=[b_stats], writes=[b_stats])
        if w_ap is None:
            return P.step("vector", lambda v: v.tensor_scalar(
                out_ap, rt, rs, None, ALU.mult),
                reads=[b_rt, b_stats], writes=[b_out])
        P.step("vector", lambda v: v.scalar_tensor_tensor(
            out_ap, rt, rs, w_ap, ALU.mult, ALU.mult),
            reads=[b_rt, b_stats, b_w], writes=[b_out])
        return P.step("vector", lambda v: v.tensor_tensor(out_ap, out_ap, b_ap, ALU.add),
                      reads=[b_out, b_w], writes=[b_out])

    def transpose128(src_ap, dst_ap, ps_ap, ps_buf, src_buf, dst_buf,
                     ident=None, evac="scalar"):
        idn = SM["ident_bf"][:] if ident is None else ident
        P.step("tensor", lambda t: t.transpose(ps_ap, src_ap, idn),
               reads=[src_buf, b_sm], writes=[ps_buf])
        if evac == "scalar":
            return P.step("scalar", lambda s: s.copy(dst_ap, ps_ap),
                          reads=[ps_buf], writes=[dst_buf])
        return P.step("vector", lambda v: v.tensor_copy(dst_ap, ps_ap),
                      reads=[ps_buf], writes=[dst_buf])

    P.dma("sync", [(RES[:].rearrange("p (k d) -> p k d", d=D),
                    resid_in.rearrange("(k p) d -> p k d", p=128))],
          writes=b_res)
    P.dma("sync", [(SM["ident_bf"][:], dt_in["ident_bf"]),
                   (SM["ident_f32"][:], dt_in["ident_f32"]),
                   (SM["triu"][:], dt_in["triu_incl"]),
                   (SM["slo"][:], dt_in["slo_mask"]),
                   (SM["gp"][0:1, 0:1], dt_in["gamma"]),
                   (SM["eps"][:], dt_in["eps_bc"])], writes=[b_sm])

    gp = SM["gp"]
    Q0 = 64

    P.step("vector", lambda v: v.memset(gp[0:1, Q0:Q0 + 1], 1.0),
           reads=[b_sm], writes=[b_sm])
    n = 1
    while n <= 64:
        P.step("vector", lambda v, n=n: v.tensor_tensor(
            gp[0:1, 32:33], gp[0:1, Q0 + n - 1:Q0 + n], gp[0:1, 0:1], ALU.mult),
            reads=[b_sm], writes=[b_sm])
        P.step("vector", lambda v, n=n: v.tensor_scalar(
            gp[0:1, Q0 + n:Q0 + 2 * n], gp[0:1, Q0:Q0 + n], gp[0:1, 32:33],
            None, ALU.mult), reads=[b_sm], writes=[b_sm])
        n *= 2
    P.step("vector", lambda v: v.memset(gp[0:1, 384:512], 0.0),
           reads=[b_sm], writes=[b_sm])
    P.step("vector", lambda v: v.tensor_copy(gp[0:1, 256:384],
                                             gp[0:1, Q0 + 127:Q0 - 1:-1]),
           reads=[b_sm], writes=[b_sm])
    P.dma("sync", [(scr_prev, gp[0:1, 256:511])], reads=[b_sm], writes=[b_prevd])
    P.dma("gpsimd",
          [(SM["D"][i:i + 1, :], scr_prev[0:1, 127 - i:255 - i]) for i in range(128)],
          reads=[b_prevd], writes=[b_sm])

    sec_dep = [P.barrier_deps()]

    def build_yt(YT, b_yt, xbf, b_xbf):
        for k in range(NT):
            par = k % 2
            xb = xbf[:, par * 1024:(par + 1) * 1024]
            P.step("scalar", lambda s, k=k, xb=xb: s.copy(xb, RES[:, k * D:(k + 1) * D]),
                   reads=[b_res[k]], writes=[b_xbf[par]])
            for j in range(ND):
                sl = j % 4
                pa = PS_A_bf[:, sl * 1024:sl * 1024 + 128]
                transpose128(xb[:, j * 128:(j + 1) * 128],
                             YT[:, j * T + k * 128: j * T + (k + 1) * 128],
                             pa, bA[sl], b_xbf[par], b_yt,
                             evac="vector" if (j % 2) else "scalar")

    def sec_l0attn():
        ar = nc.auto_arena()
        YT = ar.sbuf_tensor("a0_YT", [128, ND * T], BF16)
        Kf = ar.sbuf_tensor("a0_K", [128, ND * T], BF16)
        Vt = ar.sbuf_tensor("a0_V", [128, NT * 1024], BF16)
        wcol = ar.sbuf_tensor("a0_wcol", [128, 2 * 8 * 128], BF16)
        bpb = ar.sbuf_tensor("a0_bp", [128, 1024], F32)
        bqk = ar.sbuf_tensor("a0_bqk", [128, 24], F32)
        ar.allocate()
        ar_y = nc.auto_arena()
        Wv = ar_y.sbuf_tensor("a0_Wv", [128, 8 * 1024], BF16)
        xbf = ar_y.sbuf_tensor("a0_xbf", [128, 2048], BF16)
        vb = ar_y.sbuf_tensor("a0_vb", [128, 1024], F32)
        ar_y.allocate()

        b_yt = Buf("yt"); b_q = Buf("q"); b_k = Buf("k"); b_v = Buf("v")
        b_wv = Buf("wv"); b_wcol = [Buf("wc0"), Buf("wc1")]
        b_xbf = [Buf("xb0"), Buf("xb1")]
        b_s = Buf("s"); b_p = Buf("p"); b_pt = Buf("pt"); b_o = Buf("o")
        b_ot = Buf("ot"); b_wp = [Buf("wp0"), Buf("wp1")]
        b_cst = Buf("a0cst")

        P.fence(sec_dep[0])
        P.dma("sync", [(vb[:], dt_in["l0_vbias_bc"]), (bpb[:], dt_in["l0_bproj_bc"]),
                       (bqk[:], dt_in["l0_bqkv_pk"])], writes=[b_cst])
        build_yt(YT, b_yt, xbf, b_xbf)

        wqkv = dt_in["l0_wqkv"]
        P.dma("gpsimd", [(Wv[:].rearrange("p (k n) -> p k n", n=1024),
                          wqkv[:, 2 * D:3 * D].rearrange("(k p) n -> p k n", p=128))],
              writes=[b_wv])
        for tt in range(NT):
            ps = PS_B if tt % 2 == 0 else PS_C
            bp = [bB0, bB1] if tt % 2 == 0 else [bC0, bC1]

            def em(t, tt=tt, ps=ps):
                ins = None
                for d in range(ND):
                    lhsT = YT[:, d * T + tt * 128: d * T + (tt + 1) * 128]
                    for h in range(2):
                        ins = t.matmul(ps[:, h * 512:(h + 1) * 512], lhsT,
                                       Wv[:, d * 1024 + h * 512: d * 1024 + (h + 1) * 512],
                                       start=(d == 0), stop=(d == ND - 1))
                return ins
            P.step("tensor", em, reads=[b_yt, b_wv], writes=bp)
            P.step("vector", lambda v, tt=tt, ps=ps: v.scalar_tensor_tensor(
                Vt[:, tt * 1024:(tt + 1) * 1024], ps[:], 0.0, vb[:], ALU.add, ALU.add),
                reads=bp + [b_cst], writes=[b_v])
        for j in range(ND):
            par = j % 2
            wc = wcol[:, par * 1024:(par + 1) * 1024]
            col = D + j * 128
            P.dma("gpsimd", [(wc.rearrange("p (k c) -> p k c", c=128),
                              wqkv[:, col:col + 128].rearrange("(k p) c -> p k c", p=128))],
                  writes=[b_wcol[par]])

            def em(t, wc=wc):
                ins = None
                for d in range(ND):
                    lhsT = wc[:, d * 128:(d + 1) * 128]
                    for tc in range(4):
                        ins = t.matmul(psA(tc), lhsT,
                                       YT[:, d * T + tc * 512: d * T + (tc + 1) * 512],
                                       start=(d == 0), stop=(d == ND - 1))
                return ins
            P.step("tensor", em, reads=[b_wcol[par], b_yt], writes=bA)
            for tc in range(4):
                P.step("scalar", lambda s, j=j, tc=tc: s.activation(
                    Kf[:, j * T + tc * 512: j * T + (tc + 1) * 512], psA(tc),
                    AF.Identity, bias=bqk[:, 8 + j: 9 + j]),
                    reads=[bA[tc], b_cst], writes=[b_k])
        P.fence(P.barrier_deps())
        ar_y.free()
        ar_d = nc.auto_arena()
        Qf = ar_d.sbuf_tensor("a0_Q", [128, 8 * 512], BF16)
        pbf = ar_d.sbuf_tensor("a0_p", [128, 2048], BF16)
        ptb = ar_d.sbuf_tensor("a0_pt", [128, 16 * 128], BF16)
        obf = ar_d.sbuf_tensor("a0_o", [128, 1024], BF16)
        otb = ar_d.sbuf_tensor("a0_ot", [128, 8 * 128], BF16)
        wpb = ar_d.sbuf_tensor("a0_wp", [128, 2 * 1024], BF16)
        ar_d.allocate()
        sc = 1.0 / float(np.sqrt(D))
        st = SM["stats"]
        b_p2_box = [Buf("p2")]
        for g in range(4):
            for j in range(ND):
                par = j % 2
                wc = wcol[:, par * 1024:(par + 1) * 1024]
                P.dma("gpsimd", [(wc.rearrange("p (k c) -> p k c", c=128),
                                  wqkv[:, j * 128:(j + 1) * 128].rearrange("(k p) c -> p k c", p=128))],
                      writes=[b_wcol[par]])
                sl = j % 4

                def em(t, wc=wc, g=g, sl=sl):
                    ins = None
                    for d in range(ND):
                        ins = t.matmul(psA(sl), wc[:, d * 128:(d + 1) * 128],
                                       YT[:, d * T + g * 512: d * T + (g + 1) * 512],
                                       start=(d == 0), stop=(d == ND - 1))
                    return ins
                P.step("tensor", em, reads=[b_wcol[par], b_yt], writes=[bA[sl]])
                P.step("scalar", lambda s, j=j, sl=sl: s.activation(
                    Qf[:, j * 512:(j + 1) * 512], psA(sl), AF.Identity,
                    bias=bqk[:, j:j + 1]), reads=[bA[sl], b_cst], writes=[b_q])
            for qt in range(g * 4, g * 4 + 4):
                def em(t, qt=qt):
                    ins = None
                    for j in range(ND):
                        lhsT = Qf[:, j * 512 + (qt % 4) * 128: j * 512 + (qt % 4 + 1) * 128]
                        for kc in range(4):
                            ins = t.matmul(psA(kc), lhsT,
                                           Kf[:, j * T + kc * 512: j * T + (kc + 1) * 512],
                                           start=(j == 0), stop=(j == ND - 1))
                    return ins
                P.step("tensor", em, reads=[b_q, b_k], writes=bA)
                P.step("vector", lambda v, qt=qt: v.tensor_tensor(
                    PS_A[:, qt * 128:(qt + 1) * 128], PS_A[:, qt * 128:(qt + 1) * 128],
                    SM["D"][:], ALU.mult), reads=bA + [b_sm], writes=bA)
                P.step("vector", lambda v: v.reduce_max(st[:, 0:1], PS_A[:], axis=AX.X,
                                                        negate=True),
                       reads=bA, writes=[b_stats])
                P.step("scalar", lambda s: s.mul(st[:, 1:2], st[:, 0:1], sc),
                       reads=[b_stats], writes=[b_stats])
                P.step("scalar", lambda s: s.activation(pbf[:], PS_A[:],
                                                        AF.Identity if DBG_IDENT else AF.Exp,
                                                        bias=st[:, 1:2], scale=sc),
                       reads=bA + [b_stats], writes=[b_p])
                if DBG_EXP and qt == 15:
                    dap = nc.dram_tensor("dump_exp", [128, 2048], BF16,
                                         kind="ExternalOutput").ap()
                    sap = nc.dram_tensor("dump_st", [128, 8], F32,
                                         kind="ExternalOutput").ap()
                    P.dma("sync", [(dap, pbf[:])], reads=[b_p], writes=[b_p2_box[0]])
                    P.dma("sync", [(sap, st[:])], reads=[b_stats], writes=[b_p2_box[0]])
                P.step("vector", lambda v: v.reduce_sum(st[:, 2:3], pbf[:], axis=AX.X),
                       reads=[b_p], writes=[b_stats])
                P.step("vector", lambda v: v.reciprocal(st[:, 3:4], st[:, 2:3]),
                       reads=[b_stats], writes=[b_stats])
                P.step("vector", lambda v: v.tensor_scalar(pbf[:], pbf[:], st[:, 3:4],
                                                           None, ALU.mult),
                       reads=[b_p, b_stats], writes=[b_p])
                for kt in range(NT):
                    sl2 = kt % 2
                    transpose128(pbf[:, kt * 128:(kt + 1) * 128],
                                 ptb[:, kt * 128:(kt + 1) * 128],
                                 PS_B_bf[:, sl2 * 1024: sl2 * 1024 + 128],
                                 bB0 if sl2 == 0 else bB1, b_p, b_pt,
                                 evac="vector" if kt % 2 else "scalar")

                def em(t):
                    ins = None
                    for kt in range(NT):
                        lhsT = ptb[:, kt * 128:(kt + 1) * 128]
                        for h in range(2):
                            ins = t.matmul(PS_C[:, h * 512:(h + 1) * 512], lhsT,
                                           Vt[:, kt * 1024 + h * 512: kt * 1024 + (h + 1) * 512],
                                           start=(kt == 0), stop=(kt == NT - 1))
                    return ins
                P.step("tensor", em, reads=[b_pt, b_v], writes=[bC0, bC1])
                P.step("scalar", lambda s: s.copy(obf[:], PS_C[:]),
                       reads=[bC0, bC1], writes=[b_o])
                for j in range(ND):
                    sl2 = j % 2
                    transpose128(obf[:, j * 128:(j + 1) * 128],
                                 otb[:, j * 128:(j + 1) * 128],
                                 PS_C_bf[:, sl2 * 1024: sl2 * 1024 + 128],
                                 bC0 if sl2 == 0 else bC1, b_o, b_ot,
                                 evac="vector" if j % 2 else "scalar")
                for j in range(ND):
                    par = j % 2
                    wp = wpb[:, par * 1024:(par + 1) * 1024]
                    P.dma("gpsimd", [(wp, dt_in["l0_wproj"][j * 128:(j + 1) * 128, :])],
                          writes=[b_wp[par]])

                    def em(t, j=j, wp=wp):
                        ins = None
                        for h in range(2):
                            ins = t.matmul(PS_B[:, h * 512:(h + 1) * 512],
                                           otb[:, j * 128:(j + 1) * 128],
                                           wp[:, h * 512:(h + 1) * 512],
                                           start=(j == 0), stop=(j == ND - 1))
                        return ins
                    P.step("tensor", em, reads=[b_ot, b_wp[par]], writes=[bB0, bB1])
                rt = SM["rt"]
                P.step("vector", lambda v, qt=qt: v.tensor_tensor(
                    rt[:], PS_B[:], RES[:, qt * D:(qt + 1) * D], ALU.add),
                    reads=[bB0, bB1, b_res[qt]], writes=[b_rt])
                P.step("vector", lambda v: v.tensor_tensor(rt[:], rt[:], bpb[:], ALU.add),
                       reads=[b_rt, b_cst], writes=[b_rt])
                ln_tile(RES[:, qt * D:(qt + 1) * D], b_res[qt])
        P_reg.update(Dm=(SM["D"], b_sm),
                     YT=(YT, b_yt), Kf=(Kf, b_k), Vt=(Vt, b_v), Qf=(Qf, b_q),
                     pbf=(pbf, b_p), ptb=(ptb, b_pt), obf=(obf, b_o),
                     otb=(otb, b_ot))
        tok = P.barrier_deps()
        ar_d.free()
        ar.free()
        return tok

    def sec_ffn(lx, topk):
        ar = nc.auto_arena()
        YT = ar.sbuf_tensor(lx + "YT", [128, ND * T], BF16)
        ar.allocate()
        ar_r = nc.auto_arena()
        gwT = ar_r.sbuf_tensor(lx + "gwT", [128, 8 * 32], BF16)
        gbb = ar_r.sbuf_tensor(lx + "gbb", [128, 32], F32)
        scs = ar_r.sbuf_tensor(lx + "scs", [128, NT * 32], F32)
        w32 = ar_r.sbuf_tensor(lx + "w32", [128, NT * 32], F32)
        rsc = ar_r.sbuf_tensor(lx + "rsc", [128, 128], F32)
        wTs = ar_r.sbuf_tensor(lx + "wTs", [32, NT * 128], F32)
        xbf = ar_r.sbuf_tensor(lx + "xbf", [128, 2048], BF16)
        ar_r.allocate()

        b_yt = Buf("yt"); b_xbf = [Buf("xb0"), Buf("xb1")]
        b_wg = [Buf("wg0"), Buf("wg1")]; b_wu = [Buf("wu0"), Buf("wu1")]
        b_wd = [Buf("wd0"), Buf("wd1")]; b_wb = [Buf("wb0"), Buf("wb1")]
        b_hid = [[Buf(f"h{p}{i}") for i in range(4)] for p in range(2)]
        b_ts = [Buf("ts0"), Buf("ts1")]; b_ub = [Buf("ub0"), Buf("ub1")]
        b_h1 = Buf("h1")
        b_cst = Buf("cst"); b_scs = Buf("scs"); b_w32 = Buf("w32")
        b_rsc = Buf("rsc"); b_wTs = Buf("wTs")

        P.fence(sec_dep[0])
        P.dma("sync", [(gbb[:], dt_in[lx + "_gbias_bc"])], writes=[b_cst])
        P.dma("gpsimd", [(gwT[:].rearrange("p (k c) -> p k c", c=32),
                          dt_in[lx + "_gatewT"].rearrange("(k p) c -> p k c", p=128))],
              writes=[b_cst])
        build_yt(YT, b_yt, xbf, b_xbf)

        st = SM["stats"]
        for tt in range(NT):
            sl = tt % 4

            def em(t, tt=tt, sl=sl):
                ins = None
                for d in range(ND):
                    ins = t.matmul(psA(sl)[:, 0:32],
                                   YT[:, d * T + tt * 128: d * T + (tt + 1) * 128],
                                   gwT[:, d * 32:(d + 1) * 32],
                                   start=(d == 0), stop=(d == ND - 1))
                return ins
            P.step("tensor", em, reads=[b_yt, b_cst], writes=[bA[sl]])
            P.step("scalar", lambda s, tt=tt, sl=sl: s.activation(
                scs[:, tt * 32:(tt + 1) * 32], psA(sl)[:, 0:32], AF.Sigmoid),
                reads=[bA[sl]], writes=[b_scs])
            sfc = rsc[:, 0:32]; gs = rsc[:, 32:40]; m8 = rsc[:, 40:48]
            gm = rsc[:, 48:52]; msk = rsc[:, 64:96]; zap = rsc[:, 96:128]
            sct = scs[:, tt * 32:(tt + 1) * 32]

            mm8 = rsc[:, 96:128]

            def em_r1(v, sfc=sfc, mm8=mm8, sct=sct):
                v.tensor_tensor(sfc, sct, gbb[:], ALU.add)
                ins = None
                for g in range(4):
                    ins = v.max(out=mm8[:, g * 8:(g + 1) * 8],
                                in_=sfc[:, g * 8:(g + 1) * 8])
                return ins
            P.step("vector", em_r1, reads=[b_scs, b_cst], writes=[b_rsc])

            def em_r1b(v, gs=gs, mm8=mm8):
                v.memset(gs[:, 4:8], NEG)
                ins = None
                for g in range(4):
                    ins = v.tensor_tensor(gs[:, g:g + 1], mm8[:, g * 8:g * 8 + 1],
                                          mm8[:, g * 8 + 1:g * 8 + 2], ALU.add)
                return ins
            P.step("vector", em_r1b, reads=[b_rsc], writes=[b_rsc])
            P.step("vector", lambda v, gs=gs, m8=m8: v.max(out=m8, in_=gs),
                   reads=[b_rsc], writes=[b_rsc])
            P.step("vector", lambda v, gs=gs, gm=gm, m8=m8: v.tensor_scalar(
                gm, gs[:, 0:4], m8[:, 1:2], None, ALU.is_ge),
                reads=[b_rsc], writes=[b_rsc])

            def em_r2(v, sfc=sfc, gm=gm, msk=msk):
                ins = None
                for g in range(4):
                    ins = v.tensor_scalar(msk[:, g * 8:(g + 1) * 8],
                                          sfc[:, g * 8:(g + 1) * 8],
                                          gm[:, g:g + 1], None, ALU.mult)
                return ins
            P.step("vector", em_r2, reads=[b_rsc], writes=[b_rsc])

            P.step("vector", lambda v, msk=msk, m8=m8: v.max(out=m8, in_=msk),
                   reads=[b_rsc], writes=[b_rsc])
            if topk < 8:
                P.step("vector", lambda v, m8=m8: v.memset(m8[:, topk:8], NEG),
                       reads=[b_rsc], writes=[b_rsc])
            zap = rsc[:, 96:128]
            P.step("vector", lambda v, msk=msk, zap=zap, m8=m8: v.match_replace(
                out=zap, in_to_replace=m8, in_values=msk, imm_value=NEG),
                reads=[b_rsc], writes=[b_rsc])

            def em_r3(v, msk=msk, zap=zap, sct=sct):
                v.tensor_tensor(zap, msk, zap, ALU.not_equal)
                v.tensor_tensor(zap, zap, sct, ALU.mult)
                v.reduce_sum(st[:, 0:1], zap, axis=AX.X)
                v.tensor_scalar(st[:, 0:1], st[:, 0:1], 1e-20, None, ALU.add)
                return v.reciprocal(st[:, 1:2], st[:, 0:1])
            P.step("vector", em_r3, reads=[b_rsc, b_scs], writes=[b_rsc, b_stats])
            P.step("vector", lambda v, tt=tt, zap=zap: v.tensor_scalar(
                w32[:, tt * 32:(tt + 1) * 32], zap, st[:, 1:2], 2.5,
                ALU.mult, ALU.mult), reads=[b_rsc, b_stats], writes=[b_w32])
            sl2 = tt % 2
            bps = bB0 if sl2 == 0 else bB1
            P.step("tensor", lambda t, tt=tt, sl2=sl2: t.transpose(
                PS_B[0:32, sl2 * 512: sl2 * 512 + 128],
                w32[:, tt * 32:(tt + 1) * 32], SM["ident_f32"][:]),
                reads=[b_w32, b_sm], writes=[bps])
            P.step("vector", lambda v, tt=tt, sl2=sl2: v.tensor_copy(
                wTs[0:32, tt * 128:(tt + 1) * 128],
                PS_B[0:32, sl2 * 512: sl2 * 512 + 128]),
                reads=[bps], writes=[b_wTs])
        P.dma("sync", [(scr_wT, wTs[0:32, :])], reads=[b_wTs], writes=[b_wTd])
        if DBG_ROUTE:
            d1 = nc.dram_tensor(lx + "dump_w32", [128, NT * 32], F32,
                                kind="ExternalOutput").ap()
            d2 = nc.dram_tensor(lx + "dump_scs", [128, NT * 32], F32,
                                kind="ExternalOutput").ap()
            d3 = nc.dram_tensor(lx + "dump_rsc", [128, 128], F32,
                                kind="ExternalOutput").ap()
            P.dma("sync", [(d1, w32[:]), (d2, scs[:]), (d3, rsc[:])],
                  reads=[b_w32, b_scs, b_rsc], writes=[Buf("dbgrt")])

        P.fence(P.barrier_deps())
        ar_r.free()
        ar_e = nc.auto_arena()
        WG = ar_e.sbuf_tensor(lx + "WG", [128, 2 * 8 * 512], BF16)
        WU = ar_e.sbuf_tensor(lx + "WU", [128, 2 * 8 * 512], BF16)
        WD = ar_e.sbuf_tensor(lx + "WD", [128, 2 * 4 * 1024], BF16)
        HID = ar_e.sbuf_tensor(lx + "HID", [128, 2 * 4 * 2048], BF16)
        WB = ar_e.sbuf_tensor(lx + "WB", [128, 2 * 2048], BF16)
        tsil = ar_e.sbuf_tensor(lx + "tsil", [128, 2 * 1024], BF16)
        ubf = ar_e.sbuf_tensor(lx + "ubf", [128, 2 * 1024], BF16)
        h1b = ar_e.sbuf_tensor(lx + "h1b", [128, 1024], BF16)
        ar_e.allocate()

        eg, eu, ed = dt_in[lx + "_eg"], dt_in[lx + "_eu"], dt_in[lx + "_ed"]
        sg, su, sd = dt_in[lx + "_sg"], dt_in[lx + "_su"], dt_in[lx + "_sd"]

        def wsrc(e):
            if e < 32:
                return (eg[e], eu[e], ed[e], e)
            half = e - 32
            sl = slice(half * 512, (half + 1) * 512)
            return (sg[:, sl], su[:, sl], sd[sl, :], None)

        for e in range(34):
            par = e % 2
            g_src, u_src, d_src, widx = wsrc(e)
            wg = WG[:, par * 4096:(par + 1) * 4096]
            wu = WU[:, par * 4096:(par + 1) * 4096]
            wd = WD[:, par * 4096:(par + 1) * 4096]
            wb = WB[:, par * 2048:(par + 1) * 2048]
            P.dma("gpsimd", [(wg.rearrange("p (k n) -> p k n", n=512),
                              g_src.rearrange("(k p) n -> p k n", p=128)),
                             (wu.rearrange("p (k n) -> p k n", n=512),
                              u_src.rearrange("(k p) n -> p k n", p=128)),
                             (wd.rearrange("p (k n) -> p k n", n=1024),
                              d_src.rearrange("(k p) n -> p k n", p=128))],
                  writes=[b_wg[par], b_wu[par], b_wd[par]])
            if widx is not None:
                P.dma("gpsimd", [(wb, scr_wT[widx:widx + 1, :].to_broadcast([128, 2048]))],
                      reads=[b_wTd], writes=[b_wb[par]])
            else:
                P.step("vector", lambda v, wb=wb: v.memset(wb, 1.0),
                       writes=[b_wb[par]])
            for i in range(4):
                for th in range(2):
                    gsl = [0, 1]
                    usl = [2, 3]

                    def em_g(t, i=i, th=th, wg=wg, gsl=gsl):
                        ins = None
                        for d in range(ND):
                            lhsT = wg[:, d * 512 + i * 128: d * 512 + (i + 1) * 128]
                            for q in range(2):
                                ins = t.matmul(psA(gsl[q]), lhsT,
                                               YT[:, d * T + th * 1024 + q * 512:
                                                  d * T + th * 1024 + (q + 1) * 512],
                                               start=(d == 0), stop=(d == ND - 1))
                        return ins
                    P.step("tensor", em_g, reads=[b_wg[par], b_yt],
                           writes=[bA[gsl[0]], bA[gsl[1]]])
                    tsl = tsil[:, par * 1024:(par + 1) * 1024]
                    P.step("scalar", lambda s, gsl=gsl, tsl=tsl: s.activation(
                        tsl, PS_A[:, gsl[0] * 512:(gsl[1] + 1) * 512], AF.Silu),
                        reads=[bA[gsl[0]], bA[gsl[1]]], writes=[b_ts[par]])

                    def em_u(t, i=i, th=th, wu=wu, usl=usl):
                        ins = None
                        for d in range(ND):
                            lhsT = wu[:, d * 512 + i * 128: d * 512 + (i + 1) * 128]
                            for q in range(2):
                                ins = t.matmul(psA(usl[q]), lhsT,
                                               YT[:, d * T + th * 1024 + q * 512:
                                                  d * T + th * 1024 + (q + 1) * 512],
                                               start=(d == 0), stop=(d == ND - 1))
                        return ins
                    P.step("tensor", em_u, reads=[b_wu[par], b_yt],
                           writes=[bA[usl[0]], bA[usl[1]]])
                    ub = ubf[:, par * 1024:(par + 1) * 1024]
                    P.step("scalar", lambda s, usl=usl, ub=ub: s.copy(
                        ub, PS_A[:, usl[0] * 512:(usl[1] + 1) * 512]),
                        reads=[bA[usl[0]], bA[usl[1]]], writes=[b_ub[par]])
                    P.step("vector", lambda v, tsl=tsl, ub=ub: v.tensor_tensor(
                        h1b[:], tsl, ub, ALU.mult),
                        reads=[b_ts[par], b_ub[par]], writes=[b_h1])
                    hd = HID[:, par * 8192 + i * 2048 + th * 1024:
                             par * 8192 + i * 2048 + (th + 1) * 1024]
                    P.step("vector", lambda v, hd=hd, wb=wb, th=th: v.tensor_tensor(
                        hd, h1b[:], wb[:, th * 1024:(th + 1) * 1024], ALU.mult),
                        reads=[b_h1, b_wb[par]], writes=[b_hid[par][i]])
            if e % 2 == 1:
                for tt in range(NT):
                    ps = PS_B if tt % 2 == 0 else PS_C
                    bp = [bB0, bB1] if tt % 2 == 0 else [bC0, bC1]

                    def em_d(t, tt=tt, ps=ps):
                        ins = None
                        n = 0
                        for pp in range(2):
                            wdd = WD[:, pp * 4096:(pp + 1) * 4096]
                            for i in range(4):
                                lhsT = HID[:, pp * 8192 + i * 2048 + tt * 128:
                                           pp * 8192 + i * 2048 + (tt + 1) * 128]
                                for h in range(2):
                                    ins = t.matmul(ps[:, h * 512:(h + 1) * 512], lhsT,
                                                   wdd[:, i * 1024 + h * 512:
                                                       i * 1024 + (h + 1) * 512],
                                                   start=(n == 0), stop=(n == 7))
                                n += 1
                        return ins
                    rd = [b_wd[0], b_wd[1]] + b_hid[0] + b_hid[1]
                    P.step("tensor", em_d, reads=rd, writes=bp)
                    P.step("vector", lambda v, tt=tt, ps=ps: v.tensor_tensor(
                        RES[:, tt * D:(tt + 1) * D], RES[:, tt * D:(tt + 1) * D],
                        ps[:], ALU.add), reads=bp + [b_res[tt]], writes=[b_res[tt]])
        for tt in range(NT):
            P.step("vector", lambda v, tt=tt: v.tensor_copy(
                SM["rt"][:], RES[:, tt * D:(tt + 1) * D]),
                reads=[b_res[tt]], writes=[b_rt])
            ln_tile(RES[:, tt * D:(tt + 1) * D], b_res[tt])
        tok = P.barrier_deps()
        ar_e.free()
        ar.free()
        return tok

    def sec_l1attn():
        H, DK, DV = 4, 256, 512
        ar_pre = nc.auto_arena()
        lnw = ar_pre.sbuf_tensor("a1_lnw", [128, 1024], F32)
        lnb = ar_pre.sbuf_tensor("a1_lnb", [128, 1024], F32)
        nst = ar_pre.sbuf_tensor("a1_nst", [128, 1024], BF16)
        nst2 = ar_pre.sbuf_tensor("a1_nst2", [128, 1024], BF16)
        ar_pre.allocate()

        b_w = Buf("a1w")
        b_S = [Buf(f"S{h}") for h in range(H)]
        b_Sb = [Buf(f"Sb{h}") for h in range(H)]
        b_ncb = [Buf("nc0"), Buf("nc1")]
        b_qc = Buf("qc"); b_kc = Buf("kc"); b_vc = Buf("vc"); b_bay = Buf("bay"); b_QT = Buf("QT"); b_KT = Buf("KT")
        b_PWR = Buf("PWR"); b_Z = Buf("Z"); b_Mt = Buf("Mt"); b_Gt = Buf("Gt")
        b_B = Buf("B"); b_U = Buf("U"); b_of = Buf("of"); b_on = Buf("on")
        b_OT = Buf("OT"); b_cst = Buf("a1cst"); b_nst = Buf("nst")
        b_nst2 = Buf("nst2"); b_woq = [Buf(f"woq{i}") for i in range(2)]

        P.fence(sec_dep[0])
        P.dma("sync", [(lnw[:], dt_in["l1_lnw_bc"]), (lnb[:], dt_in["l1_lnb_bc"])],
              writes=[b_cst])

        nT_r = scr_nT.rearrange("(j p) t -> p j t", p=128)
        for tt in range(NT):
            P.step("vector", lambda v, tt=tt: v.tensor_copy(
                SM["rt"][:], RES[:, tt * D:(tt + 1) * D]),
                reads=[b_res[tt]], writes=[b_rt])
            ln_tile(nst[:], b_nst, w_ap=lnw[:], b_ap=lnb[:], b_w=b_cst)
            for j in range(ND):
                sl = j % 4
                transpose128(nst[:, j * 128:(j + 1) * 128],
                             nst2[:, j * 128:(j + 1) * 128],
                             PS_A_bf[:, sl * 1024: sl * 1024 + 128],
                             bA[sl], b_nst, b_nst2,
                             evac="vector" if j % 2 else "scalar")
            P.dma("sync", [(nT_r[:, :, tt * 128:(tt + 1) * 128],
                            nst2[:].rearrange("p (j c) -> p j c", c=128))],
                  reads=[b_nst2], writes=[b_nTd])

        P.fence(P.barrier_deps())
        ar_pre.free()
        ar = nc.auto_arena()
        Wq = ar.sbuf_tensor("a1_Wq", [128, 8 * 1024], BF16)
        Wk = ar.sbuf_tensor("a1_Wk", [128, 8 * 1024], BF16)
        Wv = ar.sbuf_tensor("a1_Wv", [128, 8 * 2048], BF16)
        Wb = ar.sbuf_tensor("a1_Wb", [128, 8 * 4], BF16)
        woq = ar.sbuf_tensor("a1_woq", [128, 2 * 1024], BF16)
        Sf = ar.sbuf_tensor("a1_S", [128, 4096], F32)
        Sb = ar.sbuf_tensor("a1_Sb", [128, 4096], BF16)
        ncb = ar.sbuf_tensor("a1_ncb", [128, 2 * 1024], BF16)
        qcb = ar.sbuf_tensor("a1_qcb", [128, 1024], BF16)
        kcb = ar.sbuf_tensor("a1_kcb", [128, 1024], BF16)
        vcb = ar.sbuf_tensor("a1_vcb", [128, 2048], BF16)
        bay = ar.sbuf_tensor("a1_bay", [128, 4], F32)
        QT = ar.sbuf_tensor("a1_QT", [128, 8 * 128], BF16)
        KT = ar.sbuf_tensor("a1_KT", [128, 8 * 128], BF16)
        PWR = ar.sbuf_tensor("a1_PWR", [128, 7 * 256], F32)
        Zb = ar.sbuf_tensor("a1_Z", [128, 2 * 128], F32)
        Mtb = ar.sbuf_tensor("a1_Mt", [128, 128], BF16)
        Gtb = ar.sbuf_tensor("a1_Gt", [128, 128], BF16)
        Bb = ar.sbuf_tensor("a1_B", [128, 512], BF16)
        Ub = ar.sbuf_tensor("a1_U", [128, 512], BF16)
        of = ar.sbuf_tensor("a1_of", [128, 512], F32)
        onb = ar.sbuf_tensor("a1_onb", [128, 512], BF16)
        OT = ar.sbuf_tensor("a1_OT", [128, 16 * 128], BF16)
        onw = ar.sbuf_tensor("a1_onw", [128, 512], F32)
        ar.allocate()
        P.dma("sync", [(onw[:], dt_in["l1_onw_bc"])], writes=[b_cst])
        P.dma("gpsimd", [
            (Wq[:].rearrange("p (k n) -> p k n", n=1024),
             dt_in["l1_wq"].rearrange("(k p) n -> p k n", p=128)),
            (Wk[:].rearrange("p (k n) -> p k n", n=1024),
             dt_in["l1_wk"].rearrange("(k p) n -> p k n", p=128)),
            (Wv[:].rearrange("p (k n) -> p k n", n=2048),
             dt_in["l1_wv"].rearrange("(k p) n -> p k n", p=128)),
            (Wb[:].rearrange("p (k n) -> p k n", n=4),
             dt_in["l1_wb"].rearrange("(k p) n -> p k n", p=128)),
        ], writes=[b_w])
        P.step("vector", lambda v: v.memset(Sf[:], 0.0), writes=b_S)
        P.step("vector", lambda v: v.memset(Sb[:], 0.0), writes=b_Sb)

        def PW(j, t):
            return PWR[:, j * 256 + t * 128: j * 256 + (t + 1) * 128]

        st = SM["stats"]
        for c in range(NT):
            par = c % 2
            nb = ncb[:, par * 1024:(par + 1) * 1024]
            P.dma("gpsimd", [(nb.rearrange("p (j x) -> p j x", x=128),
                              nT_r[:, :, c * 128:(c + 1) * 128])],
                  reads=[b_nTd], writes=[b_ncb[par]])

            def em_q(t, nb=nb):
                ins = None
                for d in range(ND):
                    lhsT = nb[:, d * 128:(d + 1) * 128]
                    for h2 in range(2):
                        ins = t.matmul(PS_B[:, h2 * 512:(h2 + 1) * 512], lhsT,
                                       Wq[:, d * 1024 + h2 * 512: d * 1024 + (h2 + 1) * 512],
                                       start=(d == 0), stop=(d == ND - 1))
                return ins
            P.step("tensor", em_q, reads=[b_ncb[par], b_w], writes=[bB0, bB1])
            P.step("scalar", lambda s: s.activation(SM["rt"][:], PS_B[:], AF.Silu),
                   reads=[bB0, bB1], writes=[b_rt])
            for h in range(H):
                qfh = SM["rt"][:, h * 256:(h + 1) * 256]
                P.step("vector", lambda v, qfh=qfh: v.tensor_mul(
                    SM["scr1k"][:, 0:256], qfh, qfh), reads=[b_rt], writes=[b_scr])
                P.step("vector", lambda v: v.reduce_sum(st[:, 0:1],
                                                        SM["scr1k"][:, 0:256], axis=AX.X),
                    reads=[b_scr], writes=[b_stats])
                P.step("scalar", lambda s: s.activation(st[:, 1:2], st[:, 0:1], AF.Ln,
                                                        bias=SM["eps"][:, 1:2]),
                       reads=[b_stats], writes=[b_stats])
                P.step("scalar", lambda s: s.activation(st[:, 1:2], st[:, 1:2],
                                                        AF.Exp, scale=-0.5),
                       reads=[b_stats], writes=[b_stats])
                P.step("scalar", lambda s: s.mul(st[:, 1:2], st[:, 1:2], 0.0625),
                       reads=[b_stats], writes=[b_stats])
                P.step("vector", lambda v, h=h, qfh=qfh: v.tensor_scalar(
                    qcb[:, h * 256:(h + 1) * 256], qfh, st[:, 1:2], None, ALU.mult),
                    reads=[b_rt, b_stats], writes=[b_qc])
            def em_b(t, nb=nb):
                ins = None
                for d in range(ND):
                    ins = t.matmul(PS_B[:, 0:4], nb[:, d * 128:(d + 1) * 128],
                                   Wb[:, d * 4:(d + 1) * 4],
                                   start=(d == 0), stop=(d == ND - 1))
                return ins
            P.step("tensor", em_b, reads=[b_ncb[par], b_w], writes=[bB0])
            P.step("scalar", lambda s: s.activation(bay[:], PS_B[:, 0:4], AF.Sigmoid),
                   reads=[bB0], writes=[b_bay])

            def em_k(t, nb=nb):
                ins = None
                for d in range(ND):
                    lhsT = nb[:, d * 128:(d + 1) * 128]
                    for h2 in range(2):
                        ins = t.matmul(PS_C[:, h2 * 512:(h2 + 1) * 512], lhsT,
                                       Wk[:, d * 1024 + h2 * 512: d * 1024 + (h2 + 1) * 512],
                                       start=(d == 0), stop=(d == ND - 1))
                return ins
            P.step("tensor", em_k, reads=[b_ncb[par], b_w], writes=[bC0, bC1])
            P.step("scalar", lambda s: s.activation(SM["rt"][:], PS_C[:], AF.Silu),
                   reads=[bC0, bC1], writes=[b_rt])
            for h in range(H):
                qfh = SM["rt"][:, h * 256:(h + 1) * 256]
                P.step("vector", lambda v, qfh=qfh: v.tensor_mul(
                    SM["scr1k"][:, 0:256], qfh, qfh), reads=[b_rt], writes=[b_scr])
                P.step("vector", lambda v: v.reduce_sum(st[:, 0:1],
                                                        SM["scr1k"][:, 0:256], axis=AX.X),
                    reads=[b_scr], writes=[b_stats])
                P.step("scalar", lambda s: s.activation(st[:, 1:2], st[:, 0:1], AF.Ln,
                                                        bias=SM["eps"][:, 1:2]),
                       reads=[b_stats], writes=[b_stats])
                P.step("scalar", lambda s: s.activation(st[:, 1:2], st[:, 1:2],
                                                        AF.Exp, scale=-0.5),
                       reads=[b_stats], writes=[b_stats])
                P.step("vector", lambda v, h=h, qfh=qfh: v.tensor_scalar(
                    kcb[:, h * 256:(h + 1) * 256], qfh, st[:, 1:2], None, ALU.mult),
                    reads=[b_rt, b_stats], writes=[b_kc])

            def em_v(t, nb=nb):
                ins = None
                for d in range(ND):
                    lhsT = nb[:, d * 128:(d + 1) * 128]
                    for q4 in range(4):
                        ins = t.matmul(psA(q4), lhsT,
                                       Wv[:, d * 2048 + q4 * 512: d * 2048 + (q4 + 1) * 512],
                                       start=(d == 0), stop=(d == ND - 1))
                return ins
            P.step("tensor", em_v, reads=[b_ncb[par], b_w], writes=bA)
            P.step("scalar", lambda s: s.activation(vcb[:], PS_A[:], AF.Silu),
                   reads=bA, writes=[b_vc])

            for h in range(H):
                for kt in range(2):
                    sl2 = (h * 2 + kt) % 2
                    transpose128(qcb[:, h * 256 + kt * 128: h * 256 + (kt + 1) * 128],
                                 QT[:, (h * 2 + kt) * 128:(h * 2 + kt + 1) * 128],
                                 PS_B_bf[:, sl2 * 1024: sl2 * 1024 + 128],
                                 bB0 if sl2 == 0 else bB1, b_qc, b_QT,
                                 evac="vector" if sl2 else "scalar")
                    transpose128(kcb[:, h * 256 + kt * 128: h * 256 + (kt + 1) * 128],
                                 KT[:, (h * 2 + kt) * 128:(h * 2 + kt + 1) * 128],
                                 PS_C_bf[:, sl2 * 1024: sl2 * 1024 + 128],
                                 bC0 if sl2 == 0 else bC1, b_kc, b_KT,
                                 evac="scalar" if sl2 else "vector")

            for h in range(H):
                slS = PS_B[:, 0:512]
                slO = PS_B[:, 512:1024]
                m0 = PS_C[:, 0:128]
                m1 = PS_C[:, 128:256]
                kth = [KT[:, (h * 2 + kt) * 128:(h * 2 + kt + 1) * 128] for kt in range(2)]
                qth = [QT[:, (h * 2 + kt) * 128:(h * 2 + kt + 1) * 128] for kt in range(2)]

                def em_A(t, kth=kth):
                    t.matmul(m0, kth[0], kth[0], start=True, stop=False)
                    return t.matmul(m0, kth[1], kth[1], start=False, stop=True)
                P.step("tensor", em_A, reads=[b_KT], writes=[bC0])
                P.step("vector", lambda v, h=h: v.scalar_tensor_tensor(
                    PW(0, 0), m0, bay[:, h:h + 1], SM["slo"][:], ALU.mult, ALU.mult),
                    reads=[bC0, b_bay, b_sm], writes=[b_PWR])
                transpose128(PW(0, 0), PW(0, 1),
                             PS_C[:, 512:640], bC1, b_PWR, b_PWR,
                             ident=SM["ident_f32"][:])
                for j in range(1, 7):
                    def em_sq(t, j=j):
                        t.matmul(m0, PW(j - 1, 1), PW(j - 1, 0), start=True, stop=True)
                        return t.matmul(m1, PW(j - 1, 0), PW(j - 1, 1),
                                        start=True, stop=True)
                    P.step("tensor", em_sq, reads=[b_PWR], writes=[bC0])
                    P.step("vector", lambda v, j=j: v.tensor_copy(
                        PWR[:, j * 256:(j + 1) * 256], PS_C[:, 0:256]),
                        reads=[bC0], writes=[b_PWR])
                zc = [Zb[:, 0:128], Zb[:, 128:256]]
                P.step("vector", lambda v: v.tensor_tensor(
                    zc[0], SM["ident_f32"][:], PW(6, 1), ALU.add),
                    reads=[b_PWR, b_sm], writes=[b_Z])
                cur = 0
                for lev in (5, 4, 3, 2, 1):
                    def em_z(t, lev=lev, cur=cur):
                        return t.matmul(m0, PW(lev, 0), zc[cur], start=True, stop=True)
                    P.step("tensor", em_z, reads=[b_PWR, b_Z], writes=[bC0])
                    P.step("vector", lambda v, cur=cur: v.tensor_tensor(
                        zc[1 - cur], m0, zc[cur], ALU.add),
                        reads=[bC0, b_Z], writes=[b_Z])
                    cur = 1 - cur

                def em_z0(t, cur=cur):
                    return t.matmul(m0, PW(0, 0), zc[cur], start=True, stop=True)
                P.step("tensor", em_z0, reads=[b_PWR, b_Z], writes=[bC0])
                P.step("vector", lambda v, cur=cur: v.tensor_tensor(
                    zc[1 - cur], zc[cur], m0, ALU.subtract),
                    reads=[bC0, b_Z], writes=[b_Z])
                cur = 1 - cur
                P.step("vector", lambda v, h=h, cur=cur: v.tensor_scalar(
                    Mtb[:], zc[cur], bay[:, h:h + 1], None, ALU.mult),
                    reads=[b_Z, b_bay], writes=[b_Mt])

                def em_ks(t, h=h, kth=kth):
                    t.matmul(slS, kth[0], Sb[:, (h * 2) * 512:(h * 2 + 1) * 512],
                             start=True, stop=False)
                    return t.matmul(slS, kth[1], Sb[:, (h * 2 + 1) * 512:(h * 2 + 2) * 512],
                                    start=False, stop=True)
                P.step("tensor", em_ks, reads=[b_KT, b_Sb[h]], writes=[bB0])
                P.step("vector", lambda v, h=h: v.tensor_tensor(
                    Bb[:], vcb[:, h * 512:(h + 1) * 512], slS, ALU.subtract),
                    reads=[bB0, b_vc], writes=[b_B])

                def em_U(t):
                    return t.matmul(slS, Mtb[:], Bb[:], start=True, stop=True)
                P.step("tensor", em_U, reads=[b_Mt, b_B], writes=[bB0])
                P.step("vector", lambda v: v.tensor_copy(Ub[:], slS),
                       reads=[bB0], writes=[b_U])

                def em_G(t, kth=kth, qth=qth):
                    t.matmul(m0, kth[0], qth[0], start=True, stop=False)
                    return t.matmul(m0, kth[1], qth[1], start=False, stop=True)
                P.step("tensor", em_G, reads=[b_KT, b_QT], writes=[bC0])
                P.step("vector", lambda v: v.tensor_tensor(
                    Gtb[:], m0, SM["triu"][:], ALU.mult),
                    reads=[bC0, b_sm], writes=[b_Gt])

                def em_O(t, h=h, qth=qth):
                    t.matmul(slO, qth[0], Sb[:, (h * 2) * 512:(h * 2 + 1) * 512],
                             start=True, stop=False)
                    t.matmul(slO, qth[1], Sb[:, (h * 2 + 1) * 512:(h * 2 + 2) * 512],
                             start=False, stop=False)
                    return t.matmul(slO, Gtb[:], Ub[:], start=False, stop=True)
                P.step("tensor", em_O, reads=[b_QT, b_Sb[h], b_Gt, b_U], writes=[bB1])
                P.step("scalar", lambda s: s.copy(of[:], slO), reads=[bB1], writes=[b_of])
                P.step("vector", lambda v: v.tensor_mul(
                    SM["scr1k"][:, 0:512], of[:], of[:]), reads=[b_of], writes=[b_scr])
                P.step("vector", lambda v: v.reduce_sum(st[:, 0:1],
                                                        SM["scr1k"][:, 0:512], axis=AX.X),
                    reads=[b_scr], writes=[b_stats])
                P.step("scalar", lambda s: s.activation(st[:, 1:2], st[:, 0:1], AF.Ln,
                                                        scale=1.0 / 512, bias=SM["eps"][:, 0:1]),
                       reads=[b_stats], writes=[b_stats])
                P.step("scalar", lambda s: s.activation(st[:, 1:2], st[:, 1:2],
                                                        AF.Exp, scale=-0.5),
                       reads=[b_stats], writes=[b_stats])
                P.step("vector", lambda v: v.scalar_tensor_tensor(
                    onb[:], of[:], st[:, 1:2], onw[:], ALU.mult, ALU.mult),
                    reads=[b_of, b_stats, b_cst], writes=[b_on])
                for kt in range(2):
                    def em_su(t, h=h, kt=kt):
                        return t.matmul(slS, kcb[:, h * 256 + kt * 128: h * 256 + (kt + 1) * 128],
                                        Ub[:], start=True, stop=True)
                    P.step("tensor", em_su, reads=[b_kc, b_U], writes=[bB0])
                    sfs = Sf[:, (h * 2 + kt) * 512:(h * 2 + kt + 1) * 512]
                    P.step("vector", lambda v, sfs=sfs: v.tensor_tensor(
                        sfs, sfs, slS, ALU.add), reads=[bB0, b_S[h]], writes=[b_S[h]])
                    P.step("scalar", lambda s, h=h, kt=kt, sfs=sfs: s.copy(
                        Sb[:, (h * 2 + kt) * 512:(h * 2 + kt + 1) * 512], sfs),
                        reads=[b_S[h]], writes=[b_Sb[h]])
                for dv in range(4):
                    sl2 = dv % 2
                    transpose128(onb[:, dv * 128:(dv + 1) * 128],
                                 OT[:, (h * 4 + dv) * 128:(h * 4 + dv + 1) * 128],
                                 PS_C_bf[:, 1024 + sl2 * 512: 1024 + sl2 * 512 + 128],
                                 bC1, b_on, b_OT,
                                 evac="vector" if sl2 else "scalar")

            for kt in range(16):
                sl4 = kt % 2
                wslice = woq[:, sl4 * 1024:(sl4 + 1) * 1024]
                P.dma("gpsimd", [(wslice, dt_in["l1_wo"][kt * 128:(kt + 1) * 128, :])],
                      writes=[b_woq[sl4]])

                def em_wo(t, kt=kt, wslice=wslice):
                    ins = None
                    lhsT = OT[:, kt * 128:(kt + 1) * 128]
                    for h2 in range(2):
                        ins = t.matmul(PS_A[:, h2 * 512:(h2 + 1) * 512], lhsT,
                                       wslice[:, h2 * 512:(h2 + 1) * 512],
                                       start=(kt == 0), stop=(kt == 15))
                    return ins
                if kt == 0:
                    P.step("tensor", em_wo, reads=[b_OT, b_woq[sl4]],
                           writes=[bA[0], bA[1]])
                else:
                    P.step("tensor", em_wo, reads=[b_OT, b_woq[sl4], bA[0], bA[1]],
                           writes=[bA[0], bA[1]] if kt == 15 else [])
            P.step("vector", lambda v, c=c: v.scalar_tensor_tensor(
                SM["rt"][:], RES[:, c * D:(c + 1) * D], 2.0, PS_A[:, 0:1024],
                ALU.mult, ALU.add),
                reads=[bA[0], bA[1], b_res[c]], writes=[b_rt])
            ln_tile(RES[:, c * D:(c + 1) * D], b_res[c])
        tok = P.barrier_deps()
        ar.free()
        return tok

    for s in sections:
        if s == "l0attn":
            sec_dep[0] = sec_l0attn()
        elif s == "l0ffn":
            sec_dep[0] = sec_ffn("l0", 8)
        elif s == "l1attn":
            sec_dep[0] = sec_l1attn()
        elif s == "l1ffn":
            sec_dep[0] = sec_ffn("l1", 4)

    for dn in dump:
        th, bf_ = P_reg[dn]
        dap = nc.dram_tensor("dump_" + dn, list(th.shape), th.dtype,
                             kind="ExternalOutput").ap()
        P.dma("sync", [(dap, th[:])], reads=[bf_], deps=sec_dep[0])
    P.dma("sync", [(out.rearrange("(k p) d -> p k d", p=128),
                    RES[:].rearrange("p (k d) -> p k d", d=D))],
          reads=b_res, deps=sec_dep[0])
    P.ops["sync"].append(
        ([], lambda eng, sem=P.sem["sync.D"], v=P.cnt["sync.D"]: eng.wait_ge(sem, v)))
    P.emit_all()
    return nc, sorted(dt_in.keys())



def _host_prepare(inputs):
    f = np.ascontiguousarray
    w = {}
    w["ident_bf"] = np.eye(128, dtype=ml_dtypes.bfloat16)
    w["ident_f32"] = np.eye(128, dtype=np.float32)
    r = np.arange(128)
    w["triu_incl"] = (r[None, :] >= r[:, None]).astype(np.float32)
    w["slo_mask"] = (r[:, None] > r[None, :]).astype(np.float32)
    w["gamma"] = np.asarray(inputs["L0_attn_gamma"], np.float32).reshape(1, 1)
    w["eps_bc"] = f(np.broadcast_to(np.array([1e-5, 1e-6], np.float32), (128, 2)))
    w["l0_wqkv"] = f(np.asarray(inputs["L0_attn_Wqkv"], np.float32))
    w["l0_bqkv_pk"] = f(np.asarray(inputs["L0_attn_bqkv"], np.float32).reshape(24, 128).T)
    w["l0_vbias_bc"] = f(np.broadcast_to(
        np.asarray(inputs["L0_attn_bqkv"], np.float32)[2048:], (128, 1024)))
    w["l0_wproj"] = f(np.asarray(inputs["L0_attn_Wproj"], np.float32))
    w["l0_bproj_bc"] = f(np.broadcast_to(
        np.asarray(inputs["L0_attn_bproj"], np.float32), (128, 1024)))
    w["l0_gatewT"] = f(np.asarray(inputs["L0_ffn_gate_w"], np.float32).T)
    w["l0_gbias_bc"] = f(np.broadcast_to(
        np.asarray(inputs["L0_ffn_gate_bias"], np.float32), (128, 32)))
    w["l0_eg"] = f(np.transpose(np.asarray(inputs["L0_ffn_expert_gate"], np.float32), (0, 2, 1)))
    w["l0_eu"] = f(np.transpose(np.asarray(inputs["L0_ffn_expert_up"], np.float32), (0, 2, 1)))
    w["l0_ed"] = f(np.transpose(np.asarray(inputs["L0_ffn_expert_down"], np.float32), (0, 2, 1)))
    w["l0_sg"] = f(np.asarray(inputs["L0_ffn_shared_gate"], np.float32).T)
    w["l0_su"] = f(np.asarray(inputs["L0_ffn_shared_up"], np.float32).T)
    w["l0_sd"] = f(np.asarray(inputs["L0_ffn_shared_down"], np.float32).T)
    w["l1_lnw_bc"] = f(np.broadcast_to(np.asarray(inputs["L1_attn_ln_w"], np.float32), (128, 1024)))
    w["l1_lnb_bc"] = f(np.broadcast_to(np.asarray(inputs["L1_attn_ln_b"], np.float32), (128, 1024)))
    w["l1_wq"] = f(np.asarray(inputs["L1_attn_Wq"], np.float32))
    w["l1_wk"] = f(np.asarray(inputs["L1_attn_Wk"], np.float32))
    w["l1_wv"] = f(np.asarray(inputs["L1_attn_Wv"], np.float32))
    w["l1_wb"] = f(np.asarray(inputs["L1_attn_Wb"], np.float32))
    w["l1_onw_bc"] = f(np.broadcast_to(np.asarray(inputs["L1_attn_o_norm_w"], np.float32), (128, 512)))
    w["l1_wo"] = f(np.asarray(inputs["L1_attn_Wo"], np.float32))
    w["l1_gatewT"] = f(np.asarray(inputs["L1_ffn_router_weight"], np.float32).T)
    w["l1_gbias_bc"] = f(np.broadcast_to(
        np.asarray(inputs["L1_ffn_router_bias"], np.float32), (128, 32)))
    w["l1_eg"] = f(np.transpose(np.asarray(inputs["L1_ffn_gate_w"], np.float32), (0, 2, 1)))
    w["l1_eu"] = f(np.transpose(np.asarray(inputs["L1_ffn_up_w"], np.float32), (0, 2, 1)))
    w["l1_ed"] = f(np.transpose(np.asarray(inputs["L1_ffn_down_w"], np.float32), (0, 2, 1)))
    w["l1_sg"] = f(np.asarray(inputs["L1_ffn_shared_gate_w"], np.float32).T)
    w["l1_su"] = f(np.asarray(inputs["L1_ffn_shared_up_w"], np.float32).T)
    w["l1_sd"] = f(np.asarray(inputs["L1_ffn_shared_down_w"], np.float32).T)
    return w


def kernel(**inputs):
    w = _host_prepare(inputs)
    x = np.ascontiguousarray(np.asarray(inputs["L0_attn_x"], np.float32))
    if MODE == "fused":
        stages = [["l0attn", "l0ffn", "l1attn", "l1ffn"]]
    else:
        stages = [["l0attn"], ["l0ffn"], ["l1attn"], ["l1ffn"]]
    cur = x
    for secs in stages:
        nc, names = build_program(secs)
        in_maps = []
        for b in range(8):
            m = {k: w[k] for k in names if k in w}
            m["resid_in"] = np.ascontiguousarray(cur[b])
            in_maps.append(m)
        res = run_bass_kernel_spmd(nc, in_maps, core_ids=list(range(8)))
        cur = np.stack([np.asarray(res.results[b]["out"]) for b in range(8)], 0)
    return cur


# version 07
# speedup vs baseline: 1.2967x; 1.0431x over previous
import numpy as np
import ml_dtypes
import concourse.bass as bass
import concourse.mybir as mybir
from concourse.bass_types import AP
from concourse.bass_utils import run_bass_kernel_spmd

F32 = mybir.dt.float32
BF16 = mybir.dt.bfloat16
AF = mybir.ActivationFunctionType
ALU = mybir.AluOpType
AX = mybir.AxisListType

T, D = 2048, 1024
NT, ND = T // 128, D // 128
NEG = -1.0e30

MODE = "fused"
DBG_EXP = False
DBG_IDENT = False
DBG_ROUTE = False
SEQ_HEADS = True

ENGINES = ("tensor", "vector", "scalar", "sync", "gpsimd")


class Buf:
    __slots__ = ("name", "writers", "readers")

    def __init__(self, name):
        self.name = name
        self.writers = {}
        self.readers = {}


class Prog:

    def __init__(self, nc):
        self.nc = nc
        self.keys = [e + ".C" for e in ENGINES] + [e + ".D" for e in ENGINES]
        self.sem = {k: nc.alloc_semaphore("sem_" + k.replace(".", "_"))
                    for k in self.keys}
        self.cnt = {k: 0 for k in self.keys}
        self.hwm = {e: {} for e in ENGINES}
        self.ops = {e: [] for e in ENGINES}
        self.bufsem = {}

    def _buf_key(self, buf, e):
        k = self.bufsem.get((id(buf), e))
        if k is None:
            k = "B%d.%s.%s" % (len(self.bufsem), buf.name, e)
            self.bufsem[(id(buf), e)] = k
            self.sem[k] = self.nc.alloc_semaphore("semb_%d" % len(self.bufsem))
            self.cnt[k] = 0
        return k

    def tok(self, e):
        return (e + ".C", self.cnt[e + ".C"])

    def tokd(self, e):
        return (e + ".D", self.cnt[e + ".D"])

    def _plan(self, e, key, emit_wrap, reads, writes, deps, inc):
        want = list(deps)
        for b in reads:
            want.extend(b.writers.items())
        for b in writes:
            want.extend(b.writers.items())
            want.extend(b.readers.items())
        waits = []
        for (k2, c) in want:
            if k2 == e + ".C" and e == "tensor":
                continue
            if c > self.hwm[e].get(k2, 0):
                self.hwm[e][k2] = c
                waits.append((self.sem[k2], c))
        self.ops[e].append((waits, emit_wrap))
        self.cnt[key] += inc
        t = (key, self.cnt[key])
        for b in reads:
            b.readers[key] = self.cnt[key]
        for b in writes:
            b.writers[key] = self.cnt[key]
            b.readers = {}
        return t

    def step(self, e, emit, reads=(), writes=(), deps=(), inc=1):
        key = e + ".C"
        sem = self.sem[key]

        def wrap(eng, emit=emit, inc=inc, sem=sem):
            last = emit(eng)
            if inc and last is not None:
                last.then_inc(sem, inc)
        return self._plan(e, key, wrap, reads, writes, deps, inc)

    def dma(self, e, movs, reads=(), writes=(), deps=()):
        if writes:
            key = self._buf_key(writes[0], e)
        else:
            key = e + ".D"
        sem = self.sem[key]

        def wrap(eng, movs=movs, sem=sem):
            for m in movs:
                kw = m[2] if len(m) > 2 else {}
                eng.dma_start(out=m[0], in_=m[1], **kw).then_inc(sem, 16)
        return self._plan(e, key, wrap, reads, writes, deps, 16 * len(movs))

    def fence(self, deps):
        for e in ENGINES:
            self.step(e, lambda eng: eng.nop(), deps=deps, inc=1)

    def barrier_deps(self):
        return ([(e + ".C", self.cnt[e + ".C"]) for e in ENGINES if self.cnt[e + ".C"] > 0]
                + [(e + ".D", self.cnt[e + ".D"]) for e in ENGINES if self.cnt[e + ".D"] > 0])

    def emit_all(self):
        with self.nc.Block() as block:
            for e in ENGINES:
                def runner(eng, steps=self.ops[e]):
                    for waits, wrap in steps:
                        for (s, c) in waits:
                            eng.wait_ge(s, c)
                        wrap(eng)
                getattr(block, e)(runner)


def build_program(sections, dump=()):
    nc = bass.Bass("TRN2", target_bir_lowering=False,
                   detect_race_conditions=False)
    P = Prog(nc)
    dt_in = {}

    def din(name, shape, dtype=F32):
        dt_in[name] = nc.dram_tensor(name, shape, dtype, kind="ExternalInput").ap()
        return dt_in[name]

    resid_in = din("resid_in", [T, D])
    out = nc.dram_tensor("out", [T, D], F32, kind="ExternalOutput").ap()

    din("ident_bf", [128, 128], BF16)
    din("ident_f32", [128, 128])
    din("triu_incl", [128, 128])
    din("slo_mask", [128, 128])
    din("gamma", [1, 1])
    din("eps_bc", [128, 2])

    need = set(sections)
    if "l0attn" in need:
        din("l0_wqkv", [D, 3 * D])
        din("l0_bqkv_pk", [128, 24])
        din("l0_vbias_bc", [128, D])
        din("l0_wproj", [D, D])
        din("l0_bproj_bc", [128, D])
    for lx in ("l0", "l1"):
        if lx + "ffn" in need:
            din(lx + "_gatewT", [D, 32])
            din(lx + "_gbias_bc", [128, 32])
            din(lx + "_eg", [32, D, 512])
            din(lx + "_eu", [32, D, 512])
            din(lx + "_ed", [32, 512, D])
            din(lx + "_sg", [D, D])
            din(lx + "_su", [D, D])
            din(lx + "_sd", [D, D])
    if "l1attn" in need:
        din("l1_lnw_bc", [128, D])
        din("l1_lnb_bc", [128, D])
        din("l1_wq", [D, D])
        din("l1_wk", [D, D])
        din("l1_wv", [D, 2 * D])
        din("l1_wb", [D, 4])
        din("l1_onw_bc", [128, 512])
        din("l1_wo", [2 * D, D])

    scr_prev = nc.dram_tensor("scr_prev", [1, 255], F32, kind="Internal").ap()
    scr_wT = nc.dram_tensor("scr_wT", [32, T], F32, kind="Internal").ap()
    scr_nT = nc.dram_tensor("scr_nT", [D, T], BF16, kind="Internal").ap()
    b_prevd = Buf("prevd")
    b_wTd = Buf("wTd")
    b_nTd = Buf("nTd")

    RES = nc.alloc_sbuf_tensor("RES", [128, NT * D], F32)
    SM = {
        "ident_bf": nc.alloc_sbuf_tensor("sm_identbf", [128, 128], BF16),
        "ident_f32": nc.alloc_sbuf_tensor("sm_identf32", [128, 128], F32),
        "triu": nc.alloc_sbuf_tensor("sm_triu", [128, 128], F32),
        "slo": nc.alloc_sbuf_tensor("sm_slo", [128, 128], F32),
        "D": nc.alloc_sbuf_tensor("sm_D", [128, 128], F32),
        "gp": nc.alloc_sbuf_tensor("sm_gp", [1, 512], F32),
        "stats": nc.alloc_sbuf_tensor("sm_stats", [128, 16], F32),
        "scr1k": nc.alloc_sbuf_tensor("sm_scr1k", [128, 1024], F32),
        "rt": nc.alloc_sbuf_tensor("sm_rt", [128, 1024], F32),
        "eps": nc.alloc_sbuf_tensor("sm_eps", [128, 2], F32),
    }
    b_res = [Buf(f"res{k}") for k in range(NT)]
    b_sm = Buf("sm")
    b_rt = Buf("rt")
    b_stats = Buf("stats")
    b_scr = Buf("scr")

    PS_A = nc.alloc_psum_tensor("PS_A", [128, 2048], F32)
    PS_B = nc.alloc_psum_tensor("PS_B", [128, 1024], F32)
    PS_C = nc.alloc_psum_tensor("PS_C", [128, 1024], F32)
    bA = [Buf(f"psA{i}") for i in range(4)]
    bB0, bB1 = Buf("psB0"), Buf("psB1")
    bC0, bC1 = Buf("psC0"), Buf("psC1")

    def psA(i):
        return PS_A[:, i * 512:(i + 1) * 512]

    P_reg = {}
    PS_B_bf = PS_B.ap().bitcast(BF16)
    PS_C_bf = PS_C.ap().bitcast(BF16)
    PS_A_bf = PS_A.ap().bitcast(BF16)

    st_ln = SM["stats"][:, 4:8]

    def ln_tile(out_ap, b_out, w_ap=None, b_ap=None, b_w=None):
        rt = SM["rt"][:]
        s1, mu, ve, rs = (st_ln[:, i:i + 1] for i in range(4))
        P.step("vector", lambda v: v.reduce_sum(s1, rt, axis=AX.X),
               reads=[b_rt], writes=[b_stats])
        P.step("scalar", lambda s: s.mul(mu, s1, 1.0 / 1024.0),
               reads=[b_stats], writes=[b_stats])
        P.step("vector", lambda v: v.tensor_scalar(rt, rt, mu, None, ALU.subtract),
               reads=[b_rt, b_stats], writes=[b_rt])
        P.step("vector", lambda v: v.tensor_mul(SM["scr1k"][:], rt, rt),
               reads=[b_rt], writes=[b_scr])
        P.step("vector", lambda v: v.reduce_sum(ve, SM["scr1k"][:], axis=AX.X),
               reads=[b_scr], writes=[b_stats])
        P.step("scalar", lambda s: s.activation(rs, ve, AF.Ln, scale=1.0 / 1024.0,
                                                bias=SM["eps"][:, 0:1]),
               reads=[b_stats], writes=[b_stats])
        P.step("scalar", lambda s: s.activation(rs, rs, AF.Exp, scale=-0.5),
               reads=[b_stats], writes=[b_stats])
        if w_ap is None:
            return P.step("vector", lambda v: v.tensor_scalar(
                out_ap, rt, rs, None, ALU.mult),
                reads=[b_rt, b_stats], writes=[b_out])
        P.step("vector", lambda v: v.scalar_tensor_tensor(
            out_ap, rt, rs, w_ap, ALU.mult, ALU.mult),
            reads=[b_rt, b_stats, b_w], writes=[b_out])
        return P.step("vector", lambda v: v.tensor_tensor(out_ap, out_ap, b_ap, ALU.add),
                      reads=[b_out, b_w], writes=[b_out])

    def transpose128(src_ap, dst_ap, ps_ap, ps_buf, src_buf, dst_buf,
                     ident=None, evac="scalar"):
        idn = SM["ident_bf"][:] if ident is None else ident
        P.step("tensor", lambda t: t.transpose(ps_ap, src_ap, idn),
               reads=[src_buf, b_sm], writes=[ps_buf])
        if evac == "scalar":
            return P.step("scalar", lambda s: s.copy(dst_ap, ps_ap),
                          reads=[ps_buf], writes=[dst_buf])
        return P.step("vector", lambda v: v.tensor_copy(dst_ap, ps_ap),
                      reads=[ps_buf], writes=[dst_buf])

    P.dma("sync", [(RES[:].rearrange("p (k d) -> p k d", d=D),
                    resid_in.rearrange("(k p) d -> p k d", p=128))],
          writes=b_res)
    P.dma("sync", [(SM["ident_bf"][:], dt_in["ident_bf"]),
                   (SM["ident_f32"][:], dt_in["ident_f32"]),
                   (SM["triu"][:], dt_in["triu_incl"]),
                   (SM["slo"][:], dt_in["slo_mask"]),
                   (SM["gp"][0:1, 0:1], dt_in["gamma"]),
                   (SM["eps"][:], dt_in["eps_bc"])], writes=[b_sm])

    gp = SM["gp"]
    Q0 = 64

    P.step("vector", lambda v: v.memset(gp[0:1, Q0:Q0 + 1], 1.0),
           reads=[b_sm], writes=[b_sm])
    n = 1
    while n <= 64:
        P.step("vector", lambda v, n=n: v.tensor_tensor(
            gp[0:1, 32:33], gp[0:1, Q0 + n - 1:Q0 + n], gp[0:1, 0:1], ALU.mult),
            reads=[b_sm], writes=[b_sm])
        P.step("vector", lambda v, n=n: v.tensor_scalar(
            gp[0:1, Q0 + n:Q0 + 2 * n], gp[0:1, Q0:Q0 + n], gp[0:1, 32:33],
            None, ALU.mult), reads=[b_sm], writes=[b_sm])
        n *= 2
    P.step("vector", lambda v: v.memset(gp[0:1, 384:512], 0.0),
           reads=[b_sm], writes=[b_sm])
    P.step("vector", lambda v: v.tensor_copy(gp[0:1, 256:384],
                                             gp[0:1, Q0 + 127:Q0 - 1:-1]),
           reads=[b_sm], writes=[b_sm])
    P.dma("sync", [(scr_prev, gp[0:1, 256:511])], reads=[b_sm], writes=[b_prevd])
    P.dma("gpsimd",
          [(SM["D"][i:i + 1, :], scr_prev[0:1, 127 - i:255 - i]) for i in range(128)],
          reads=[b_prevd], writes=[b_sm])

    sec_dep = [P.barrier_deps()]

    def build_yt(YT, b_yt, xbf, b_xbf):
        for k in range(NT):
            par = k % 2
            xb = xbf[:, par * 1024:(par + 1) * 1024]
            P.step("scalar", lambda s, k=k, xb=xb: s.copy(xb, RES[:, k * D:(k + 1) * D]),
                   reads=[b_res[k]], writes=[b_xbf[par]])
            for j in range(ND):
                sl = j % 4
                pa = PS_A_bf[:, sl * 1024:sl * 1024 + 128]
                transpose128(xb[:, j * 128:(j + 1) * 128],
                             YT[:, j * T + k * 128: j * T + (k + 1) * 128],
                             pa, bA[sl], b_xbf[par], b_yt,
                             evac="vector" if (j % 2) else "scalar")

    def sec_l0attn():
        ar = nc.auto_arena()
        YT = ar.sbuf_tensor("a0_YT", [128, ND * T], BF16)
        Kf = ar.sbuf_tensor("a0_K", [128, ND * T], BF16)
        Vt = ar.sbuf_tensor("a0_V", [128, NT * 1024], BF16)
        wcol = ar.sbuf_tensor("a0_wcol", [128, 2 * 8 * 128], BF16)
        bpb = ar.sbuf_tensor("a0_bp", [128, 1024], F32)
        bqk = ar.sbuf_tensor("a0_bqk", [128, 24], F32)
        ar.allocate()
        ar_y = nc.auto_arena()
        Wv = ar_y.sbuf_tensor("a0_Wv", [128, 8 * 1024], BF16)
        xbf = ar_y.sbuf_tensor("a0_xbf", [128, 2048], BF16)
        vb = ar_y.sbuf_tensor("a0_vb", [128, 1024], F32)
        ar_y.allocate()

        b_yt = Buf("yt"); b_q = Buf("q"); b_k = Buf("k"); b_v = Buf("v")
        b_wv = Buf("wv"); b_wcol = [Buf("wc0"), Buf("wc1")]
        b_xbf = [Buf("xb0"), Buf("xb1")]
        b_s = Buf("s"); b_p = Buf("p"); b_pt = Buf("pt"); b_o = Buf("o")
        b_ot = Buf("ot"); b_wp = [Buf("wp0"), Buf("wp1")]
        b_cst = Buf("a0cst")

        P.fence(sec_dep[0])
        P.dma("sync", [(vb[:], dt_in["l0_vbias_bc"]), (bpb[:], dt_in["l0_bproj_bc"]),
                       (bqk[:], dt_in["l0_bqkv_pk"])], writes=[b_cst])
        build_yt(YT, b_yt, xbf, b_xbf)

        wqkv = dt_in["l0_wqkv"]
        P.dma("gpsimd", [(Wv[:].rearrange("p (k n) -> p k n", n=1024),
                          wqkv[:, 2 * D:3 * D].rearrange("(k p) n -> p k n", p=128))],
              writes=[b_wv])
        for tt in range(NT):
            ps = PS_B if tt % 2 == 0 else PS_C
            bp = [bB0, bB1] if tt % 2 == 0 else [bC0, bC1]

            def em(t, tt=tt, ps=ps):
                ins = None
                for d in range(ND):
                    lhsT = YT[:, d * T + tt * 128: d * T + (tt + 1) * 128]
                    for h in range(2):
                        ins = t.matmul(ps[:, h * 512:(h + 1) * 512], lhsT,
                                       Wv[:, d * 1024 + h * 512: d * 1024 + (h + 1) * 512],
                                       start=(d == 0), stop=(d == ND - 1))
                return ins
            P.step("tensor", em, reads=[b_yt, b_wv], writes=bp)
            P.step("vector", lambda v, tt=tt, ps=ps: v.scalar_tensor_tensor(
                Vt[:, tt * 1024:(tt + 1) * 1024], ps[:], 0.0, vb[:], ALU.add, ALU.add),
                reads=bp + [b_cst], writes=[b_v])
        for j in range(ND):
            par = j % 2
            wc = wcol[:, par * 1024:(par + 1) * 1024]
            col = D + j * 128
            P.dma("gpsimd", [(wc.rearrange("p (k c) -> p k c", c=128),
                              wqkv[:, col:col + 128].rearrange("(k p) c -> p k c", p=128))],
                  writes=[b_wcol[par]])

            def em(t, wc=wc):
                ins = None
                for d in range(ND):
                    lhsT = wc[:, d * 128:(d + 1) * 128]
                    for tc in range(4):
                        ins = t.matmul(psA(tc), lhsT,
                                       YT[:, d * T + tc * 512: d * T + (tc + 1) * 512],
                                       start=(d == 0), stop=(d == ND - 1))
                return ins
            P.step("tensor", em, reads=[b_wcol[par], b_yt], writes=bA)
            for tc in range(4):
                P.step("scalar", lambda s, j=j, tc=tc: s.activation(
                    Kf[:, j * T + tc * 512: j * T + (tc + 1) * 512], psA(tc),
                    AF.Identity, bias=bqk[:, 8 + j: 9 + j]),
                    reads=[bA[tc], b_cst], writes=[b_k])
        P.fence(P.barrier_deps())
        ar_y.free()
        ar_d = nc.auto_arena()
        Qf = ar_d.sbuf_tensor("a0_Q", [128, 8 * 512], BF16)
        pbf = ar_d.sbuf_tensor("a0_p", [128, 2048], BF16)
        ptb = ar_d.sbuf_tensor("a0_pt", [128, 16 * 128], BF16)
        obf = ar_d.sbuf_tensor("a0_o", [128, 1024], BF16)
        otb = ar_d.sbuf_tensor("a0_ot", [128, 8 * 128], BF16)
        wpb = ar_d.sbuf_tensor("a0_wp", [128, 2 * 1024], BF16)
        ar_d.allocate()
        sc = 1.0 / float(np.sqrt(D))
        st = SM["stats"]
        b_p2_box = [Buf("p2")]
        for g in range(4):
            for j in range(ND):
                par = j % 2
                wc = wcol[:, par * 1024:(par + 1) * 1024]
                P.dma("gpsimd", [(wc.rearrange("p (k c) -> p k c", c=128),
                                  wqkv[:, j * 128:(j + 1) * 128].rearrange("(k p) c -> p k c", p=128))],
                      writes=[b_wcol[par]])
                sl = j % 4

                def em(t, wc=wc, g=g, sl=sl):
                    ins = None
                    for d in range(ND):
                        ins = t.matmul(psA(sl), wc[:, d * 128:(d + 1) * 128],
                                       YT[:, d * T + g * 512: d * T + (g + 1) * 512],
                                       start=(d == 0), stop=(d == ND - 1))
                    return ins
                P.step("tensor", em, reads=[b_wcol[par], b_yt], writes=[bA[sl]])
                P.step("scalar", lambda s, j=j, sl=sl: s.activation(
                    Qf[:, j * 512:(j + 1) * 512], psA(sl), AF.Identity,
                    bias=bqk[:, j:j + 1]), reads=[bA[sl], b_cst], writes=[b_q])
            for qt in range(g * 4, g * 4 + 4):
                def em(t, qt=qt):
                    ins = None
                    for j in range(ND):
                        lhsT = Qf[:, j * 512 + (qt % 4) * 128: j * 512 + (qt % 4 + 1) * 128]
                        for kc in range(4):
                            ins = t.matmul(psA(kc), lhsT,
                                           Kf[:, j * T + kc * 512: j * T + (kc + 1) * 512],
                                           start=(j == 0), stop=(j == ND - 1))
                    return ins
                P.step("tensor", em, reads=[b_q, b_k], writes=bA)
                P.step("vector", lambda v, qt=qt: v.tensor_tensor(
                    PS_A[:, qt * 128:(qt + 1) * 128], PS_A[:, qt * 128:(qt + 1) * 128],
                    SM["D"][:], ALU.mult), reads=bA + [b_sm], writes=bA)
                P.step("vector", lambda v: v.reduce_max(st[:, 0:1], PS_A[:], axis=AX.X,
                                                        negate=True),
                       reads=bA, writes=[b_stats])
                P.step("scalar", lambda s: s.mul(st[:, 1:2], st[:, 0:1], sc),
                       reads=[b_stats], writes=[b_stats])
                P.step("scalar", lambda s: s.activation(pbf[:], PS_A[:],
                                                        AF.Identity if DBG_IDENT else AF.Exp,
                                                        bias=st[:, 1:2], scale=sc),
                       reads=bA + [b_stats], writes=[b_p])
                if DBG_EXP and qt == 15:
                    dap = nc.dram_tensor("dump_exp", [128, 2048], BF16,
                                         kind="ExternalOutput").ap()
                    sap = nc.dram_tensor("dump_st", [128, 8], F32,
                                         kind="ExternalOutput").ap()
                    P.dma("sync", [(dap, pbf[:])], reads=[b_p], writes=[b_p2_box[0]])
                    P.dma("sync", [(sap, st[:])], reads=[b_stats], writes=[b_p2_box[0]])
                P.step("vector", lambda v: v.reduce_sum(st[:, 2:3], pbf[:], axis=AX.X),
                       reads=[b_p], writes=[b_stats])
                P.step("vector", lambda v: v.reciprocal(st[:, 3:4], st[:, 2:3]),
                       reads=[b_stats], writes=[b_stats])
                P.step("vector", lambda v: v.tensor_scalar(pbf[:], pbf[:], st[:, 3:4],
                                                           None, ALU.mult),
                       reads=[b_p, b_stats], writes=[b_p])
                for kt in range(NT):
                    sl2 = kt % 2
                    transpose128(pbf[:, kt * 128:(kt + 1) * 128],
                                 ptb[:, kt * 128:(kt + 1) * 128],
                                 PS_B_bf[:, sl2 * 1024: sl2 * 1024 + 128],
                                 bB0 if sl2 == 0 else bB1, b_p, b_pt,
                                 evac="vector" if kt % 2 else "scalar")

                def em(t):
                    ins = None
                    for kt in range(NT):
                        lhsT = ptb[:, kt * 128:(kt + 1) * 128]
                        for h in range(2):
                            ins = t.matmul(PS_C[:, h * 512:(h + 1) * 512], lhsT,
                                           Vt[:, kt * 1024 + h * 512: kt * 1024 + (h + 1) * 512],
                                           start=(kt == 0), stop=(kt == NT - 1))
                    return ins
                P.step("tensor", em, reads=[b_pt, b_v], writes=[bC0, bC1])
                P.step("scalar", lambda s: s.copy(obf[:], PS_C[:]),
                       reads=[bC0, bC1], writes=[b_o])
                for j in range(ND):
                    sl2 = j % 2
                    transpose128(obf[:, j * 128:(j + 1) * 128],
                                 otb[:, j * 128:(j + 1) * 128],
                                 PS_C_bf[:, sl2 * 1024: sl2 * 1024 + 128],
                                 bC0 if sl2 == 0 else bC1, b_o, b_ot,
                                 evac="vector" if j % 2 else "scalar")
                for j in range(ND):
                    par = j % 2
                    wp = wpb[:, par * 1024:(par + 1) * 1024]
                    P.dma("gpsimd", [(wp, dt_in["l0_wproj"][j * 128:(j + 1) * 128, :])],
                          writes=[b_wp[par]])

                    def em(t, j=j, wp=wp):
                        ins = None
                        for h in range(2):
                            ins = t.matmul(PS_B[:, h * 512:(h + 1) * 512],
                                           otb[:, j * 128:(j + 1) * 128],
                                           wp[:, h * 512:(h + 1) * 512],
                                           start=(j == 0), stop=(j == ND - 1))
                        return ins
                    P.step("tensor", em, reads=[b_ot, b_wp[par]], writes=[bB0, bB1])
                rt = SM["rt"]
                P.step("vector", lambda v, qt=qt: v.tensor_tensor(
                    rt[:], PS_B[:], RES[:, qt * D:(qt + 1) * D], ALU.add),
                    reads=[bB0, bB1, b_res[qt]], writes=[b_rt])
                P.step("vector", lambda v: v.tensor_tensor(rt[:], rt[:], bpb[:], ALU.add),
                       reads=[b_rt, b_cst], writes=[b_rt])
                ln_tile(RES[:, qt * D:(qt + 1) * D], b_res[qt])
        P_reg.update(Dm=(SM["D"], b_sm),
                     YT=(YT, b_yt), Kf=(Kf, b_k), Vt=(Vt, b_v), Qf=(Qf, b_q),
                     pbf=(pbf, b_p), ptb=(ptb, b_pt), obf=(obf, b_o),
                     otb=(otb, b_ot))
        tok = P.barrier_deps()
        ar_d.free()
        ar.free()
        return tok

    def sec_ffn(lx, topk):
        ar = nc.auto_arena()
        YT = ar.sbuf_tensor(lx + "YT", [128, ND * T], BF16)
        ar.allocate()
        ar_r = nc.auto_arena()
        gwT = ar_r.sbuf_tensor(lx + "gwT", [128, 8 * 32], BF16)
        gbb = ar_r.sbuf_tensor(lx + "gbb", [128, 32], F32)
        scs = ar_r.sbuf_tensor(lx + "scs", [128, NT * 32], F32)
        w32 = ar_r.sbuf_tensor(lx + "w32", [128, NT * 32], F32)
        rsc = ar_r.sbuf_tensor(lx + "rsc", [128, 128], F32)
        wTs = ar_r.sbuf_tensor(lx + "wTs", [32, NT * 128], F32)
        xbf = ar_r.sbuf_tensor(lx + "xbf", [128, 2048], BF16)
        ar_r.allocate()

        b_yt = Buf("yt"); b_xbf = [Buf("xb0"), Buf("xb1")]
        b_wg = [Buf("wg0"), Buf("wg1")]; b_wu = [Buf("wu0"), Buf("wu1")]
        b_wd = [Buf("wd0"), Buf("wd1")]; b_wb = [Buf("wb0"), Buf("wb1")]
        b_hid = [[Buf(f"h{p}{i}") for i in range(4)] for p in range(2)]
        b_ts = [Buf("ts0"), Buf("ts1")]; b_ub = [Buf("ub0"), Buf("ub1")]
        b_h1 = Buf("h1")
        b_cst = Buf("cst"); b_scs = Buf("scs"); b_w32 = Buf("w32")
        b_rsc = Buf("rsc"); b_wTs = Buf("wTs")

        P.fence(sec_dep[0])
        P.dma("sync", [(gbb[:], dt_in[lx + "_gbias_bc"])], writes=[b_cst])
        P.dma("gpsimd", [(gwT[:].rearrange("p (k c) -> p k c", c=32),
                          dt_in[lx + "_gatewT"].rearrange("(k p) c -> p k c", p=128))],
              writes=[b_cst])
        build_yt(YT, b_yt, xbf, b_xbf)

        st = SM["stats"]
        for tt in range(NT):
            sl = tt % 4

            def em(t, tt=tt, sl=sl):
                ins = None
                for d in range(ND):
                    ins = t.matmul(psA(sl)[:, 0:32],
                                   YT[:, d * T + tt * 128: d * T + (tt + 1) * 128],
                                   gwT[:, d * 32:(d + 1) * 32],
                                   start=(d == 0), stop=(d == ND - 1))
                return ins
            P.step("tensor", em, reads=[b_yt, b_cst], writes=[bA[sl]])
            P.step("scalar", lambda s, tt=tt, sl=sl: s.activation(
                scs[:, tt * 32:(tt + 1) * 32], psA(sl)[:, 0:32], AF.Sigmoid),
                reads=[bA[sl]], writes=[b_scs])
            sfc = rsc[:, 0:32]; gs = rsc[:, 32:40]; m8 = rsc[:, 40:48]
            gm = rsc[:, 48:52]; msk = rsc[:, 64:96]; zap = rsc[:, 96:128]
            sct = scs[:, tt * 32:(tt + 1) * 32]

            mm8 = rsc[:, 96:128]

            def em_r1(v, sfc=sfc, mm8=mm8, sct=sct):
                v.tensor_tensor(sfc, sct, gbb[:], ALU.add)
                ins = None
                for g in range(4):
                    ins = v.max(out=mm8[:, g * 8:(g + 1) * 8],
                                in_=sfc[:, g * 8:(g + 1) * 8])
                return ins
            P.step("vector", em_r1, reads=[b_scs, b_cst], writes=[b_rsc])

            def em_r1b(v, gs=gs, mm8=mm8):
                v.memset(gs[:, 4:8], NEG)
                ins = None
                for g in range(4):
                    ins = v.tensor_tensor(gs[:, g:g + 1], mm8[:, g * 8:g * 8 + 1],
                                          mm8[:, g * 8 + 1:g * 8 + 2], ALU.add)
                return ins
            P.step("vector", em_r1b, reads=[b_rsc], writes=[b_rsc])
            P.step("vector", lambda v, gs=gs, m8=m8: v.max(out=m8, in_=gs),
                   reads=[b_rsc], writes=[b_rsc])
            P.step("vector", lambda v, gs=gs, gm=gm, m8=m8: v.tensor_scalar(
                gm, gs[:, 0:4], m8[:, 1:2], None, ALU.is_ge),
                reads=[b_rsc], writes=[b_rsc])

            def em_r2(v, sfc=sfc, gm=gm, msk=msk):
                ins = None
                for g in range(4):
                    ins = v.tensor_scalar(msk[:, g * 8:(g + 1) * 8],
                                          sfc[:, g * 8:(g + 1) * 8],
                                          gm[:, g:g + 1], None, ALU.mult)
                return ins
            P.step("vector", em_r2, reads=[b_rsc], writes=[b_rsc])

            P.step("vector", lambda v, msk=msk, m8=m8: v.max(out=m8, in_=msk),
                   reads=[b_rsc], writes=[b_rsc])
            if topk < 8:
                P.step("vector", lambda v, m8=m8: v.memset(m8[:, topk:8], NEG),
                       reads=[b_rsc], writes=[b_rsc])
            zap = rsc[:, 96:128]
            P.step("vector", lambda v, msk=msk, zap=zap, m8=m8: v.match_replace(
                out=zap, in_to_replace=m8, in_values=msk, imm_value=NEG),
                reads=[b_rsc], writes=[b_rsc])

            def em_r3(v, msk=msk, zap=zap, sct=sct):
                v.tensor_tensor(zap, msk, zap, ALU.not_equal)
                v.tensor_tensor(zap, zap, sct, ALU.mult)
                v.reduce_sum(st[:, 0:1], zap, axis=AX.X)
                v.tensor_scalar(st[:, 0:1], st[:, 0:1], 1e-20, None, ALU.add)
                return v.reciprocal(st[:, 1:2], st[:, 0:1])
            P.step("vector", em_r3, reads=[b_rsc, b_scs], writes=[b_rsc, b_stats])
            P.step("vector", lambda v, tt=tt, zap=zap: v.tensor_scalar(
                w32[:, tt * 32:(tt + 1) * 32], zap, st[:, 1:2], 2.5,
                ALU.mult, ALU.mult), reads=[b_rsc, b_stats], writes=[b_w32])
            sl2 = tt % 2
            bps = bB0 if sl2 == 0 else bB1
            P.step("tensor", lambda t, tt=tt, sl2=sl2: t.transpose(
                PS_B[0:32, sl2 * 512: sl2 * 512 + 128],
                w32[:, tt * 32:(tt + 1) * 32], SM["ident_f32"][:]),
                reads=[b_w32, b_sm], writes=[bps])
            P.step("vector", lambda v, tt=tt, sl2=sl2: v.tensor_copy(
                wTs[0:32, tt * 128:(tt + 1) * 128],
                PS_B[0:32, sl2 * 512: sl2 * 512 + 128]),
                reads=[bps], writes=[b_wTs])
        P.dma("sync", [(scr_wT, wTs[0:32, :])], reads=[b_wTs], writes=[b_wTd])
        if DBG_ROUTE:
            d1 = nc.dram_tensor(lx + "dump_w32", [128, NT * 32], F32,
                                kind="ExternalOutput").ap()
            d2 = nc.dram_tensor(lx + "dump_scs", [128, NT * 32], F32,
                                kind="ExternalOutput").ap()
            d3 = nc.dram_tensor(lx + "dump_rsc", [128, 128], F32,
                                kind="ExternalOutput").ap()
            P.dma("sync", [(d1, w32[:]), (d2, scs[:]), (d3, rsc[:])],
                  reads=[b_w32, b_scs, b_rsc], writes=[Buf("dbgrt")])

        P.fence(P.barrier_deps())
        ar_r.free()
        ar_e = nc.auto_arena()
        WG = ar_e.sbuf_tensor(lx + "WG", [128, 2 * 8 * 512], BF16)
        WU = ar_e.sbuf_tensor(lx + "WU", [128, 2 * 8 * 512], BF16)
        WD = ar_e.sbuf_tensor(lx + "WD", [128, 2 * 4 * 1024], BF16)
        HID = ar_e.sbuf_tensor(lx + "HID", [128, 2 * 4 * 2048], BF16)
        WB = ar_e.sbuf_tensor(lx + "WB", [128, 2 * 2048], BF16)
        tsil = ar_e.sbuf_tensor(lx + "tsil", [128, 2 * 1024], BF16)
        ubf = ar_e.sbuf_tensor(lx + "ubf", [128, 2 * 1024], BF16)
        h1b = ar_e.sbuf_tensor(lx + "h1b", [128, 1024], BF16)
        ar_e.allocate()

        eg, eu, ed = dt_in[lx + "_eg"], dt_in[lx + "_eu"], dt_in[lx + "_ed"]
        sg, su, sd = dt_in[lx + "_sg"], dt_in[lx + "_su"], dt_in[lx + "_sd"]

        def wsrc(e):
            if e < 32:
                return (eg[e], eu[e], ed[e], e)
            half = e - 32
            sl = slice(half * 512, (half + 1) * 512)
            return (sg[:, sl], su[:, sl], sd[sl, :], None)

        for e in range(34):
            par = e % 2
            g_src, u_src, d_src, widx = wsrc(e)
            wg = WG[:, par * 4096:(par + 1) * 4096]
            wu = WU[:, par * 4096:(par + 1) * 4096]
            wd = WD[:, par * 4096:(par + 1) * 4096]
            wb = WB[:, par * 2048:(par + 1) * 2048]
            P.dma("gpsimd", [(wg.rearrange("p (k n) -> p k n", n=512),
                              g_src.rearrange("(k p) n -> p k n", p=128)),
                             (wu.rearrange("p (k n) -> p k n", n=512),
                              u_src.rearrange("(k p) n -> p k n", p=128)),
                             (wd.rearrange("p (k n) -> p k n", n=1024),
                              d_src.rearrange("(k p) n -> p k n", p=128))],
                  writes=[b_wg[par], b_wu[par], b_wd[par]])
            if widx is not None:
                P.dma("gpsimd", [(wb, scr_wT[widx:widx + 1, :].to_broadcast([128, 2048]))],
                      reads=[b_wTd], writes=[b_wb[par]])
            else:
                P.step("vector", lambda v, wb=wb: v.memset(wb, 1.0),
                       writes=[b_wb[par]])
            for i in range(4):
                for th in range(2):
                    gsl = [0, 1]
                    usl = [2, 3]

                    def em_g(t, i=i, th=th, wg=wg, gsl=gsl):
                        ins = None
                        for d in range(ND):
                            lhsT = wg[:, d * 512 + i * 128: d * 512 + (i + 1) * 128]
                            for q in range(2):
                                ins = t.matmul(psA(gsl[q]), lhsT,
                                               YT[:, d * T + th * 1024 + q * 512:
                                                  d * T + th * 1024 + (q + 1) * 512],
                                               start=(d == 0), stop=(d == ND - 1))
                        return ins
                    P.step("tensor", em_g, reads=[b_wg[par], b_yt],
                           writes=[bA[gsl[0]], bA[gsl[1]]])
                    tsl = tsil[:, par * 1024:(par + 1) * 1024]
                    P.step("scalar", lambda s, gsl=gsl, tsl=tsl: s.activation(
                        tsl, PS_A[:, gsl[0] * 512:(gsl[1] + 1) * 512], AF.Silu),
                        reads=[bA[gsl[0]], bA[gsl[1]]], writes=[b_ts[par]])

                    def em_u(t, i=i, th=th, wu=wu, usl=usl):
                        ins = None
                        for d in range(ND):
                            lhsT = wu[:, d * 512 + i * 128: d * 512 + (i + 1) * 128]
                            for q in range(2):
                                ins = t.matmul(psA(usl[q]), lhsT,
                                               YT[:, d * T + th * 1024 + q * 512:
                                                  d * T + th * 1024 + (q + 1) * 512],
                                               start=(d == 0), stop=(d == ND - 1))
                        return ins
                    P.step("tensor", em_u, reads=[b_wu[par], b_yt],
                           writes=[bA[usl[0]], bA[usl[1]]])
                    ub = ubf[:, par * 1024:(par + 1) * 1024]
                    P.step("scalar", lambda s, usl=usl, ub=ub: s.copy(
                        ub, PS_A[:, usl[0] * 512:(usl[1] + 1) * 512]),
                        reads=[bA[usl[0]], bA[usl[1]]], writes=[b_ub[par]])
                    P.step("vector", lambda v, tsl=tsl, ub=ub: v.tensor_tensor(
                        h1b[:], tsl, ub, ALU.mult),
                        reads=[b_ts[par], b_ub[par]], writes=[b_h1])
                    hd = HID[:, par * 8192 + i * 2048 + th * 1024:
                             par * 8192 + i * 2048 + (th + 1) * 1024]
                    P.step("vector", lambda v, hd=hd, wb=wb, th=th: v.tensor_tensor(
                        hd, h1b[:], wb[:, th * 1024:(th + 1) * 1024], ALU.mult),
                        reads=[b_h1, b_wb[par]], writes=[b_hid[par][i]])
            if e % 2 == 1:
                for tt in range(NT):
                    ps = PS_B if tt % 2 == 0 else PS_C
                    bp = [bB0, bB1] if tt % 2 == 0 else [bC0, bC1]

                    def em_d(t, tt=tt, ps=ps):
                        ins = None
                        n = 0
                        for pp in range(2):
                            wdd = WD[:, pp * 4096:(pp + 1) * 4096]
                            for i in range(4):
                                lhsT = HID[:, pp * 8192 + i * 2048 + tt * 128:
                                           pp * 8192 + i * 2048 + (tt + 1) * 128]
                                for h in range(2):
                                    ins = t.matmul(ps[:, h * 512:(h + 1) * 512], lhsT,
                                                   wdd[:, i * 1024 + h * 512:
                                                       i * 1024 + (h + 1) * 512],
                                                   start=(n == 0), stop=(n == 7))
                                n += 1
                        return ins
                    rd = [b_wd[0], b_wd[1]] + b_hid[0] + b_hid[1]
                    P.step("tensor", em_d, reads=rd, writes=bp)
                    P.step("vector", lambda v, tt=tt, ps=ps: v.tensor_tensor(
                        RES[:, tt * D:(tt + 1) * D], RES[:, tt * D:(tt + 1) * D],
                        ps[:], ALU.add), reads=bp + [b_res[tt]], writes=[b_res[tt]])
        for tt in range(NT):
            P.step("vector", lambda v, tt=tt: v.tensor_copy(
                SM["rt"][:], RES[:, tt * D:(tt + 1) * D]),
                reads=[b_res[tt]], writes=[b_rt])
            ln_tile(RES[:, tt * D:(tt + 1) * D], b_res[tt])
        tok = P.barrier_deps()
        ar_e.free()
        ar.free()
        return tok

    def sec_l1attn():
        H, DK, DV = 4, 256, 512
        ar_pre = nc.auto_arena()
        lnw = ar_pre.sbuf_tensor("a1_lnw", [128, 1024], F32)
        lnb = ar_pre.sbuf_tensor("a1_lnb", [128, 1024], F32)
        nst = ar_pre.sbuf_tensor("a1_nst", [128, 1024], BF16)
        nst2 = ar_pre.sbuf_tensor("a1_nst2", [128, 1024], BF16)
        ar_pre.allocate()

        b_w = Buf("a1w")
        b_S = [Buf(f"S{h}") for h in range(H)]
        b_Sb = [Buf(f"Sb{h}") for h in range(H)]
        b_ncb = [Buf("nc0"), Buf("nc1")]
        b_qc = Buf("qc"); b_kc = Buf("kc"); b_vc = Buf("vc"); b_bay = Buf("bay"); b_QT = Buf("QT"); b_KT = Buf("KT")
        b_PWR = Buf("PWR"); b_Z = Buf("Z"); b_Mt = Buf("Mt"); b_Gt = Buf("Gt")
        b_B = Buf("B"); b_U = Buf("U"); b_on = Buf("on")
        b_Mt2 = Buf("Mt2"); b_Gt2 = Buf("Gt2"); b_B2 = Buf("B2"); b_U2 = Buf("U2")
        b_OT = Buf("OT"); b_cst = Buf("a1cst"); b_nst = Buf("nst")
        b_nst2 = Buf("nst2"); b_woq = [Buf(f"woq{i}") for i in range(2)]

        P.fence(sec_dep[0])
        P.dma("sync", [(lnw[:], dt_in["l1_lnw_bc"]), (lnb[:], dt_in["l1_lnb_bc"])],
              writes=[b_cst])

        nT_r = scr_nT.rearrange("(j p) t -> p j t", p=128)
        for tt in range(NT):
            P.step("vector", lambda v, tt=tt: v.tensor_copy(
                SM["rt"][:], RES[:, tt * D:(tt + 1) * D]),
                reads=[b_res[tt]], writes=[b_rt])
            ln_tile(nst[:], b_nst, w_ap=lnw[:], b_ap=lnb[:], b_w=b_cst)
            for j in range(ND):
                sl = j % 4
                transpose128(nst[:, j * 128:(j + 1) * 128],
                             nst2[:, j * 128:(j + 1) * 128],
                             PS_A_bf[:, sl * 1024: sl * 1024 + 128],
                             bA[sl], b_nst, b_nst2,
                             evac="vector" if j % 2 else "scalar")
            P.dma("sync", [(nT_r[:, :, tt * 128:(tt + 1) * 128],
                            nst2[:].rearrange("p (j c) -> p j c", c=128))],
                  reads=[b_nst2], writes=[b_nTd])

        P.fence(P.barrier_deps())
        ar_pre.free()
        ar = nc.auto_arena()
        Wq = ar.sbuf_tensor("a1_Wq", [128, 8 * 1024], BF16)
        Wk = ar.sbuf_tensor("a1_Wk", [128, 8 * 1024], BF16)
        Wv = ar.sbuf_tensor("a1_Wv", [128, 8 * 2048], BF16)
        Wb = ar.sbuf_tensor("a1_Wb", [128, 8 * 4], BF16)
        woq = ar.sbuf_tensor("a1_woq", [128, 2 * 1024], BF16)
        Sf = ar.sbuf_tensor("a1_S", [128, 4096], F32)
        Sb = ar.sbuf_tensor("a1_Sb", [128, 4096], BF16)
        ncb = ar.sbuf_tensor("a1_ncb", [128, 2 * 1024], BF16)
        qcb = ar.sbuf_tensor("a1_qcb", [128, 1024], BF16)
        kcb = ar.sbuf_tensor("a1_kcb", [128, 1024], BF16)
        vcb = ar.sbuf_tensor("a1_vcb", [128, 2048], BF16)
        bay = ar.sbuf_tensor("a1_bay", [128, 4], F32)
        QT = ar.sbuf_tensor("a1_QT", [128, 8 * 128], BF16)
        KT = ar.sbuf_tensor("a1_KT", [128, 8 * 128], BF16)
        PWR = ar.sbuf_tensor("a1_PWR", [128, 7 * 256], F32)
        Zb = ar.sbuf_tensor("a1_Z", [128, 2 * 128], F32)
        Mtb = ar.sbuf_tensor("a1_Mt", [128, 128], BF16)
        Gtb = ar.sbuf_tensor("a1_Gt", [128, 128], BF16)
        Bb = ar.sbuf_tensor("a1_B", [128, 512], BF16)
        Ub = ar.sbuf_tensor("a1_U", [128, 512], BF16)
        Mtb2 = ar.sbuf_tensor("a1_Mt2", [128, 128], BF16)
        Gtb2 = ar.sbuf_tensor("a1_Gt2", [128, 128], BF16)
        Bb2 = ar.sbuf_tensor("a1_B2", [128, 512], BF16)
        Ub2 = ar.sbuf_tensor("a1_U2", [128, 512], BF16)
        onb4 = ar.sbuf_tensor("a1_onb4", [128, 4 * 512], BF16)
        OT = ar.sbuf_tensor("a1_OT", [128, 16 * 128], BF16)
        onw = ar.sbuf_tensor("a1_onw", [128, 512], F32)
        ar.allocate()
        P.dma("sync", [(onw[:], dt_in["l1_onw_bc"])], writes=[b_cst])
        P.dma("gpsimd", [
            (Wq[:].rearrange("p (k n) -> p k n", n=1024),
             dt_in["l1_wq"].rearrange("(k p) n -> p k n", p=128)),
            (Wk[:].rearrange("p (k n) -> p k n", n=1024),
             dt_in["l1_wk"].rearrange("(k p) n -> p k n", p=128)),
            (Wv[:].rearrange("p (k n) -> p k n", n=2048),
             dt_in["l1_wv"].rearrange("(k p) n -> p k n", p=128)),
            (Wb[:].rearrange("p (k n) -> p k n", n=4),
             dt_in["l1_wb"].rearrange("(k p) n -> p k n", p=128)),
        ], writes=[b_w])
        P.step("vector", lambda v: v.memset(Sf[:], 0.0), writes=b_S)
        P.step("vector", lambda v: v.memset(Sb[:], 0.0), writes=b_Sb)

        def PW(j, t):
            return PWR[:, j * 256 + t * 128: j * 256 + (t + 1) * 128]

        st = SM["stats"]
        for c in range(NT):
            par = c % 2
            nb = ncb[:, par * 1024:(par + 1) * 1024]
            P.dma("gpsimd", [(nb.rearrange("p (j x) -> p j x", x=128),
                              nT_r[:, :, c * 128:(c + 1) * 128])],
                  reads=[b_nTd], writes=[b_ncb[par]])

            def em_q(t, nb=nb):
                ins = None
                for d in range(ND):
                    lhsT = nb[:, d * 128:(d + 1) * 128]
                    for h2 in range(2):
                        ins = t.matmul(PS_B[:, h2 * 512:(h2 + 1) * 512], lhsT,
                                       Wq[:, d * 1024 + h2 * 512: d * 1024 + (h2 + 1) * 512],
                                       start=(d == 0), stop=(d == ND - 1))
                return ins
            P.step("tensor", em_q, reads=[b_ncb[par], b_w], writes=[bB0, bB1])
            P.step("scalar", lambda s: s.activation(qcb[:], PS_B[:], AF.Silu),
                   reads=[bB0, bB1], writes=[b_qc])
            def em_b(t, nb=nb):
                ins = None
                for d in range(ND):
                    ins = t.matmul(PS_B[:, 0:4], nb[:, d * 128:(d + 1) * 128],
                                   Wb[:, d * 4:(d + 1) * 4],
                                   start=(d == 0), stop=(d == ND - 1))
                return ins
            P.step("tensor", em_b, reads=[b_ncb[par], b_w, b_qc], writes=[bB0])
            P.step("scalar", lambda s: s.activation(bay[:], PS_B[:, 0:4], AF.Sigmoid),
                   reads=[bB0], writes=[b_bay])

            def em_k(t, nb=nb):
                ins = None
                for d in range(ND):
                    lhsT = nb[:, d * 128:(d + 1) * 128]
                    for h2 in range(2):
                        ins = t.matmul(PS_C[:, h2 * 512:(h2 + 1) * 512], lhsT,
                                       Wk[:, d * 1024 + h2 * 512: d * 1024 + (h2 + 1) * 512],
                                       start=(d == 0), stop=(d == ND - 1))
                return ins
            P.step("tensor", em_k, reads=[b_ncb[par], b_w], writes=[bC0, bC1])
            P.step("scalar", lambda s: s.activation(kcb[:], PS_C[:], AF.Silu),
                   reads=[bC0, bC1], writes=[b_kc])

            def em_v(t, nb=nb):
                ins = None
                for d in range(ND):
                    lhsT = nb[:, d * 128:(d + 1) * 128]
                    for q4 in range(4):
                        ins = t.matmul(psA(q4), lhsT,
                                       Wv[:, d * 2048 + q4 * 512: d * 2048 + (q4 + 1) * 512],
                                       start=(d == 0), stop=(d == ND - 1))
                return ins
            P.step("tensor", em_v, reads=[b_ncb[par], b_w], writes=bA)
            P.step("scalar", lambda s: s.activation(vcb[:], PS_A[:], AF.Silu),
                   reads=bA, writes=[b_vc])

            def em_ss(v):
                ins = None
                for h in range(H):
                    v.tensor_mul(SM["rt"][:, 0:256],
                                 qcb[:, h * 256:(h + 1) * 256],
                                 qcb[:, h * 256:(h + 1) * 256])
                    v.reduce_sum(st[:, h:h + 1], SM["rt"][:, 0:256], axis=AX.X)
                    v.tensor_mul(SM["rt"][:, 0:256],
                                 kcb[:, h * 256:(h + 1) * 256],
                                 kcb[:, h * 256:(h + 1) * 256])
                    ins = v.reduce_sum(st[:, 4 + h:5 + h], SM["rt"][:, 0:256],
                                       axis=AX.X)
                return ins
            P.step("vector", em_ss, reads=[b_qc, b_kc], writes=[b_rt, b_stats])
            P.step("scalar", lambda s: s.activation(st[:, 8:16], st[:, 0:8], AF.Ln,
                                                    bias=SM["eps"][:, 1:2]),
                   reads=[b_stats], writes=[b_stats])
            P.step("scalar", lambda s: s.activation(st[:, 8:16], st[:, 8:16],
                                                    AF.Exp, scale=-0.5),
                   reads=[b_stats], writes=[b_stats])

            def em_qs(v):
                ins = None
                for h in range(H):
                    v.tensor_scalar(qcb[:, h * 256:(h + 1) * 256],
                                    qcb[:, h * 256:(h + 1) * 256],
                                    st[:, 8 + h:9 + h], 0.0625, ALU.mult, ALU.mult)
                    ins = v.tensor_scalar(kcb[:, h * 256:(h + 1) * 256],
                                          kcb[:, h * 256:(h + 1) * 256],
                                          st[:, 12 + h:13 + h], None, ALU.mult)
                return ins
            P.step("vector", em_qs, reads=[b_qc, b_kc, b_stats],
                   writes=[b_qc, b_kc])

            for h in range(H):
                for kt in range(2):
                    sl2 = (h * 2 + kt) % 2
                    transpose128(qcb[:, h * 256 + kt * 128: h * 256 + (kt + 1) * 128],
                                 QT[:, (h * 2 + kt) * 128:(h * 2 + kt + 1) * 128],
                                 PS_B_bf[:, sl2 * 1024: sl2 * 1024 + 128],
                                 bB0 if sl2 == 0 else bB1, b_qc, b_QT,
                                 evac="vector" if sl2 else "scalar")
                    transpose128(kcb[:, h * 256 + kt * 128: h * 256 + (kt + 1) * 128],
                                 KT[:, (h * 2 + kt) * 128:(h * 2 + kt + 1) * 128],
                                 PS_C_bf[:, sl2 * 1024: sl2 * 1024 + 128],
                                 bC0 if sl2 == 0 else bC1, b_kc, b_KT,
                                 evac="scalar" if sl2 else "vector")

            OTf = OT.ap().bitcast(F32)
            rtf = SM["rt"]

            class Lane:
                pass
            L0 = Lane()
            L0.slS, L0.bS = PS_B[:, 0:512], bB0
            L0.slO, L0.bO = PS_B[:, 512:1024], bB1
            L0.m0, L0.m1, L0.bm = PS_C[:, 0:128], PS_C[:, 128:256], bC0
            L0.nt, L0.bnt = PS_C[:, 512:640], bC1
            L0.pwn = lambda j: PWR[:, j * 256: j * 256 + 128]
            L0.pwt = lambda j: PWR[:, j * 256 + 128: j * 256 + 256]
            L0.bwn = b_PWR; L0.bwt = b_PWR; L0.dual_evac = False
            L0.z = [Zb[:, 0:128], Zb[:, 128:256]]; L0.bz = b_Z
            L0.Mt, L0.bMt = Mtb[:], b_Mt
            L0.Gt, L0.bGt = Gtb[:], b_Gt
            L0.B, L0.bB = Bb[:], b_B
            L0.U, L0.bU = Ub[:], b_U
            L1 = Lane()
            L1.slS, L1.bS = PS_A[:, 0:512], bA[0]
            L1.slO, L1.bO = PS_A[:, 512:1024], bA[1]
            L1.m0, L1.m1, L1.bm = PS_A[:, 1024:1152], PS_A[:, 1152:1280], bA[2]
            L1.nt, L1.bnt = PS_A[:, 1536:1664], bA[3]
            L1.pwn = lambda j: OTf[:, j * 128:(j + 1) * 128]
            L1.pwt = lambda j: SM["scr1k"][:, j * 128:(j + 1) * 128]
            L1.bwn = b_OT; L1.bwt = b_scr; L1.dual_evac = True
            L1.z = [rtf[:, 512:640], rtf[:, 640:768]]; L1.bz = b_rt
            L1.Mt, L1.bMt = Mtb2[:], b_Mt2
            L1.Gt, L1.bGt = Gtb2[:], b_Gt2
            L1.B, L1.bB = Bb2[:], b_B2
            L1.U, L1.bU = Ub2[:], b_U2

            def head_math(h, L):
                kth = [KT[:, (h * 2 + kt) * 128:(h * 2 + kt + 1) * 128] for kt in range(2)]
                qth = [QT[:, (h * 2 + kt) * 128:(h * 2 + kt + 1) * 128] for kt in range(2)]

                def em_A(t, kth=kth, L=L):
                    t.matmul(L.m0, kth[0], kth[0], start=True, stop=False)
                    return t.matmul(L.m0, kth[1], kth[1], start=False, stop=True)
                P.step("tensor", em_A, reads=[b_KT], writes=[L.bm]); yield
                P.step("vector", lambda v, h=h, L=L: v.scalar_tensor_tensor(
                    L.pwn(0), L.m0, bay[:, h:h + 1], SM["slo"][:], ALU.mult, ALU.mult),
                    reads=[L.bm, b_bay, b_sm], writes=[L.bwn]); yield
                P.step("tensor", lambda t, L=L: t.transpose(
                    L.nt, L.pwn(0), SM["ident_f32"][:]),
                    reads=[L.bwn, b_sm], writes=[L.bnt]); yield
                P.step("vector", lambda v, L=L: v.tensor_copy(L.pwt(0), L.nt),
                       reads=[L.bnt], writes=[L.bwt]); yield
                for j in range(1, 7):
                    def em_sq(t, j=j, L=L):
                        t.matmul(L.m0, L.pwt(j - 1), L.pwn(j - 1), start=True, stop=True)
                        return t.matmul(L.m1, L.pwn(j - 1), L.pwt(j - 1),
                                        start=True, stop=True)
                    P.step("tensor", em_sq, reads=[L.bwn, L.bwt], writes=[L.bm]); yield
                    if L.dual_evac:
                        P.step("vector", lambda v, j=j, L=L: v.tensor_copy(L.pwn(j), L.m0),
                               reads=[L.bm], writes=[L.bwn])
                        P.step("vector", lambda v, j=j, L=L: v.tensor_copy(L.pwt(j), L.m1),
                               reads=[L.bm], writes=[L.bwt]); yield
                    else:
                        P.step("vector", lambda v, j=j: v.tensor_copy(
                            PWR[:, j * 256:(j + 1) * 256], PS_C[:, 0:256]),
                            reads=[L.bm], writes=[L.bwn]); yield
                P.step("vector", lambda v, L=L: v.tensor_tensor(
                    L.z[0], SM["ident_f32"][:], L.pwt(6), ALU.add),
                    reads=[L.bwt, b_sm], writes=[L.bz]); yield
                cur = 0
                for lev in (5, 4, 3, 2, 1):
                    def em_z(t, lev=lev, cur=cur, L=L):
                        return t.matmul(L.m0, L.pwn(lev), L.z[cur], start=True, stop=True)
                    P.step("tensor", em_z, reads=[L.bwn, L.bz], writes=[L.bm]); yield
                    P.step("vector", lambda v, cur=cur, L=L: v.tensor_tensor(
                        L.z[1 - cur], L.m0, L.z[cur], ALU.add),
                        reads=[L.bm, L.bz], writes=[L.bz]); yield
                    cur = 1 - cur

                def em_z0(t, cur=cur, L=L):
                    return t.matmul(L.m0, L.pwn(0), L.z[cur], start=True, stop=True)
                P.step("tensor", em_z0, reads=[L.bwn, L.bz], writes=[L.bm]); yield
                P.step("vector", lambda v, cur=cur, L=L: v.tensor_tensor(
                    L.z[1 - cur], L.z[cur], L.m0, ALU.subtract),
                    reads=[L.bm, L.bz], writes=[L.bz]); yield
                cur = 1 - cur
                P.step("vector", lambda v, h=h, cur=cur, L=L: v.tensor_scalar(
                    L.Mt, L.z[cur], bay[:, h:h + 1], None, ALU.mult),
                    reads=[L.bz, b_bay], writes=[L.bMt]); yield

                def em_ks(t, h=h, kth=kth, L=L):
                    t.matmul(L.slS, kth[0], Sb[:, (h * 2) * 512:(h * 2 + 1) * 512],
                             start=True, stop=False)
                    return t.matmul(L.slS, kth[1], Sb[:, (h * 2 + 1) * 512:(h * 2 + 2) * 512],
                                    start=False, stop=True)
                P.step("tensor", em_ks, reads=[b_KT, b_Sb[h]], writes=[L.bS]); yield
                P.step("vector", lambda v, h=h, L=L: v.tensor_tensor(
                    L.B, vcb[:, h * 512:(h + 1) * 512], L.slS, ALU.subtract),
                    reads=[L.bS, b_vc], writes=[L.bB]); yield

                def em_U(t, L=L):
                    return t.matmul(L.slS, L.Mt, L.B, start=True, stop=True)
                P.step("tensor", em_U, reads=[L.bMt, L.bB], writes=[L.bS]); yield
                P.step("vector", lambda v, L=L: v.tensor_copy(L.U, L.slS),
                       reads=[L.bS], writes=[L.bU]); yield

                def em_G(t, kth=kth, qth=qth, L=L):
                    t.matmul(L.m0, kth[0], qth[0], start=True, stop=False)
                    return t.matmul(L.m0, kth[1], qth[1], start=False, stop=True)
                P.step("tensor", em_G, reads=[b_KT, b_QT], writes=[L.bm]); yield
                P.step("vector", lambda v, L=L: v.tensor_tensor(
                    L.Gt, L.m0, SM["triu"][:], ALU.mult),
                    reads=[L.bm, b_sm], writes=[L.bGt]); yield

                def em_O(t, h=h, qth=qth, L=L):
                    t.matmul(L.slO, qth[0], Sb[:, (h * 2) * 512:(h * 2 + 1) * 512],
                             start=True, stop=False)
                    t.matmul(L.slO, qth[1], Sb[:, (h * 2 + 1) * 512:(h * 2 + 2) * 512],
                             start=False, stop=False)
                    return t.matmul(L.slO, L.Gt, L.U, start=False, stop=True)
                P.step("tensor", em_O, reads=[b_QT, b_Sb[h], L.bGt, L.bU],
                       writes=[L.bO]); yield
                P.step("scalar", lambda s, h=h, L=L: s.copy(
                    onb4[:, h * 512:(h + 1) * 512], L.slO),
                    reads=[L.bO], writes=[b_on]); yield
                P.step("vector", lambda v, h=h: v.tensor_mul(
                    rtf[:, 0:512], onb4[:, h * 512:(h + 1) * 512],
                    onb4[:, h * 512:(h + 1) * 512]), reads=[b_on], writes=[b_rt]); yield
                P.step("vector", lambda v, h=h: v.reduce_sum(
                    st[:, h:h + 1], rtf[:, 0:512], axis=AX.X),
                    reads=[b_rt], writes=[b_stats]); yield
                for kt in range(2):
                    def em_su(t, h=h, kt=kt, L=L):
                        return t.matmul(L.slS, kcb[:, h * 256 + kt * 128: h * 256 + (kt + 1) * 128],
                                        L.U, start=True, stop=True)
                    P.step("tensor", em_su, reads=[b_kc, L.bU], writes=[L.bS]); yield
                    sfs = Sf[:, (h * 2 + kt) * 512:(h * 2 + kt + 1) * 512]
                    P.step("vector", lambda v, sfs=sfs, L=L: v.tensor_tensor(
                        sfs, sfs, L.slS, ALU.add), reads=[L.bS, b_S[h]], writes=[b_S[h]]); yield
                    P.step("scalar", lambda s, h=h, kt=kt, sfs=sfs: s.copy(
                        Sb[:, (h * 2 + kt) * 512:(h * 2 + kt + 1) * 512], sfs),
                        reads=[b_S[h]], writes=[b_Sb[h]]); yield

            for pair in ((0, 1), (2, 3)):
                if SEQ_HEADS:
                    for hh, LL in ((pair[0], L0), (pair[1], L1)):
                        for _ in head_math(hh, LL):
                            pass
                else:
                    gens = [head_math(pair[0], L0), head_math(pair[1], L1)]
                    while gens:
                        for g_ in list(gens):
                            try:
                                next(g_)
                            except StopIteration:
                                gens.remove(g_)

            P.step("scalar", lambda s: s.activation(st[:, 8:12], st[:, 0:4], AF.Ln,
                                                    scale=1.0 / 512, bias=SM["eps"][:, 0:1]),
                   reads=[b_stats], writes=[b_stats])
            P.step("scalar", lambda s: s.activation(st[:, 8:12], st[:, 8:12],
                                                    AF.Exp, scale=-0.5),
                   reads=[b_stats], writes=[b_stats])

            def em_onorm(v):
                ins = None
                for h in range(H):
                    ins = v.scalar_tensor_tensor(
                        onb4[:, h * 512:(h + 1) * 512], onb4[:, h * 512:(h + 1) * 512],
                        st[:, 8 + h:9 + h], onw[:], ALU.mult, ALU.mult)
                return ins
            P.step("vector", em_onorm, reads=[b_on, b_stats, b_cst], writes=[b_on])
            for h in range(H):
                for dv in range(4):
                    sl2 = dv % 2
                    transpose128(onb4[:, h * 512 + dv * 128: h * 512 + (dv + 1) * 128],
                                 OT[:, (h * 4 + dv) * 128:(h * 4 + dv + 1) * 128],
                                 PS_C_bf[:, 1024 + sl2 * 512: 1024 + sl2 * 512 + 128],
                                 bC1, b_on, b_OT,
                                 evac="vector" if sl2 else "scalar")

            for kt in range(16):
                sl4 = kt % 2
                wslice = woq[:, sl4 * 1024:(sl4 + 1) * 1024]
                P.dma("gpsimd", [(wslice, dt_in["l1_wo"][kt * 128:(kt + 1) * 128, :])],
                      writes=[b_woq[sl4]])

                def em_wo(t, kt=kt, wslice=wslice):
                    ins = None
                    lhsT = OT[:, kt * 128:(kt + 1) * 128]
                    for h2 in range(2):
                        ins = t.matmul(PS_A[:, h2 * 512:(h2 + 1) * 512], lhsT,
                                       wslice[:, h2 * 512:(h2 + 1) * 512],
                                       start=(kt == 0), stop=(kt == 15))
                    return ins
                if kt == 0:
                    P.step("tensor", em_wo, reads=[b_OT, b_woq[sl4]],
                           writes=[bA[0], bA[1]])
                else:
                    P.step("tensor", em_wo, reads=[b_OT, b_woq[sl4], bA[0], bA[1]],
                           writes=[bA[0], bA[1]] if kt == 15 else [])
            P.step("vector", lambda v, c=c: v.scalar_tensor_tensor(
                SM["rt"][:], RES[:, c * D:(c + 1) * D], 2.0, PS_A[:, 0:1024],
                ALU.mult, ALU.add),
                reads=[bA[0], bA[1], b_res[c]], writes=[b_rt])
            ln_tile(RES[:, c * D:(c + 1) * D], b_res[c])
        tok = P.barrier_deps()
        ar.free()
        return tok

    for s in sections:
        if s == "l0attn":
            sec_dep[0] = sec_l0attn()
        elif s == "l0ffn":
            sec_dep[0] = sec_ffn("l0", 8)
        elif s == "l1attn":
            sec_dep[0] = sec_l1attn()
        elif s == "l1ffn":
            sec_dep[0] = sec_ffn("l1", 4)

    for dn in dump:
        th, bf_ = P_reg[dn]
        dap = nc.dram_tensor("dump_" + dn, list(th.shape), th.dtype,
                             kind="ExternalOutput").ap()
        P.dma("sync", [(dap, th[:])], reads=[bf_], deps=sec_dep[0])
    P.dma("sync", [(out.rearrange("(k p) d -> p k d", p=128),
                    RES[:].rearrange("p (k d) -> p k d", d=D))],
          reads=b_res, deps=sec_dep[0])
    P.ops["sync"].append(
        ([], lambda eng, sem=P.sem["sync.D"], v=P.cnt["sync.D"]: eng.wait_ge(sem, v)))
    P.emit_all()
    return nc, sorted(dt_in.keys())



def _host_prepare(inputs):
    f = np.ascontiguousarray
    w = {}
    w["ident_bf"] = np.eye(128, dtype=ml_dtypes.bfloat16)
    w["ident_f32"] = np.eye(128, dtype=np.float32)
    r = np.arange(128)
    w["triu_incl"] = (r[None, :] >= r[:, None]).astype(np.float32)
    w["slo_mask"] = (r[:, None] > r[None, :]).astype(np.float32)
    w["gamma"] = np.asarray(inputs["L0_attn_gamma"], np.float32).reshape(1, 1)
    w["eps_bc"] = f(np.broadcast_to(np.array([1e-5, 1e-6], np.float32), (128, 2)))
    w["l0_wqkv"] = f(np.asarray(inputs["L0_attn_Wqkv"], np.float32))
    w["l0_bqkv_pk"] = f(np.asarray(inputs["L0_attn_bqkv"], np.float32).reshape(24, 128).T)
    w["l0_vbias_bc"] = f(np.broadcast_to(
        np.asarray(inputs["L0_attn_bqkv"], np.float32)[2048:], (128, 1024)))
    w["l0_wproj"] = f(np.asarray(inputs["L0_attn_Wproj"], np.float32))
    w["l0_bproj_bc"] = f(np.broadcast_to(
        np.asarray(inputs["L0_attn_bproj"], np.float32), (128, 1024)))
    w["l0_gatewT"] = f(np.asarray(inputs["L0_ffn_gate_w"], np.float32).T)
    w["l0_gbias_bc"] = f(np.broadcast_to(
        np.asarray(inputs["L0_ffn_gate_bias"], np.float32), (128, 32)))
    w["l0_eg"] = f(np.transpose(np.asarray(inputs["L0_ffn_expert_gate"], np.float32), (0, 2, 1)))
    w["l0_eu"] = f(np.transpose(np.asarray(inputs["L0_ffn_expert_up"], np.float32), (0, 2, 1)))
    w["l0_ed"] = f(np.transpose(np.asarray(inputs["L0_ffn_expert_down"], np.float32), (0, 2, 1)))
    w["l0_sg"] = f(np.asarray(inputs["L0_ffn_shared_gate"], np.float32).T)
    w["l0_su"] = f(np.asarray(inputs["L0_ffn_shared_up"], np.float32).T)
    w["l0_sd"] = f(np.asarray(inputs["L0_ffn_shared_down"], np.float32).T)
    w["l1_lnw_bc"] = f(np.broadcast_to(np.asarray(inputs["L1_attn_ln_w"], np.float32), (128, 1024)))
    w["l1_lnb_bc"] = f(np.broadcast_to(np.asarray(inputs["L1_attn_ln_b"], np.float32), (128, 1024)))
    w["l1_wq"] = f(np.asarray(inputs["L1_attn_Wq"], np.float32))
    w["l1_wk"] = f(np.asarray(inputs["L1_attn_Wk"], np.float32))
    w["l1_wv"] = f(np.asarray(inputs["L1_attn_Wv"], np.float32))
    w["l1_wb"] = f(np.asarray(inputs["L1_attn_Wb"], np.float32))
    w["l1_onw_bc"] = f(np.broadcast_to(np.asarray(inputs["L1_attn_o_norm_w"], np.float32), (128, 512)))
    w["l1_wo"] = f(np.asarray(inputs["L1_attn_Wo"], np.float32))
    w["l1_gatewT"] = f(np.asarray(inputs["L1_ffn_router_weight"], np.float32).T)
    w["l1_gbias_bc"] = f(np.broadcast_to(
        np.asarray(inputs["L1_ffn_router_bias"], np.float32), (128, 32)))
    w["l1_eg"] = f(np.transpose(np.asarray(inputs["L1_ffn_gate_w"], np.float32), (0, 2, 1)))
    w["l1_eu"] = f(np.transpose(np.asarray(inputs["L1_ffn_up_w"], np.float32), (0, 2, 1)))
    w["l1_ed"] = f(np.transpose(np.asarray(inputs["L1_ffn_down_w"], np.float32), (0, 2, 1)))
    w["l1_sg"] = f(np.asarray(inputs["L1_ffn_shared_gate_w"], np.float32).T)
    w["l1_su"] = f(np.asarray(inputs["L1_ffn_shared_up_w"], np.float32).T)
    w["l1_sd"] = f(np.asarray(inputs["L1_ffn_shared_down_w"], np.float32).T)
    return w


def kernel(**inputs):
    w = _host_prepare(inputs)
    x = np.ascontiguousarray(np.asarray(inputs["L0_attn_x"], np.float32))
    if MODE == "fused":
        stages = [["l0attn", "l0ffn", "l1attn", "l1ffn"]]
    else:
        stages = [["l0attn"], ["l0ffn"], ["l1attn"], ["l1ffn"]]
    cur = x
    for secs in stages:
        nc, names = build_program(secs)
        in_maps = []
        for b in range(8):
            m = {k: w[k] for k in names if k in w}
            m["resid_in"] = np.ascontiguousarray(cur[b])
            in_maps.append(m)
        res = run_bass_kernel_spmd(nc, in_maps, core_ids=list(range(8)))
        cur = np.stack([np.asarray(res.results[b]["out"]) for b in range(8)], 0)
    return cur
